# version 23
# baseline (speedup 1.0000x reference)
import numpy as np
import concourse.bass as bass
import concourse.mybir as mybir
from concourse.bass_utils import run_bass_kernel_spmd

F32 = mybir.dt.float32
BF16 = mybir.dt.bfloat16
AF = mybir.ActivationFunctionType
ALU = mybir.AluOpType
AX = mybir.AxisListType

T = 2048
D = 1024
NT = T // 128
ND = D // 128
HA = 16
DH = 64
HB = 8
INNER = 512
WIN = 512
HID = 2730
HIDP = 2816
NHC = HIDP // 128
NEG = -1.0e5


class Ctr:
    def __init__(self, sem):
        self.sem = sem
        self.v = 0

    def inc(self, n=1):
        self.v += n
        return self.v


class Em:

    def __init__(self):
        self.q = {"t": [], "v": [], "s": [], "g": [], "y": []}

    def t(self, f): self.q["t"].append(f)
    def v(self, f): self.q["v"].append(f)
    def s(self, f): self.q["s"].append(f)
    def g(self, f): self.q["g"].append(f)
    def y(self, f): self.q["y"].append(f)


def host_constants():
    c = {}
    c["ident_f"] = np.eye(128, dtype=np.float32)
    c["ident_b"] = np.eye(128, dtype=np.float32)
    tri = np.zeros((128, 128), dtype=np.float32)
    low = np.zeros((128, 128), dtype=np.float32)
    for p in range(128):
        tri[p, :p] = NEG
        low[p, p + 1:] = NEG
    c["mask_tri"] = tri
    c["mask_low"] = low
    c["ones_b"] = np.ones((128, 128), dtype=np.float32)
    half = np.arange(0, DH, 2, dtype=np.float64)
    inv_freq = 1.0 / 10000.0 ** (half / DH)
    t = np.arange(T, dtype=np.float64)
    freqs = t[:, None] * inv_freq[None, :]
    freqs = np.concatenate([freqs, freqs], axis=-1)
    base = (half + 0.4 * DH) / (1.4 * DH)
    power = (t - T // 2) / float(WIN // 2)
    scale = base[None, :] ** power[:, None]
    scale = np.concatenate([scale, scale], axis=-1)
    cos = np.cos(freqs)
    sin = np.sin(freqs)
    sign = np.ones((1, DH))
    sign[0, :DH // 2] = -1.0
    qc = 8.0 * cos * scale
    qs = 8.0 * sin * scale * sign
    kc = cos / scale
    ks = sin / scale * sign

    def tm(a):
        return np.ascontiguousarray(
            a.reshape(NT, 128, DH).transpose(1, 0, 2).reshape(128, NT * DH)
        ).astype(np.float32)

    c["rot_qc"] = tm(qc)
    c["rot_qs"] = tm(qs)
    c["rot_kc"] = tm(kc)
    c["rot_ks"] = tm(ks)
    return c


def col_major(a, npart=128):
    n = a.shape[-1] // npart
    return np.ascontiguousarray(
        a.reshape(a.shape[:-1] + (n, npart)).swapaxes(-1, -2))


class MK:

    def __init__(self, upto="full"):
        self.upto = upto
        nc = bass.Bass("TRN2", target_bir_lowering=False, debug=False)
        self.nc = nc
        self.em = Em()
        self.dr = {}
        self.sb = {}
        self.ps = {}
        self._semstack = []
        self._declare_dram()
        self._alloc_sbuf()
        self._alloc_psum()
        self._alloc_sems()
        self.bank_free = [(self.sems["ct"], 0)] * 4

    def din(self, name, shape, dt=F32):
        self.dr[name] = self.nc.dram_tensor(name, list(shape), dt,
                                            kind="ExternalInput").ap()

    def dout(self, name, shape, dt=F32):
        self.dr[name] = self.nc.dram_tensor(name, list(shape), dt,
                                            kind="ExternalOutput").ap()

    def dint(self, name, shape, dt=F32):
        self.dr[name] = self.nc.dram_tensor(name, list(shape), dt,
                                            kind="Internal").ap()

    def _declare_dram(self):
        self.din("x0", (T, D))
        self.din("wq", (D, D))
        self.din("wkv", (2 * D, D))
        self.din("wo_a", (D, D))
        self.din("wqkv", (2, D, 3 * INNER))
        self.din("qs_rep", (2, 128, INNER))
        self.din("ks_rep", (2, 128, INNER))
        self.din("wo_f", (2, INNER, D))
        self.din("lng_c", (2, 128, ND))
        self.din("lnb_c", (2, 128, ND))
        self.din("w1i", (2, D, 2 * HIDP))
        self.din("b1a_c", (2, 128, NHC))
        self.din("b1g_c", (2, 128, NHC))
        self.din("w2p", (2, HIDP, D))
        self.din("b2_c", (2, 128, ND))
        self.din("ident_f", (128, 128))
        self.din("ident_b", (128, 128), BF16)
        self.din("mask_tri", (128, 128))
        self.din("mask_low", (128, 128))
        self.din("ones_b", (128, 128), BF16)
        self.din("rot_qc", (128, NT * DH))
        self.din("rot_qs", (128, NT * DH))
        self.din("rot_kc", (128, NT * DH))
        self.din("rot_ks", (128, NT * DH))
        self.dout("y", (T, D))
        self.dint("xa_spill", (128, ND * T))
        self.dint("w1s", (2, NHC, 128, 2048), BF16)
        self.dint("w2s", (2, ND, 128, 2816), BF16)
        self.dint("mscr", (2 * HB, 16, 128), BF16)

    def _alloc_sbuf(self):
        def sbuf(name, shape, dt):
            self.sb[name] = self.nc.alloc_sbuf_tensor(name, list(shape), dt).ap()
        sbuf("X", (128, ND * T), F32)
        sbuf("ARA", (128, 16384), BF16)
        sbuf("ARB", (128, 16384), BF16)
        sbuf("ARC", (128, 16384), BF16)
        sbuf("ARW", (128, 8192), BF16)
        sbuf("SM", (128, 1536), F32)
        sbuf("QKS", (128, 2 * INNER), F32)
        sbuf("STG", (128, 2 * 512), BF16)
        sbuf("STK", (128, 2 * 128), BF16)
        sbuf("ET", (128, 4 * 512), BF16)
        sbuf("PRP", (128, 3 * 512), F32)
        sbuf("IDF", (128, 128), F32)
        sbuf("IDB", (128, 128), BF16)
        sbuf("MTRI", (128, 128), F32)
        sbuf("MLOW", (128, 128), F32)
        sbuf("ONE", (128, 128), BF16)
        sbuf("MTB", (128, 256), BF16)
        sbuf("COLS", (128, 88), F32)
        sbuf("HSJ", (128, 2048), BF16)
        sbuf("SMB", (128, 1280), BF16)

    def _alloc_psum(self):
        def psum(name, shape):
            self.ps[name] = self.nc.alloc_psum_tensor(name, list(shape), F32).ap()
        psum("PA", (128, 1024))
        psum("PB", (128, 1024))
        psum("PN0", (128, 512))
        psum("PN1", (128, 512))
        psum("PX", (128, 512))
        psum("PY", (128, 512))

    def bank(self, i):
        t = self.ps["PA"] if (i % 4) < 2 else self.ps["PB"]
        off = 512 * (i % 2)
        return t[:, off:off + 512]

    def _alloc_sems(self):
        self.sems = {}
        for name in ["sw", "sh", "sm", "ct", "cv", "cs", "cg", "fb", "fb2",
                     "x0", "nat0", "nat1", "qkv0", "qkv1", "wod", "w10", "w11",
                     "w20", "w21", "m10", "m11", "m20", "m21", "rel", "spl",
                     "pc", "prp", "st0", "st1"]:
            h = self.nc.semaphore(name)
            hh = h.__enter__()
            self._semstack.append(h)
            self.sems[name] = Ctr(hh)
        for k, v in self.sems.items():
            setattr(self, "c_" + k, v)

    def finalize(self):
        em = self.em
        with self.nc.allow_low_precision(
                reason="bf16 matmul operands w/ fp32 accumulation per stated "
                       "problem tolerance"), self.nc.Block() as block:
            @block.tensor
            def _(t):
                for f in em.q["t"]:
                    f(t)

            @block.vector
            def _(v):
                for f in em.q["v"]:
                    f(v)

            @block.scalar
            def _(s):
                for f in em.q["s"]:
                    f(s)

            @block.gpsimd
            def _(g):
                for f in em.q["g"]:
                    f(g)

            @block.sync
            def _(y):
                for f in em.q["y"]:
                    f(y)
        for h in self._semstack:
            h.__exit__(None, None, None)
        return self.nc

    def op(self, eng, fn, waits=(), inc=None, inc_n=1):
        tgt = inc.inc(inc_n) if inc is not None else None

        def closure(e, fn=fn, waits=tuple(waits), inc=inc, inc_n=inc_n):
            for (c, v) in waits:
                e.wait_ge(c.sem, v)
            r = fn(e)
            if inc is not None:
                r.then_inc(inc.sem, inc_n)
        self.em.q[eng].append(closure)
        return tgt

    def dma(self, eng, dst, src, waits=(), inc=None):
        inc = inc or (self.c_sw if eng == "g" else self.c_sh)
        return self.op(eng, lambda e, d=dst, s=src: e.dma_start(out=d, in_=s),
                       waits=waits, inc=inc, inc_n=16)

    def ld_sync(self, dst, src, wait=None):
        c = self.c_sh

        def f(y, dst=dst, src=src, wait=wait):
            if wait is not None:
                y.wait_ge(wait[0].sem, wait[1])
            y.dma_start(out=dst, in_=src).then_inc(c.sem, 16)
        self.em.y(f)
        return c.inc(16)

    def ld_gp(self, dst, src, wait=None):
        c = self.c_sw

        def f(g, dst=dst, src=src, wait=wait):
            if wait is not None:
                g.wait_ge(wait[0].sem, wait[1])
            g.dma_start(out=dst, in_=src).then_inc(c.sem, 16)
        self.em.g(f)
        return c.inc(16)

    def st_setup(self):
        sb, dr, ps = self.sb, self.dr, self.ps
        self.ld_sync(sb["IDF"][:, :], dr["ident_f"][:, :])
        self.ld_sync(sb["IDB"][:, :], dr["ident_b"][:, :])
        self.ld_sync(sb["MTRI"][:, :], dr["mask_tri"][:, :])
        self.ld_sync(sb["MLOW"][:, :], dr["mask_low"][:, :])
        vm = self.ld_sync(sb["ONE"][:, :], dr["ones_b"][:, :])
        self.const_t = vm
        self.op("s", lambda ss: ss.copy(out=sb["MTB"][:, 0:128],
                                        in_=sb["MTRI"][:, :]),
                waits=[(self.c_sh, vm)])
        self.mtb_ready = (self.c_cs,
                          self.op("s", lambda ss:
                                  ss.copy(out=sb["MTB"][:, 128:256],
                                          in_=sb["MLOW"][:, :]),
                                  inc=self.c_cs))

        xtm = sb["ARB"].bitcast(F32)
        X = sb["X"]
        XB = sb["ARC"]
        x0r = dr["x0"].rearrange("(n p) d -> p n d", p=128)
        tz = self.c_ct.v
        vz = self.c_cv.v
        k = 0
        for half in range(2):
            wait = None
            if half == 1:
                wait = (self.c_ct, tz + 64)
            v_x0 = self.dma("y", xtm[:, 0:8 * D],
                            x0r[:, half * 8:(half + 1) * 8, :],
                            waits=[wait] if wait else [], inc=self.c_x0)
            for idx in range(8 * ND):
                n, c = divmod(idx, ND)
                nn = half * 8 + n
                pt = self.bank(k)[:, 0:128]
                vt = self.c_ct.inc()

                def f(t, n=n, c=c, pt=pt, k=k, v_x0=v_x0, first=(idx == 0)):
                    if first:
                        t.wait_ge(self.c_sh.sem, self.const_t)
                        t.wait_ge(self.c_x0.sem, v_x0)
                    if k >= 4:
                        t.wait_ge(self.c_cv.sem, vz + 2 * (k - 3))
                    t.matmul(pt, xtm[:, n * D + c * 128: n * D + (c + 1) * 128],
                             sb["IDF"][:, :], start=True, stop=True
                             ).then_inc(self.c_ct.sem, 1)
                self.em.t(f)

                dst_f = X[:, c * T + nn * 128: c * T + (nn + 1) * 128]
                dst_b = XB[:, c * T + nn * 128: c * T + (nn + 1) * 128]
                self.c_cv.inc(2)

                def fv(v, pt=pt, dst_f=dst_f, dst_b=dst_b, vt=vt):
                    v.wait_ge(self.c_ct.sem, vt)
                    v.tensor_copy(out=dst_f, in_=pt).then_inc(self.c_cv.sem, 1)
                    v.tensor_copy(out=dst_b, in_=pt).then_inc(self.c_cv.sem, 1)
                self.em.v(fv)
                k += 1
        self.v_x0_done = self.c_cv.v
        self.bank_free = [(self.c_cv, self.v_x0_done)] * 4
        self.op("v", lambda v: v.memset(self.sb["COLS"][:, 4:5], 1e-5))


    def st_out(self, ln=False):
        sb, dr = self.sb, self.dr
        X = sb["X"]
        ytile = sb["ARW"].bitcast(F32)
        yout = dr["y"].rearrange("(n p) d -> p n d", p=128)
        x_ready = (self.c_cv, self.c_cv.v)
        store_t = {}
        for n in range(NT):
            dst = ytile[:, (n % 2) * D:(n % 2) * D + D]
            evs = []
            for c in range(ND):
                b = self.bank(c % 4)
                w = [self.bank_free[c % 4]]
                if n == 0 and c == 0:
                    w.append(x_ready)
                tt = self.op("t", lambda t, b=b[:, 0:128], src=X[:, c * T + n * 128: c * T + (n + 1) * 128]:
                             t.matmul(b, src, sb["IDF"][:, :], start=True, stop=True),
                             waits=w, inc=self.c_ct)
                wv = [(self.c_ct, tt)]
                if n >= 2 and c == 0:
                    wv.append(((self.c_st0 if n % 2 == 0 else self.c_st1),
                               store_t[n - 2]))
                ev = self.op("v", lambda v, b=b[:, 0:128], d=dst[:, c * 128:(c + 1) * 128]:
                             v.tensor_copy(out=d, in_=b),
                             waits=wv, inc=self.c_cv)
                self.bank_free[c % 4] = (self.c_cv, ev)
                evs.append(ev)
            vdone = (self.c_cv, evs[-1])
            if ln:
                vdone = self._ln_row_tile(dst, vdone)
            store_t[n] = self.dma("y", yout[:, n:n + 1, :], dst.unsqueeze(1),
                                  waits=[vdone],
                                  inc=(self.c_st0 if n % 2 == 0 else self.c_st1))
        self.op("y", lambda y: y.nop(),
                waits=[(self.c_sh, self.c_sh.v), (self.c_st0, self.c_st0.v),
                       (self.c_st1, self.c_st1.v)])

    def _ln_row_tile(self, dst, vdone):
        sb = self.sb
        sm = sb["SM"]
        s1, s2, mu, e2, mu2, var, rstd = (sm[:, i:i + 1] for i in
                                          (0, 4, 8, 12, 16, 20, 24))
        sq = sb["PRP"][:, 0:1024]
        fill = sm[:, 127:128]
        r1 = self.op("v", lambda v: v.reduce_sum(s1, dst, axis=AX.X),
                     waits=[vdone], inc=self.c_cv)
        r2 = self.op("v", lambda v: v.tensor_mul(sq, dst, dst),
                     inc=self.c_cv)
        r3 = self.op("v", lambda v: v.reduce_sum(s2, sq, axis=AX.X),
                     waits=[(self.c_cv, r2)], inc=self.c_cv)
        m1 = self.op("v", lambda v: v.tensor_scalar(out=mu, in0=s1,
                                                    scalar1=1.0 / D,
                                                    scalar2=None, op0=ALU.mult),
                     waits=[(self.c_cv, r1)], inc=self.c_cv)
        self.op("v", lambda v: v.tensor_scalar(out=e2, in0=s2,
                                               scalar1=1.0 / D,
                                               scalar2=None, op0=ALU.mult),
                waits=[(self.c_cv, r3)], inc=self.c_cv)
        m3 = self.op("v", lambda v: v.tensor_scalar(out=mu2, in0=mu,
                                                    scalar1=mu, scalar2=None,
                                                    op0=ALU.mult),
                     waits=[(self.c_cv, m1)], inc=self.c_cv)
        vv = self.op("v", lambda v: v.tensor_scalar(
            out=var, in0=e2, scalar1=mu2, scalar2=None, op0=ALU.subtract),
            waits=[(self.c_cv, m3)], inc=self.c_cv)
        st0 = self.op("s", lambda s: s.activation(
            out=sm[:, 28:29], in_=var, func=AF.Ln, bias=sb["COLS"][:, 4:5]),
            waits=[(self.c_cv, vv)], inc=self.c_cs)
        st = self.op("s", lambda s: s.activation(
            out=rstd, in_=sm[:, 28:29], func=AF.Exp, scale=-0.5),
            waits=[(self.c_cs, st0)], inc=self.c_cs)
        out = self.op("v", lambda v: v.tensor_scalar(
            out=dst, in0=dst, scalar1=mu, scalar2=rstd, op0=ALU.subtract,
            op1=ALU.mult), waits=[(self.c_cs, st), (self.c_cv, vv)],
            inc=self.c_cv)
        return (self.c_cv, out)

    def dbg(self, name, ap, shape, dt=None):
        dt = dt or ap.dtype
        self.dout(name, shape, dt)
        waits = [(self.c_ct, self.c_ct.v), (self.c_cv, self.c_cv.v),
                 (self.c_cs, self.c_cs.v)]
        self.dma("y", self.dr[name][:, :] if len(shape) == 2 else self.dr[name],
                 ap, waits=waits)
        self.op("y", lambda y: y.nop(), waits=[(self.c_sh, self.c_sh.v)])

    def st_partA(self):
        self.natk = 0
        self.nat_free = [(self.c_ct, 0), (self.c_ct, 0)]
        self.pair_ct = {}
        self.proj_ready = {}
        self.vb_ready = {}
        self.wslot_ready = {}
        self.proj_done_ct = {}
        self.et_free = [None, None, None, None]
        self.nb_free = [(self.c_cs, 0), (self.c_cs, 0)]
        self.rc_free = [(self.c_ct, 0), (self.c_ct, 0)]
        self.pn_free = [(self.c_cv, 0)] * 4
        self.ao_ready = {0: None, 1: None}
        self.wo_done = {}
        self._gg = 0
        self._den_pending = []
        self.a_wot()
        self._cv_ao = {0: [], 1: []}
        fd = None
        for p in range(9):
            if p < 8:
                self.a_pair_weights(p)
                fd = self.a_proj(p)
                if p == 0:
                    for _ in fd:
                        pass
                    fd = None
            else:
                fd = None
            if p >= 1:
                self.a_attn(2 * (p - 1), feeder=fd)
                self.a_attn(2 * (p - 1) + 1, feeder=fd)
                if fd is not None:
                    for _ in fd:
                        pass
            if p == 4:
                for fn in self._den_pending:
                    fn()
                self._den_pending = []
                self.ao_ready[0] = self.c_cv.v
                self.a_wo_half(0)
        for fn in self._den_pending:
            fn()
        self._den_pending = []
        self.ao_ready[1] = self.c_cv.v
        self.a_wo_half(1)
        self.a_ln1()


    def nat_slot(self):
        s = self.natk % 2
        self.natk += 1
        return s, self.sb["ARW"][:, s * 1024:(s + 1) * 1024]

    def nat_sem(self, s):
        return self.c_nat0 if s == 0 else self.c_nat1

    def a_wot(self):
        sb, dr = self.sb, self.dr
        WOT = sb["ARA"]
        for o in range(ND):
            s, nat = self.nat_slot()
            dt = self.dma("g", nat, dr["wo_a"][o * 128:(o + 1) * 128, :],
                          waits=[self.nat_free[s]], inc=self.nat_sem(s))
            last = None
            for c in range(ND):
                b = self.bank(c % 4)
                w = [self.bank_free[c % 4]]
                if c == 0:
                    w.append((self.nat_sem(s), dt))
                tt = self.op("t", lambda t, b=b[:, 0:128],
                             src=nat[:, c * 128:(c + 1) * 128]:
                             t.matmul(b, src, sb["IDB"][:, :], start=True, stop=True),
                             waits=w, inc=self.c_ct)
                ev = self.op("v", lambda v, b=b[:, 0:128],
                             d=WOT[:, 8192 + c * 1024 + o * 128: 8192 + c * 1024 + (o + 1) * 128]:
                             v.tensor_copy(out=d, in_=b),
                             waits=[(self.c_ct, tt)], inc=self.c_cv)
                self.bank_free[c % 4] = (self.c_cv, ev)
                last = tt
            self.nat_free[s] = (self.c_ct, last)
        self.wot_ready = self.c_cv.v

    def a_pair_weights(self, p):
        sb, dr = self.sb, self.dr
        s = p % 2
        base = 2048 + s * 3072
        srcs = [dr["wq"][p * 128:(p + 1) * 128, :],
                dr["wkv"][p * 128:(p + 1) * 128, :],
                dr["wkv"][D + p * 128: D + (p + 1) * 128, :]]
        slot_free = (self.c_ct, self.proj_done_ct.get(p - 2, 0))
        evs = []
        for wi, src in enumerate(srcs):
            ns, nat = self.nat_slot()
            dt = self.dma("g", nat, src, waits=[self.nat_free[ns]],
                          inc=self.nat_sem(ns))
            last = None
            for c in range(ND):
                b = self.bank((c + wi) % 4)
                w = [self.bank_free[(c + wi) % 4]]
                if c == 0:
                    w.append((self.nat_sem(ns), dt))
                    w.append(slot_free)
                tt = self.op("t", lambda t, b=b[:, 0:128],
                             src2=nat[:, c * 128:(c + 1) * 128]:
                             t.matmul(b, src2, sb["IDB"][:, :], start=True, stop=True),
                             waits=w, inc=self.c_ct)
                ev = self.op("v", lambda v, b=b[:, 0:128],
                             d=sb["ARW"][:, base + wi * 1024 + c * 128: base + wi * 1024 + (c + 1) * 128]:
                             v.tensor_copy(out=d, in_=b),
                             waits=[(self.c_ct, tt)], inc=self.c_cv)
                self.bank_free[(c + wi) % 4] = (self.c_cv, ev)
                last = tt
            self.nat_free[ns] = (self.c_ct, last)
            evs.append(self.c_cv.v)
        self.wslot_ready[p] = self.c_cv.v

    def a_proj(self, p):
        sb = self.sb
        s = p % 2
        QT = sb["ARA"][:, s * 4096: s * 4096 + 2048]
        KT = sb["ARA"][:, s * 4096 + 2048: s * 4096 + 4096]
        VB = sb["ARB"][:, 8192 + s * 2080: 8192 + (s + 1) * 2080]
        XB = sb["ARC"]
        WQ = sb["ARW"][:, 2048 + s * 3072: 2048 + s * 3072 + 1024]
        WK = sb["ARW"][:, 2048 + s * 3072 + 1024: 2048 + s * 3072 + 2048]
        WV = sb["ARW"][:, 2048 + s * 3072 + 2048: 2048 + s * 3072 + 3072]
        wready = (self.c_cv, self.wslot_ready[p])
        slot_free = (self.c_ct, self.pair_ct.get(p - 2, 0))
        vb_ones = bass.AP(sb["ARB"].tensor, 8192 + s * 2080 + 64,
                          [[16384, 128], [130, 16], [65, 2]])
        mz = self.op("v", lambda v: v.memset(vb_ones, 1.0),
                     waits=[slot_free], inc=self.c_cv)
        self.vb_ready[p] = mz
        for wi, (W, DST) in enumerate([(WQ, QT), (WK, KT)]):
            for c in range(ND):
                for tc in range(4):
                    w = []
                    if c == 0:
                        w.append(self.bank_free[tc])
                        if tc == 0:
                            w.append(wready)
                            w.append(slot_free)
                            if wi == 0 and p == 0:
                                w.append((self.c_cv, self.v_x0_done))
                    self.op("t", lambda t, b=self.bank(tc),
                            lh=W[:, c * 128:(c + 1) * 128],
                            rh=XB[:, c * T + tc * 512: c * T + tc * 512 + 512],
                            st=(c == 0), sp=(c == ND - 1):
                            t.matmul(b, lh, rh, start=st, stop=sp),
                            waits=w, inc=self.c_ct)
            mmt = self.c_ct.v
            for tc in range(4):
                ev = self.op("v", lambda v, b=self.bank(tc),
                             d=DST[:, tc * 512:(tc + 1) * 512]:
                             v.tensor_copy(out=d, in_=b),
                             waits=[(self.c_ct, mmt)], inc=self.c_cv)
                self.bank_free[tc] = (self.c_cv, ev)
            yield
        for n in range(NT):
            bidx = n % 4
            for c in range(ND):
                w = []
                if c == 0:
                    w.append(self.bank_free[bidx])
                tt = self.op("t", lambda t, b=self.bank(bidx)[:, 0:128],
                             lh=XB[:, c * T + n * 128: c * T + (n + 1) * 128],
                             rh=WV[:, c * 128:(c + 1) * 128],
                             st=(c == 0), sp=(c == ND - 1):
                             t.matmul(b, lh, rh, start=st, stop=sp),
                             waits=w, inc=self.c_ct)
            ev = None
            for hb in range(2):
                ev = self.op("v", lambda v, b=self.bank(bidx)[:, hb * 64:hb * 64 + 64],
                             d=VB[:, n * 130 + hb * 65: n * 130 + hb * 65 + 64]:
                             v.tensor_copy(out=d, in_=b),
                             waits=[(self.c_ct, tt)] if hb == 0 else [],
                             inc=self.c_cv)
            self.bank_free[bidx] = (self.c_cv, ev)
            yield
        self.proj_ready[p] = self.c_cv.v
        self.proj_done_ct[p] = self.c_ct.v

    def a_attn(self, h, feeder=None):
        sb = self.sb
        p, hb = divmod(h, 2)
        s = p % 2
        QT = sb["ARA"][:, s * 4096: s * 4096 + 2048]
        KT = sb["ARA"][:, s * 4096 + 2048: s * 4096 + 4096]
        vb_base = 8192 + s * 2080
        hh = h % 8
        ao_p0 = (hh % 2) * 64
        ao_cb = (hh // 2) * 2048
        pready = (self.c_cv, self.proj_ready[p])
        vbready = (self.c_cv, self.vb_ready[p])
        pnb = [self.ps["PN0"], self.ps["PN1"], self.ps["PX"], self.ps["PY"]]
        for g in range(4):
            gg = self._gg
            njc = 4 * g + 4
            pn = pnb[gg % 4]
            ndg = 4 * g
            units = [(jc, jc + 1) for jc in range(0, ndg - 1, 2)]
            units += [(jc,) for jc in range(ndg, njc)]
            for unit in units:
                simt = None
                for jc in unit:
                    diag = jc >= ndg
                    delta = max(0, (jc - ndg) * 128)
                    N = 512 - delta
                    sbk = jc % 4
                    w = [self.bank_free[sbk]]
                    if jc == 0:
                        w.append(pready)
                        w.append(self.mtb_ready)
                    if diag and N > 128:
                        qb = g * 512 + delta
                        self.op("t", lambda t, b=self.bank(sbk)[:, 128:N],
                                lh=KT[hb * 64: hb * 64 + 64,
                                      jc * 128:(jc + 1) * 128],
                                rh=QT[hb * 64: hb * 64 + 64,
                                      qb + 128:(g + 1) * 512]:
                                t.matmul(b, lh, rh, start=True, stop=False),
                                waits=w, inc=self.c_ct)
                        self.op("t", lambda t, b=self.bank(sbk)[:, 0:128]:
                                t.matmul(b, sb["IDB"][:, :],
                                         sb["MTB"][:, 0:128],
                                         start=False, stop=False),
                                inc=self.c_ct)
                        simt = self.op("t", lambda t, b=self.bank(sbk)[:, 0:128],
                                       lh=KT[hb * 64: hb * 64 + 64,
                                             jc * 128:(jc + 1) * 128],
                                       rh=QT[hb * 64: hb * 64 + 64,
                                             qb:qb + 128]:
                                       t.matmul(b, lh, rh, start=False,
                                                stop=True),
                                       inc=self.c_ct)
                    else:
                        if diag:
                            self.op("t", lambda t, b=self.bank(sbk)[:, 0:128]:
                                    t.matmul(b, sb["IDB"][:, :],
                                             sb["MTB"][:, 0:128],
                                             start=True, stop=False),
                                    waits=w, inc=self.c_ct)
                            w = []
                        simt = self.op("t", lambda t, b=self.bank(sbk)[:, 0:N],
                                       lh=KT[hb * 64: hb * 64 + 64,
                                             jc * 128:(jc + 1) * 128],
                                       rh=QT[hb * 64: hb * 64 + 64,
                                             g * 512 + delta: (g + 1) * 512],
                                       st=(not diag):
                                       t.matmul(b, lh, rh, start=st, stop=True),
                                       waits=w, inc=self.c_ct)
                we = [(self.c_ct, simt)]
                if len(unit) == 2:
                    sbkA = unit[0] % 4
                    for sbx in (sbkA, sbkA + 1):
                        if self.et_free[sbx] is not None:
                            we.append(self.et_free[sbx])
                    pt = self.ps["PA"] if sbkA < 2 else self.ps["PB"]
                    et = self.op("s", lambda ss, b=pt[:, 0:1024],
                                 d=sb["ET"][0:128, sbkA * 512:
                                            sbkA * 512 + 1024]:
                                 ss.activation(out=d, in_=b, func=AF.Exp,
                                               scale=1.0 / 8.0),
                                 waits=we, inc=self.c_cs)
                    self.bank_free[sbkA] = (self.c_cs, et)
                    self.bank_free[sbkA + 1] = (self.c_cs, et)
                else:
                    jc = unit[0]
                    sbk = jc % 4
                    N = 512 - max(0, (jc - ndg) * 128)
                    if self.et_free[sbk] is not None:
                        we.append(self.et_free[sbk])
                    et = self.op("s", lambda ss, b=self.bank(sbk)[:, 0:N],
                                 d=sb["ET"][0:128, sbk * 512: sbk * 512 + N]:
                                 ss.activation(out=d, in_=b, func=AF.Exp,
                                               scale=1.0 / 8.0),
                                 waits=we, inc=self.c_cs)
                    self.bank_free[sbk] = (self.c_cs, et)
                for jc in unit:
                    sbk = jc % 4
                    delta = max(0, (jc - ndg) * 128)
                    N = 512 - delta
                    wn = [(self.c_cs, et)] if jc == unit[0] else []
                    if jc == 0:
                        wn.append(self.pn_free[gg % 4])
                        wn.append(vbready)
                    nstop = self.op("t", lambda t, b=pn[0:65, delta:512],
                                    lh=VB_ap(sb, vb_base, jc, hb),
                                    rh=sb["ET"][0:128, sbk * 512:
                                                sbk * 512 + N],
                                    st=(jc == 0), sp=(jc == njc - 1):
                                    t.matmul(b, lh, rh, start=st, stop=sp),
                                    waits=wn, inc=self.c_ct)
                    self.et_free[sbk] = (self.c_ct, nstop)
                if feeder is not None:
                    next(feeder, None)
                if any(jc == 1 for jc in unit) and self._den_pending:
                    for fn in self._den_pending:
                        fn()
                    self._den_pending = []
            numt = self.c_ct.v
            self._den_pending.append(
                lambda gg=gg, pn=pn, numt=numt, h=h, g=g, ao_p0=ao_p0,
                ao_cb=ao_cb: self._a_den(gg, pn, numt, h, g, ao_p0, ao_cb))
            self._gg += 1
        self.pair_ct[p] = self.c_ct.v

    def _a_den(self, gg, pn, numt, h, g, ao_p0, ao_cb):
            sb = self.sb
            rcslot = gg % 2
            rc = sb["SMB"][0:1, rcslot * 512: rcslot * 512 + 512]
            ri = sb["SM"][0:1, 512 + rcslot * 512: 1024 + rcslot * 512]
            lnt = self.op("s", lambda ss, ri=ri, pn=pn:
                          ss.activation(out=ri, in_=pn[64:65, 0:512],
                                        func=AF.Ln),
                          waits=[(self.c_ct, numt), self.rc_free[rcslot]],
                          inc=self.c_cs)
            rt = self.op("s", lambda ss, ri=ri, rc=rc:
                         ss.activation(out=rc, in_=ri, func=AF.Exp,
                                       scale=-1.0),
                         waits=[(self.c_cs, lnt)], inc=self.c_cs)
            nbslot = gg % 2
            nb = sb["SMB"][64:128, nbslot * 512: nbslot * 512 + 512]
            nbt = self.op("s", lambda ss, nb=nb, pn=pn:
                          ss.copy(out=nb, in_=pn[0:64, 0:512]),
                          waits=[self.nb_free[nbslot]],
                          inc=self.c_cs)
            rpt = self.op("t", lambda t, pn=pn, rc=rc:
                          t.matmul(pn[64:128, 0:512], sb["ONE"][0:1, 0:64], rc,
                                   start=True, stop=True),
                          waits=[(self.c_cs, nbt)], inc=self.c_ct)
            self.rc_free[rcslot] = (self.c_ct, rpt)
            wao = [(self.c_ct, rpt)]
            if h >= 8 and self.wo_done.get(0) is not None:
                wao.append((self.c_ct, self.wo_done[0]))
            ev = self.op("v", lambda v, pn=pn, nb=nb,
                         d=sb["ARB"][ao_p0:ao_p0 + 64,
                                     ao_cb + g * 512: ao_cb + (g + 1) * 512]:
                         v.scalar_tensor_tensor(out=d, in0=pn[64:128, 0:512],
                                                scalar=1.0, in1=nb,
                                                op0=ALU.mult, op1=ALU.mult),
                         waits=wao, inc=self.c_cv)
            self.pn_free[gg % 4] = (self.c_cv, ev)
            self.nb_free[nbslot] = (self.c_cv, ev)

    def a_wo_half(self, half):
        sb = self.sb
        WOT = sb["ARA"]
        aoready = (self.c_cv, self.ao_ready[half])
        k = 0
        for o in range(ND):
            for tc in range(4):
                bidx = k % 4
                for f in range(4):
                    w = []
                    if f == 0:
                        w.append(self.bank_free[bidx])
                        if k == 0:
                            w.append(aoready)
                            w.append((self.c_cv, self.wot_ready))
                    self.op("t", lambda t, b=self.bank(bidx),
                            lh=WOT[:, 8192 + (half * 4 + f) * 1024 + o * 128:
                                   8192 + (half * 4 + f) * 1024 + (o + 1) * 128],
                            rh=sb["ARB"][:, f * 2048 + tc * 512: f * 2048 + tc * 512 + 512],
                            st=(f == 0), sp=(f == 3):
                            t.matmul(b, lh, rh, start=st, stop=sp),
                            waits=w, inc=self.c_ct)
                mmt = self.c_ct.v
                ev = self.op("v", lambda v, b=self.bank(bidx),
                             d=sb["X"][:, o * T + tc * 512: o * T + tc * 512 + 512]:
                             v.scalar_tensor_tensor(out=d, in0=b, scalar=1.0,
                                                    in1=d, op0=ALU.mult,
                                                    op1=ALU.add),
                             waits=[(self.c_ct, mmt)], inc=self.c_cv)
                self.bank_free[bidx] = (self.c_cv, ev)
                k += 1
        self.wo_done[half] = self.c_ct.v
        self.x1_ready = self.c_cv.v

    def a_ln1(self):
        sb = self.sb
        X = sb["X"]
        XBL = sb["ARB"][:, 0:4096]
        SQB = sb["ARB"][:, 4096:8192]
        wo1 = (self.c_ct, self.wo_done[1])
        xready = (self.c_cv, self.x1_ready)
        PX, PY = self.ps["PX"], self.ps["PY"]
        PN0, PN1 = self.ps["PN0"], self.ps["PN1"]
        apply_done = 0
        for tc in range(4):
            xr = X[:, :].rearrange("q (c t) -> q c t", c=ND)[:, :, tc * 512:(tc + 1) * 512]
            cst = self.op("s", lambda ss, xr=xr,
                          d=XBL.rearrange("q (c t) -> q c t", c=ND):
                          ss.copy(out=d, in_=xr),
                          waits=[xready, wo1,
                                 (self.c_cv, apply_done)], inc=self.c_cs)
            sqt = self.op("v", lambda v: v.tensor_mul(SQB, XBL, XBL),
                          waits=[(self.c_cs, cst)], inc=self.c_cv)
            for c in range(ND):
                self.op("t", lambda t, rh=XBL[:, c * 512:(c + 1) * 512],
                        st=(c == 0), sp=(c == ND - 1):
                        t.matmul(PX[0:1, 0:512], sb["ONE"][0:128, 0:1], rh,
                                 start=st, stop=sp),
                        waits=[(self.c_cs, cst), self.pn_free[2]] if c == 0 else [],
                        inc=self.c_ct)
            for c in range(ND):
                self.op("t", lambda t, rh=SQB[:, c * 512:(c + 1) * 512],
                        st=(c == 0), sp=(c == ND - 1):
                        t.matmul(PY[0:1, 0:512], sb["ONE"][0:128, 0:1], rh,
                                 start=st, stop=sp),
                        waits=[(self.c_cv, sqt), self.pn_free[3]] if c == 0 else [],
                        inc=self.c_ct)
            sums = self.c_ct.v
            sm = sb["SM"]
            murow = sm[0:1, 0:512]
            e2row = sm[0:1, 512:1024]
            varrow = sm[0:1, 1024:1536]
            rrowb = sb["SMB"][0:1, 0:512]
            srowb = sb["SMB"][0:1, 512:1024]
            mu1 = self.op("v", lambda v: v.tensor_scalar(
                out=murow, in0=PX[0:1, 0:512], scalar1=1.0 / D, scalar2=None,
                op0=ALU.mult), waits=[(self.c_ct, sums)], inc=self.c_cv)
            self.op("v", lambda v: v.tensor_scalar(
                out=e2row, in0=PY[0:1, 0:512], scalar1=1.0 / D, scalar2=None,
                op0=ALU.mult), inc=self.c_cv)
            m3 = self.op("v", lambda v: v.tensor_mul(varrow, murow, murow),
                         waits=[(self.c_cv, mu1)], inc=self.c_cv)
            vv = self.op("v", lambda v: v.tensor_tensor(
                out=varrow, in0=e2row, in1=varrow, op=ALU.subtract),
                waits=[(self.c_cv, m3)], inc=self.c_cv)
            st0 = self.op("s", lambda s2: s2.activation(
                out=sm[0:1, 1024:1536], in_=varrow, func=AF.Ln,
                bias=sb["COLS"][0:1, 4:5]), waits=[(self.c_cv, vv)],
                inc=self.c_cs)
            st1 = self.op("s", lambda s2: s2.activation(
                out=sm[0:1, 512:1024], in_=sm[0:1, 1024:1536], func=AF.Exp,
                scale=-0.5), waits=[(self.c_cs, st0)], inc=self.c_cs)
            vv2 = self.op("v", lambda v: v.tensor_copy(out=rrowb,
                                                       in_=sm[0:1, 512:1024]),
                          waits=[(self.c_cs, st1), self.rc_free[0],
                                 self.rc_free[1]])
            vv2 = self.op("v", lambda v: v.tensor_tensor(
                out=srowb, in0=murow, in1=sm[0:1, 512:1024], op=ALU.mult),
                waits=[(self.c_cv, mu1)], inc=self.c_cv)
            rep1 = self.op("t", lambda t: t.matmul(
                PN0[0:128, 0:512], sb["ONE"][0:1, 0:128], rrowb,
                start=True, stop=True),
                waits=[(self.c_cv, vv2), self.pn_free[0]], inc=self.c_ct)
            rep2 = self.op("t", lambda t: t.matmul(
                PN1[0:128, 0:512], sb["ONE"][0:1, 0:128], srowb,
                start=True, stop=True),
                waits=[self.pn_free[1]], inc=self.c_ct)
            mts = None
            for c in range(ND):
                mts = self.op("v", lambda v,
                              d=X[:, c * T + tc * 512: c * T + tc * 512 + 512]:
                              v.tensor_tensor(out=d, in0=d,
                                              in1=PN0[0:128, 0:512],
                                              op=ALU.mult),
                              waits=[(self.c_ct, rep2)] if c == 0 else [],
                              inc=self.c_cv if c == ND - 1 else None)
            ap = None
            for c in range(ND):
                ap = self.op("v", lambda v,
                             d=X[:, c * T + tc * 512: c * T + tc * 512 + 512]:
                             v.tensor_tensor(out=d, in0=d,
                                             in1=PN1[0:128, 0:512],
                                             op=ALU.subtract),
                             waits=[(self.c_cv, mts)] if c == 0 else [],
                             inc=self.c_cv)
            self.pn_free = [(self.c_cv, ap)] * 4
            apply_done = ap
        self.xa_ready = apply_done
        self.xa_spilled = self.dma("y", self.dr["xa_spill"][:, :], X[:, :],
                                   waits=[(self.c_cv, apply_done)],
                                   inc=self.c_spl)


    def emit_precast(self):
        W1ROW = 2 * HIDP
        for li in range(2):
            for hc in range(NHC):
                s1 = bass.AP(self.dr["w1i"].tensor, li * D * W1ROW + hc * 256,
                             [[W1ROW, 128], [128 * W1ROW, 8], [1, 256]])
                d1 = bass.AP(self.dr["w1s"].tensor,
                             (li * NHC + hc) * 128 * 2048,
                             [[2048, 128], [256, 8], [1, 256]])
                self.dma("g", d1, s1, inc=self.c_pc)
            for o in range(ND):
                s2 = bass.AP(self.dr["w2p"].tensor, li * HIDP * D + o * 128,
                             [[D, 128], [128 * D, NHC], [1, 128]])
                d2 = bass.AP(self.dr["w2s"].tensor, (li * ND + o) * 128 * 2816,
                             [[2816, 128], [128, NHC], [1, 128]])
                self.dma("g", d2, s2, inc=self.c_pc)
        self.precast_t = self.c_pc.v

    def st_partB(self, l):
        self.b_prep(l)
        self.b_qkv(l)
        wsrc = self.dr["wo_f"][l].rearrange("(k p) m -> p k m", p=128)
        self.wo_dma_t = self.dma(
            "g", self.sb["ARW"][:, 0:4096].rearrange("p (k m) -> p k m", k=4),
            wsrc, waits=[(self.c_ct, self.qkv_wslot_done)], inc=self.c_wod)
        if l == 0:
            self.emit_precast()
        self.attn_state_init()
        for h in range(HB + 1):
            feeder = self.b_pass1_iter(l, h) if h < HB else None
            if h == 0:
                for _ in feeder:
                    pass
                feeder = None
            if h >= 1:
                for g in range(4):
                    self.b_pass2_g(l, h - 1, g, feeder=feeder)
            if feeder is not None:
                for _ in feeder:
                    pass
            if h < HB:
                self.b_transport(l, h)
        for fn in self._den_pending:
            fn()
        self._den_pending = []
        self.ao2_ready = self.c_cv.v
        self.attn_done_ct = self.c_ct.v
        self.bank_free[0] = self.pa_free[0] or self.bank_free[0]
        self.bank_free[1] = self.pa_free[0] or self.bank_free[1]
        self.bank_free[2] = self.pa_free[1] or self.bank_free[2]
        self.bank_free[3] = self.pa_free[1] or self.bank_free[3]
        self.b_wo(l)
        self.b_ffn(l)

    def b_prep(self, l):
        sb, dr = self.sb, self.dr
        rot = sb["ARC"].bitcast(F32)
        if l == 0:
            wct = [(self.c_ct, self.proj_done_ct[7])]
        else:
            wct = [(self.c_ct, self.w2mm_done), (self.c_cv, self.ffn_done_cv),
                   (self.c_cs, self.c_cs.v)]
        for i, nm in enumerate(["rot_qc", "rot_qs", "rot_kc", "rot_ks"]):
            self.dma("y", rot[:, i * 1024:(i + 1) * 1024], dr[nm][:, :],
                     waits=wct if i == 0 else [], inc=self.c_prp)
        if l == 0:
            self.stk_ones = self.op("v", lambda v: v.memset(
                sb["STK"][64:65, :], 1.0), inc=self.c_cv)
        self.dma("y", sb["QKS"][:, 0:512], dr["qs_rep"][l], inc=self.c_prp)
        self.dma("y", sb["QKS"][:, 512:1024], dr["ks_rep"][l], inc=self.c_prp)
        self.dma("y", sb["COLS"][:, 8:16], dr["lng_c"][l], inc=self.c_prp)
        self.dma("y", sb["COLS"][:, 16:24], dr["lnb_c"][l], inc=self.c_prp)
        self.dma("y", sb["COLS"][:, 24:32], dr["b2_c"][l], inc=self.c_prp)
        self.dma("y", sb["COLS"][:, 32:54], dr["b1a_c"][l], inc=self.c_prp)
        self.prep_t = self.dma("y", sb["COLS"][:, 54:76], dr["b1g_c"][l],
                               inc=self.c_prp)

    def b_qkv(self, l):
        sb, dr = self.sb, self.dr
        X = sb["X"]
        rot = sb["ARC"].bitcast(F32)
        PRP = sb["PRP"]
        arbf = sb["ARB"].bitcast(F32)
        PSET = [
            dict(P0=PRP[:, 0:512], P1=PRP[:, 512:1024], P2=PRP[:, 1024:1536],
                 gt=sb["PRP"].tensor, gstride=1536, gbase=0),
            dict(P0=arbf[:, 4160:4672], P1=arbf[:, 4672:5184],
                 P2=arbf[:, 5184:5696], gt=arbf.tensor, gstride=8192,
                 gbase=4160),
        ]
        sm = sb["SM"]
        fill = sm[:, 127:128]
        if l == 0:
            wx = (self.c_cv, self.xa_ready)
            wfence = (self.c_ct, self.wo_done[1])
        else:
            wx = (self.c_cv, self.ffn_done_cv)
            wfence = (self.c_ct, self.w2mm_done)
        vb1 = bass.AP(sb["ARB"].tensor, 64, [[16384, 128], [520, 16], [65, 8]])
        self.vb2_ones = self.op("v", lambda v: v.memset(vb1, 1.0),
                                waits=[wx, wfence], inc=self.c_cv)
        hsj_free = [None, None]
        wslot_t = [None, None]
        stg_free = [None, None]
        prp_free = [[], []]
        sm_free = [None, None]

        for f in range(3):
            ws = f % 2
            wsl = sb["ARW"][:, ws * 4096:(ws + 1) * 4096]
            src = dr["wqkv"][l][:, f * 512:(f + 1) * 512].rearrange(
                "(k p) f -> p k f", p=128)
            wwait = list(prp_free) and []
            wwait = []
            if wslot_t[ws] is not None:
                wwait.append((self.c_ct, wslot_t[ws]))
            if f == 0:
                wwait.append(wfence)
            wdma = self.dma("g", wsl.rearrange("p (k f) -> p k f", k=8),
                            src, waits=wwait,
                            inc=(self.c_qkv0 if ws == 0 else self.c_qkv1))
            for n in range(NT):
                hs = n % 2
                hbase = hs * 1024
                PS = PSET[hs]
                P0, P1, P2 = PS["P0"], PS["P1"], PS["P2"]
                wcast = [wx]
                if hsj_free[hs] is not None:
                    wcast.append((self.c_ct, hsj_free[hs]))
                d_lo = bass.AP(sb["HSJ"].tensor, hbase,
                               [[2048, 128], [128, 4], [1, 128]])
                s_lo = bass.AP(X.tensor, n * 128,
                               [[ND * T, 128], [T, 4], [1, 128]])
                self.op("s", lambda ss, d=d_lo, s2=s_lo:
                        ss.copy(out=d, in_=s2), waits=wcast)
                wz = []
                if n == 0:
                    ed = bass.AP(sb["HSJ"].tensor, hbase + 512,
                                 [[2048, 128], [128, 4], [1, 2]])
                    mz = self.op("s", lambda ss, ed=ed: ss.memzero(ed),
                                 inc=self.c_cs)
                    wz = [(self.c_cs, mz)]
                    d_hi = bass.AP(sb["HSJ"].tensor, hbase + 513,
                                   [[2048, 128], [128, 4], [1, 127]])
                    s_hi = bass.AP(X.tensor, 4 * T,
                                   [[ND * T, 128], [T, 4], [1, 127]])
                else:
                    d_hi = bass.AP(sb["HSJ"].tensor, hbase + 512,
                                   [[2048, 128], [128, 4], [1, 128]])
                    s_hi = bass.AP(X.tensor, 4 * T + n * 128 - 1,
                                   [[ND * T, 128], [T, 4], [1, 128]])
                ct_cast = self.op("s", lambda ss, d=d_hi, s2=s_hi:
                                  ss.copy(out=d, in_=s2), waits=wz,
                                  inc=self.c_cs)
                bidx = n % 4
                for c in range(ND):
                    w = []
                    if c == 0:
                        w += [self.bank_free[bidx], (self.c_cs, ct_cast)]
                        if n == 0:
                            w.append(((self.c_qkv0 if ws == 0 else self.c_qkv1),
                                      wdma))
                    self.op("t", lambda t, b=self.bank(bidx),
                            lh=sb["HSJ"][:, hbase + c * 128: hbase + (c + 1) * 128],
                            rh=wsl[:, c * 512:(c + 1) * 512],
                            st=(c == 0), sp=(c == ND - 1):
                            t.matmul(b, lh, rh, start=st, stop=sp),
                            waits=w, inc=self.c_ct)
                mmt = self.c_ct.v
                hsj_free[hs] = mmt
                if f == 2:
                    dv = bass.AP(sb["ARB"].tensor, n * 520,
                                 [[16384, 128], [65, 8], [1, 64]])
                    ev = self.op("s", lambda ss, b=self.bank(bidx), dv=dv:
                                 ss.copy(out=dv, in_=b.rearrange(
                                     "p (h e) -> p h e", h=8)),
                                 waits=[(self.c_ct, mmt), wfence],
                                 inc=self.c_cs)
                    self.bank_free[bidx] = (self.c_cs, ev)
                    continue
                qoff = 0 if f == 0 else 512
                cosb = 0 if f == 0 else 2048
                sinb = 1024 if f == 0 else 3072
                smq = sm[:, 1024 + hs * 64: 1024 + hs * 64 + 8]
                smc = sm[:, 1040 + hs * 64: 1040 + hs * 64 + 8]
                sml = sm[:, 1056 + hs * 64: 1056 + hs * 64 + 8]
                smr = sm[:, 1072 + hs * 64: 1072 + hs * 64 + 8]
                wa1 = [(self.c_ct, mmt)] + prp_free[hs]
                a1 = self.op("s", lambda ss, b=self.bank(bidx), P0=P0:
                             ss.copy(out=P0, in_=b), waits=wa1, inc=self.c_cs)
                sq2 = self.op("v", lambda v, P0=P0, P1=P1:
                              v.tensor_mul(P1, P0, P0),
                              waits=[(self.c_cs, a1)], inc=self.c_cv)
                wsm = [(self.c_cv, sq2)]
                if sm_free[hs] is not None:
                    wsm.append((self.c_cs, sm_free[hs]))
                rs3 = self.op("v", lambda v, smq=smq, P1=P1: v.reduce_sum(
                    smq, P1.rearrange("p (h e) -> p h e", h=8), axis=AX.X),
                    waits=wsm, inc=self.c_cv)
                self.op("v", lambda v, qo=qoff, P0=P0, P2=P2: v.tensor_mul(
                    P2, P0, sb["QKS"][:, qo:qo + 512]),
                    waits=[(self.c_prp, self.prep_t)] if n == 0 else [])
                v4 = self.op("v", lambda v, smq=smq, smc=smc:
                             v.tensor_scalar_max(smc, smq, 1e-24),
                             waits=[(self.c_cv, rs3)], inc=self.c_cv)
                ln6 = self.op("s", lambda ss, smc=smc, sml=sml:
                              ss.activation(out=sml, in_=smc, func=AF.Ln),
                              waits=[(self.c_cv, v4)], inc=self.c_cs)
                s3 = self.op("s", lambda ss, sml=sml, smr=smr:
                             ss.activation(out=smr, in_=sml, func=AF.Exp,
                                           scale=-0.5),
                             waits=[(self.c_cs, ln6)], inc=self.c_cs)
                sm_free[hs] = s3
                hl8 = None
                for hh in range(HB):
                    hl8 = self.op("v", lambda v, hh=hh, smr=smr, P0=P0, P2=P2:
                                  v.tensor_scalar(
                        out=P0[:, hh * 64:(hh + 1) * 64],
                        in0=P2[:, hh * 64:(hh + 1) * 64],
                        scalar1=smr[:, hh:hh + 1], scalar2=None, op0=ALU.mult),
                        waits=[(self.c_cs, s3)] if hh == 0 else [],
                        inc=self.c_cv if hh == HB - 1 else None)
                gsrc = bass.AP(PS["gt"], PS["gbase"] + 32,
                               [[PS["gstride"], 128], [64, 8], [-32, 2], [1, 32]])
                gdst = bass.AP(PS["gt"], PS["gbase"] + 512,
                               [[PS["gstride"], 128], [64, 8], [32, 2], [1, 32]])
                sin4 = rot[:, sinb + n * 64: sinb + (n + 1) * 64].rearrange(
                    "p (two e) -> p two e", two=2).unsqueeze(
                    1).broadcast_to((128, 8, 2, 32))
                g1 = self.op("g", lambda g, gsrc=gsrc, gdst=gdst, s4=sin4:
                             g.tensor_tensor(out=gdst, in0=gsrc, in1=s4,
                                             op=ALU.mult),
                             waits=[(self.c_cv, hl8)], inc=self.c_cg)
                cos3 = rot[:, cosb + n * 64: cosb + (n + 1) * 64].unsqueeze(
                    1).broadcast_to((128, 8, 64))
                c15 = self.op("v", lambda v, c3=cos3, P0=P0, P2=P2:
                              v.tensor_tensor(
                    out=P2.rearrange("p (h e) -> p h e", h=8),
                    in0=P0.rearrange("p (h e) -> p h e", h=8),
                    in1=c3, op=ALU.mult),
                    waits=[(self.c_cv, hl8)], inc=self.c_cv)
                wst = [(self.c_cg, g1), (self.c_cv, c15)]
                if stg_free[hs] is not None:
                    wst.append((self.c_ct, stg_free[hs]))
                v14 = self.op("v", lambda v, d=sb["STG"][:, hs * 512:(hs + 1) * 512],
                              P1=P1, P2=P2:
                              v.tensor_add(d, P2, P1),
                              waits=wst, inc=self.c_cv)
                prp_free[hs] = [(self.c_cv, v14), (self.c_cg, g1)]
                dbase = 0 if f == 0 else 8192
                ltt = None
                for tp in range(4):
                    b2 = (n + tp) % 4
                    w = [self.bank_free[b2]]
                    if tp == 0:
                        w.append((self.c_cv, v14))
                    ltt = self.op("t", lambda t, b=self.bank(b2)[:, 0:128],
                                  lh=sb["STG"][:, hs * 512 + tp * 128: hs * 512 + (tp + 1) * 128]:
                                  t.matmul(b, lh, sb["IDB"][:, :],
                                           start=True, stop=True),
                                  waits=w, inc=self.c_ct)
                    wev = [(self.c_ct, ltt)]
                    if f == 0 and n == 0 and tp == 0:
                        wev.append(wfence)
                    ev = self.op("s", lambda ss, b=self.bank(b2)[:, 0:128],
                                 d=sb["ARA"][:, dbase + tp * 2048 + n * 128:
                                             dbase + tp * 2048 + (n + 1) * 128]:
                                 ss.copy(out=d, in_=b),
                                 waits=wev, inc=self.c_cs)
                    self.bank_free[b2] = (self.c_cs, ev)
                stg_free[hs] = ltt
            wslot_t[ws] = self.c_ct.v
        self.qkv_wslot_done = self.c_ct.v
        self.qt2_ready = self.c_cs.v
        self.stgk_init = self.c_ct.v
        self.qkv_prep_cg = self.c_cg.v
        self.qkv_prep_cv = self.c_cv.v
        self.hln_last = None

    def attn_state_init(self):
        self._gg2 = 0
        self.pa_pidx = 0
        self.pa_free = [None, None]
        self.mslot_free = [None, None]
        self.stq_free = [None, None]
        self.stk_free = [None, None]
        self.sbk_free = [None, None]
        self.hop1_t = {}
        self.mred_t = {}
        self.pn_free = [self.pn_free[0], self.pn_free[1],
                        self.bank_free[2], self.bank_free[3]]

    def b_pass1_iter(self, l, h):
        sb = self.sb
        j, hb = divmod(h, 2)
        QTp = sb["ARA"][:, j * 2048:(j + 1) * 2048]
        KTp = sb["ARA"][:, 8192 + j * 2048: 8192 + (j + 1) * 2048]
        hs2 = h % 2
        mA = sb["SM"][:, hs2 * 32: hs2 * 32 + 16]
        mC = sb["SM"][:, 128 + hs2 * 32: 128 + hs2 * 32 + 16]
        for c in range(NT):
            jlo = max(0, (c - 4) * 128)
            jhi = c * 128 + 128
            w = jhi - jlo
            par = self.pa_pidx % 2
            reg = self.ps["PA"] if par == 0 else self.ps["PB"]
            wts = []
            if self.pa_free[par] is not None:
                wts.append(self.pa_free[par])
            else:
                wts += [self.bank_free[2 * par], self.bank_free[2 * par + 1]]
            if c == 0:
                wts.append((self.c_cs, self.qt2_ready))
            lt = self.op("t", lambda t, b=reg[:, 0:min(w, 512)],
                         lh=QTp[hb * 64: hb * 64 + 64, c * 128:(c + 1) * 128],
                         rh=KTp[hb * 64: hb * 64 + 64, jlo:jlo + min(w, 512)]:
                         t.matmul(b, lh, rh, start=True, stop=True),
                         waits=wts, inc=self.c_ct)
            if w > 512:
                lt = self.op("t", lambda t, b=reg[:, 512:w],
                             lh=QTp[hb * 64: hb * 64 + 64, c * 128:(c + 1) * 128],
                             rh=KTp[hb * 64: hb * 64 + 64, jlo + 512:jhi]:
                             t.matmul(b, lh, rh, start=True, stop=True),
                             inc=self.c_ct)
            wrd = [(self.c_ct, lt)]
            if c == 0 and self.mslot_free[hs2] is not None:
                wrd.append((self.c_ct, self.mslot_free[hs2]))
            rd = self.op("v", lambda v, b=reg[:, 0:w], m=mA[:, c:c + 1]:
                         v.reduce_max(m, b, axis=AX.X, negate=True),
                         waits=wrd, inc=self.c_cv)
            self.pa_free[par] = (self.c_cv, rd)
            self.pa_pidx += 1
            yield
        wcmb = [(self.c_cv, rd)]
        if self.mslot_free[hs2] is not None:
            wcmb.append((self.c_ct, self.mslot_free[hs2]))
        cmb = self.op("v", lambda v, mA=mA, mC=mC:
                      v.tensor_copy(out=mC, in_=mA),
                      waits=wcmb, inc=self.c_cv)
        self.mred_t[h] = cmb

    def b_transport(self, l, h):
        sb, dr = self.sb, self.dr
        hs2 = h % 2
        mC = sb["SM"][:, 128 + hs2 * 32: 128 + hs2 * 32 + 16]
        PA = self.ps["PA"]
        tp = self.op("t", lambda t, mC=mC: t.matmul(
            PA[0:16, 0:128], mC, sb["IDF"][:, :], start=True, stop=True),
            waits=[(self.c_cv, self.mred_t[h]), self.pa_free[0]],
            inc=self.c_ct)
        self.mslot_free[hs2] = tp
        m16 = sb["SMB"][0:16, 1024 + hs2 * 128: 1024 + hs2 * 128 + 128]
        wev = [(self.c_ct, tp)]
        if h >= 2 and (h - 2) in self.hop1_t:
            wev.append(((self.c_m10 if hs2 == 0 else self.c_m11),
                        self.hop1_t[h - 2]))
        ev = self.op("v", lambda v, m16=m16: v.tensor_copy(
            out=m16, in_=PA[0:16, 0:128]),
            waits=wev, inc=self.c_cv)
        self.pa_free[0] = (self.c_cv, ev)
        self.hop1_t[h] = self.dma("s", dr["mscr"][l * HB + h], m16,
                                  waits=[(self.c_cv, ev)],
                                  inc=(self.c_m10 if hs2 == 0 else self.c_m11))

    def b_pass2_g(self, l, h, g, feeder=None):
        sb, dr = self.sb, self.dr
        j, hb = divmod(h, 2)
        QTp = sb["ARA"][:, j * 2048:(j + 1) * 2048]
        KTp = sb["ARA"][:, 8192 + j * 2048: 8192 + (j + 1) * 2048]
        mrow = dr["mscr"][l * HB + h].rearrange("p f -> (p f)")
        ao_p0 = (h % 2) * 64
        ao_cb = 8192 + (h // 2) * 2048
        pnb = [self.ps["PN0"], self.ps["PN1"]]
        if True:
            gg2 = self._gg2
            gs = g % 2
            pn = pnb[gg2 % 2]
            wq = [(self.c_ct, self.stgk_init)]
            if self.stq_free[gs] is not None:
                wq = [(self.c_ct, self.stq_free[gs])]
            sA = self.op("s", lambda ss, d=sb["STG"][0:64, gs * 512:(gs + 1) * 512],
                         s2=QTp[hb * 64: hb * 64 + 64, g * 512:(g + 1) * 512]:
                         ss.copy(out=d, in_=s2), waits=wq, inc=self.c_cs)
            hp2 = self.dma("s", sb["STG"][64:65, gs * 512:(gs + 1) * 512],
                           mrow[g * 512:(g + 1) * 512].unsqueeze(0),
                           waits=[((self.c_m10 if h % 2 == 0 else self.c_m11),
                                   self.hop1_t[h])] + wq,
                           inc=(self.c_m20 if gs == 0 else self.c_m21))
            jc0 = max(0, 4 * g - 4)
            cc = 0
            first = True
            nstop = None
            jc_order = [4 * g] + [j_ for j_ in range(jc0, 4 * g + 4)
                                  if j_ != 4 * g]
            for jc in jc_order:
                if jc < 4 * g:
                    a = jc - (4 * g - 4)
                    N = 128 * (a + 1)
                    ocol = 0
                    roff = 0
                    mcol = 128 * a
                    moff = 128
                else:
                    b_ = jc - 4 * g
                    delta = 128 * b_
                    N = 512 - delta
                    ocol = delta
                    roff = delta
                    mcol = 0
                    moff = 0
                ks = cc % 2
                wk = [(self.c_ct, self.stgk_init),
                      (self.c_cv, self.stk_ones)]
                if self.stk_free[ks] is not None:
                    wk = [(self.c_ct, self.stk_free[ks])]
                kA = self.op("s", lambda ss, d=sb["STK"][0:64, ks * 128:(ks + 1) * 128],
                             s2=KTp[hb * 64: hb * 64 + 64, jc * 128:(jc + 1) * 128]:
                             ss.copy(out=d, in_=s2), waits=wk, inc=self.c_cs)
                sbi = cc % 2
                sbank = self.ps["PX"] if sbi == 0 else self.ps["PY"]
                wsim = [(self.c_cs, kA)]
                if self.sbk_free[sbi] is not None:
                    wsim.append(self.sbk_free[sbi])
                if cc == 0:
                    wsim += [(self.c_cs, sA),
                             ((self.c_m20 if gs == 0 else self.c_m21), hp2)]
                if N > 128:
                    rlo, rhi = (128, N) if mcol == 0 else (0, mcol)
                    self.op("t", lambda t, b=sbank[0:128, rlo:rhi],
                            lh=sb["STK"][0:65, ks * 128:(ks + 1) * 128],
                            rh=sb["STG"][0:65, gs * 512 + roff + rlo:
                                         gs * 512 + roff + rhi]:
                            t.matmul(b, lh, rh, start=True, stop=False),
                            waits=wsim, inc=self.c_ct)
                    self.op("t", lambda t, b=sbank[:, mcol:mcol + 128],
                            moff=moff:
                            t.matmul(b, sb["IDB"][:, :],
                                     sb["MTB"][:, moff:moff + 128],
                                     start=False, stop=False),
                            inc=self.c_ct)
                    st_ = self.op("t", lambda t, b=sbank[0:128, mcol:mcol + 128],
                                  lh=sb["STK"][0:65, ks * 128:(ks + 1) * 128],
                                  rh=sb["STG"][0:65, gs * 512 + roff + mcol:
                                               gs * 512 + roff + mcol + 128]:
                                  t.matmul(b, lh, rh, start=False, stop=True),
                                  inc=self.c_ct)
                else:
                    self.op("t", lambda t, b=sbank[:, mcol:mcol + 128],
                            moff=moff:
                            t.matmul(b, sb["IDB"][:, :],
                                     sb["MTB"][:, moff:moff + 128],
                                     start=True, stop=False),
                            waits=wsim, inc=self.c_ct)
                    st_ = self.op("t", lambda t, b=sbank[0:128, 0:N],
                                  lh=sb["STK"][0:65, ks * 128:(ks + 1) * 128],
                                  rh=sb["STG"][0:65, gs * 512 + roff:
                                               gs * 512 + roff + N]:
                                  t.matmul(b, lh, rh, start=False, stop=True),
                                  inc=self.c_ct)
                self.stk_free[ks] = st_
                we = [(self.c_ct, st_)]
                es = cc % 4
                if self.et_free[es] is not None:
                    we.append(self.et_free[es])
                et = self.op("s", lambda ss, b=sbank[0:128, 0:N],
                             d=sb["ET"][0:128, es * 512: es * 512 + N]:
                             ss.activation(out=d, in_=b, func=AF.Exp),
                             waits=we, inc=self.c_cs)
                self.sbk_free[sbi] = (self.c_cs, et)
                wn = [(self.c_cs, et)]
                if first:
                    wn.append(self.pn_free[gg2 % 2])
                    wn.append((self.c_cv, self.vb2_ones))
                    if l == 0 and h == 0 and g == 0:
                        wn.append((self.c_ct, self.proj_done_ct[7]))
                nstop = self.op("t", lambda t, b=pn[0:65, ocol:ocol + N],
                                lh=sb["ARB"][0:128, jc * 520 + h * 65:
                                             jc * 520 + h * 65 + 65],
                                rh=sb["ET"][0:128, es * 512: es * 512 + N],
                                st=first, sp=(jc == 4 * g + 3):
                                t.matmul(b, lh, rh, start=st, stop=sp),
                                waits=wn, inc=self.c_ct)
                self.et_free[es] = (self.c_ct, nstop)
                if feeder is not None:
                    next(feeder, None)
                if cc == 1 and self._den_pending:
                    for fn in self._den_pending:
                        fn()
                    self._den_pending = []
                first = False
                cc += 1
            self.stq_free[gs] = nstop
            self._den_pending.append(
                lambda gg2=gg2, pn=pn, nstop=nstop, l=l, h=h, g=g,
                ao_p0=ao_p0, ao_cb=ao_cb:
                self._b_den(gg2, pn, nstop, l, h, g, ao_p0, ao_cb))
            self._gg2 += 1

    def _b_den(self, gg2, pn, nstop, l, h, g, ao_p0, ao_cb):
            sb = self.sb
            rs = gg2 % 2
            rc = sb["SMB"][0:1, rs * 512: rs * 512 + 512]
            ri = sb["SM"][0:1, 512 + rs * 512: 1024 + rs * 512]
            lnt = self.op("s", lambda ss, ri=ri, pn=pn:
                          ss.activation(out=ri, in_=pn[64:65, 0:512],
                                        func=AF.Ln),
                          waits=[(self.c_ct, nstop), self.rc_free[rs]],
                          inc=self.c_cs)
            rt = self.op("s", lambda ss, ri=ri, rc=rc:
                         ss.activation(out=rc, in_=ri, func=AF.Exp,
                                       scale=-1.0),
                         waits=[(self.c_cs, lnt)], inc=self.c_cs)
            nb = sb["SMB"][64:128, rs * 512: rs * 512 + 512]
            nbt = self.op("s", lambda ss, nb=nb, pn=pn:
                          ss.copy(out=nb, in_=pn[0:64, 0:512]),
                          waits=[self.nb_free[rs]],
                          inc=self.c_cs)
            rpt = self.op("t", lambda t, pn=pn, rc=rc:
                          t.matmul(pn[64:128, 0:512], sb["ONE"][0:1, 0:64], rc,
                                   start=True, stop=True),
                          waits=[(self.c_cs, nbt)], inc=self.c_ct)
            self.rc_free[rs] = (self.c_ct, rpt)
            wao = [(self.c_ct, rpt)]
            if l == 1:
                wao.append((self.c_ct, self.wo_f_done))
            ev = self.op("v", lambda v, pn=pn, nb=nb,
                         d=sb["ARC"][ao_p0:ao_p0 + 64,
                                     ao_cb + g * 512: ao_cb + (g + 1) * 512]:
                         v.scalar_tensor_tensor(out=d, in0=pn[64:128, 0:512],
                                                scalar=1.0, in1=nb,
                                                op0=ALU.mult, op1=ALU.mult),
                         waits=wao, inc=self.c_cv)
            self.pn_free[gg2 % 2] = (self.c_cv, ev)
            self.nb_free[rs] = (self.c_cv, ev)

    def b_wo(self, l):
        sb, dr = self.sb, self.dr
        wdma = self.wo_dma_t
        k = 0
        for o in range(ND):
            for tc in range(4):
                bidx = k % 4
                for kk in range(4):
                    w = []
                    if kk == 0:
                        w.append(self.bank_free[bidx])
                        if k == 0:
                            w += [(self.c_cv, self.ao2_ready),
                                  (self.c_wod, wdma)]
                    self.op("t", lambda t, b=self.bank(bidx),
                            lh=sb["ARW"][:, kk * 1024 + o * 128: kk * 1024 + (o + 1) * 128],
                            rh=sb["ARC"][0:128, 8192 + kk * 2048 + tc * 512:
                                         8192 + kk * 2048 + tc * 512 + 512],
                            st=(kk == 0), sp=(kk == 3):
                            t.matmul(b, lh, rh, start=st, stop=sp),
                            waits=w, inc=self.c_ct)
                mmt = self.c_ct.v
                wev = [(self.c_ct, mmt)]
                if k == 0 and l == 0:
                    wev.append((self.c_spl, self.xa_spilled))
                ev = self.op("v", lambda v, b=self.bank(bidx),
                             d=sb["X"][:, o * T + tc * 512: o * T + tc * 512 + 512]:
                             v.scalar_tensor_tensor(out=d, in0=b, scalar=1.0,
                                                    in1=d, op0=ALU.mult,
                                                    op1=ALU.add),
                             waits=wev, inc=self.c_cv)
                self.bank_free[bidx] = (self.c_cv, ev)
                k += 1
        self.wo_f_done = self.c_ct.v
        self.x2_ready = self.c_cv.v

    def b_ffn(self, l):
        sb, dr = self.sb, self.dr
        X = sb["X"]
        XB2 = sb["ARB"][:, 0:4096]
        SQB = sb["ARB"][:, 4096:8192]
        hl_base = 8320
        H2C = sb["ARA"]
        PX, PY = self.ps["PX"], self.ps["PY"]
        PN0, PN1 = self.ps["PN0"], self.ps["PN1"]
        sm = sb["SM"]
        fill = sm[:, 127:128]
        w2mm_prev = None
        w1slot_t = [None, None]
        w2slot_t = [None, None]
        stg2_free = [None, None]
        for tq in range(4):
            wc = [(self.c_cv, self.x2_ready)]
            if tq == 0:
                wc += [(self.c_ct, self.attn_done_ct)]
            else:
                wc += [(self.c_ct, self.stats_mm_t)]
            d_lo = bass.AP(sb["ARB"].tensor, 0, [[16384, 128], [512, 4], [1, 512]])
            s_lo = bass.AP(X.tensor, tq * 512, [[ND * T, 128], [T, 4], [1, 512]])
            self.op("s", lambda ss, d=d_lo, s2=s_lo: ss.copy(out=d, in_=s2),
                    waits=wc)
            wz2 = []
            if tq == 0:
                ed = bass.AP(sb["ARB"].tensor, 2048, [[16384, 128], [512, 4], [1, 2]])
                mz2 = self.op("s", lambda ss, ed=ed: ss.memzero(ed),
                              inc=self.c_cs)
                wz2 = [(self.c_cs, mz2)]
            else:
                ed = bass.AP(sb["ARB"].tensor, 2048, [[16384, 128], [512, 4], [1, 1]])
                sv = bass.AP(sb["COLS"].tensor, 80, [[88, 128], [1, 4], [1, 1]])
                self.op("s", lambda ss, ed=ed, sv=sv: ss.copy(out=ed, in_=sv),
                        waits=[(self.c_cv, self.sv_t)])
            d_hi = bass.AP(sb["ARB"].tensor, 2049, [[16384, 128], [512, 4], [1, 511]])
            s_hi = bass.AP(X.tensor, 4 * T + tq * 512, [[ND * T, 128], [T, 4], [1, 511]])
            cst = self.op("s", lambda ss, d=d_hi, s2=s_hi: ss.copy(out=d, in_=s2),
                          waits=wz2, inc=self.c_cs)
            sqt = self.op("v", lambda v: v.tensor_mul(SQB, XB2, XB2),
                          waits=[(self.c_cs, cst)], inc=self.c_cv)
            for c in range(ND):
                self.op("t", lambda t, rh=XB2[:, c * 512:(c + 1) * 512],
                        st=(c == 0), sp=(c == ND - 1):
                        t.matmul(PX[0:1, 0:512], sb["ONE"][0:128, 0:1], rh,
                                 start=st, stop=sp),
                        waits=[(self.c_cs, cst), self.sbk_free[0] or (self.c_ct, 0)]
                        if c == 0 else [], inc=self.c_ct)
            for c in range(ND):
                self.op("t", lambda t, rh=SQB[:, c * 512:(c + 1) * 512],
                        st=(c == 0), sp=(c == ND - 1):
                        t.matmul(PY[0:1, 0:512], sb["ONE"][0:128, 0:1], rh,
                                 start=st, stop=sp),
                        waits=[(self.c_cv, sqt), self.sbk_free[1] or (self.c_ct, 0)]
                        if c == 0 else [], inc=self.c_ct)
            self.sbk_free = [None, None]
            sums = self.c_ct.v
            self.stats_mm_t = sums
            murow = sm[0:1, 0:512]
            e2row = sm[0:1, 512:1024]
            varrow = sm[0:1, 1024:1536]
            rrowb = sb["SMB"][0:1, 0:512]
            srowb = sb["SMB"][0:1, 512:1024]
            mu1 = self.op("v", lambda v: v.tensor_scalar(
                out=murow, in0=PX[0:1, 0:512], scalar1=1.0 / D, scalar2=None,
                op0=ALU.mult), waits=[(self.c_ct, sums)], inc=self.c_cv)
            self.op("v", lambda v: v.tensor_scalar(
                out=e2row, in0=PY[0:1, 0:512], scalar1=1.0 / D, scalar2=None,
                op0=ALU.mult), inc=self.c_cv)
            m3 = self.op("v", lambda v: v.tensor_mul(varrow, murow, murow),
                         waits=[(self.c_cv, mu1)], inc=self.c_cv)
            vv = self.op("v", lambda v: v.tensor_tensor(
                out=varrow, in0=e2row, in1=varrow, op=ALU.subtract),
                waits=[(self.c_cv, m3)], inc=self.c_cv)
            self.sbk_free = [(self.c_cv, vv), (self.c_cv, vv)]
            st0 = self.op("s", lambda s2: s2.activation(
                out=sm[0:1, 1024:1536], in_=varrow, func=AF.Ln,
                bias=sb["COLS"][0:1, 4:5]), waits=[(self.c_cv, vv)],
                inc=self.c_cs)
            st1 = self.op("s", lambda s2: s2.activation(
                out=sm[0:1, 512:1024], in_=sm[0:1, 1024:1536], func=AF.Exp,
                scale=-0.5), waits=[(self.c_cs, st0)], inc=self.c_cs)
            vv2 = self.op("v", lambda v: v.tensor_copy(out=rrowb,
                                                       in_=sm[0:1, 512:1024]),
                          waits=[(self.c_cs, st1)])
            vv2 = self.op("v", lambda v: v.tensor_tensor(
                out=srowb, in0=murow, in1=sm[0:1, 512:1024], op=ALU.mult),
                waits=[(self.c_cv, mu1)], inc=self.c_cv)
            rep1 = self.op("t", lambda t: t.matmul(
                PN0[0:128, 0:512], sb["ONE"][0:1, 0:128], rrowb,
                start=True, stop=True),
                waits=[(self.c_cv, vv2), self.pn_free[0]], inc=self.c_ct)
            rep2 = self.op("t", lambda t: t.matmul(
                PN1[0:128, 0:512], sb["ONE"][0:1, 0:128], srowb,
                start=True, stop=True),
                waits=[self.pn_free[1]], inc=self.c_ct)
            hln_t = None
            for p_ in range(4):
                stg_t = None
                for c in (2 * p_, 2 * p_ + 1):
                    Pc = sb["PRP"][:, (c % 2) * 512:(c % 2) * 512 + 512]
                    shift = 1 if c >= 4 else 0
                    w = [(self.c_ct, rep2)] if c == 0 else []
                    if c == 0 and tq == 0:
                        w.append((self.c_prp, self.prep_t))
                    if c >= 2 and c % 2 == 0:
                        w.append((self.c_cv, hln_t))
                    if c == 0 and tq == 0:
                        w.append((self.c_cg, self.qkv_prep_cg))
                        if self.hln_last is not None:
                            w.append((self.c_cv, self.hln_last))
                    elif c == 0 and self.hln_last is not None:
                        w.append((self.c_cv, self.hln_last))
                    if shift:
                        if tq == 0:
                            self.op("v", lambda v, Pc=Pc:
                                    v.memset(Pc[:, 0:1], 0.0), waits=w)
                        else:
                            self.op("v", lambda v, Pc=Pc,
                                    sv=sb["COLS"][:, 76 + c:77 + c]:
                                    v.tensor_tensor(out=Pc[:, 0:1], in0=sv,
                                                    in1=PN0[0:128, 0:1],
                                                    op=ALU.mult),
                                    waits=w)
                        stg_t = self.op("v", lambda v, Pc=Pc,
                                        xs=X[:, c * T + tq * 512:
                                             c * T + tq * 512 + 511]:
                                        v.tensor_tensor(out=Pc[:, 1:512],
                                                        in0=xs,
                                                        in1=PN0[0:128, 1:512],
                                                        op=ALU.mult),
                                        inc=self.c_cv)
                    else:
                        stg_t = self.op("v", lambda v, Pc=Pc,
                                        xs=X[:, c * T + tq * 512:
                                             c * T + tq * 512 + 512]:
                                        v.tensor_tensor(out=Pc, in0=xs,
                                                        in1=PN0[0:128, 0:512],
                                                        op=ALU.mult),
                                        waits=w, inc=self.c_cv)
                sub_t = None
                for i_, c in enumerate((2 * p_, 2 * p_ + 1)):
                    Pc = sb["PRP"][:, (c % 2) * 512:(c % 2) * 512 + 512]
                    sub_t = self.op("v", lambda v, Pc=Pc: v.tensor_tensor(
                        out=Pc, in0=Pc, in1=PN1[0:128, 0:512],
                        op=ALU.subtract),
                        waits=[(self.c_cv, stg_t)] if i_ == 0 else [],
                        inc=self.c_cv)
                for i_, c in enumerate((2 * p_, 2 * p_ + 1)):
                    Pc = sb["PRP"][:, (c % 2) * 512:(c % 2) * 512 + 512]
                    hln_t = self.op("v", lambda v, Pc=Pc,
                                    d=sb["ARB"][:, hl_base + c * 512:
                                                hl_base + (c + 1) * 512],
                                    sc=sb["COLS"][:, 8 + c:9 + c],
                                    bi=sb["COLS"][:, 16 + c:17 + c]:
                                    v.tensor_scalar(out=d, in0=Pc, scalar1=sc,
                                                    scalar2=bi, op0=ALU.mult,
                                                    op1=ALU.add),
                                    waits=[(self.c_cv, sub_t)] if i_ == 0
                                    else [], inc=self.c_cv)
            hln_done = hln_t
            self.hln_last = hln_done
            self.pn_free[0] = (self.c_cv, hln_done)
            self.pn_free[1] = (self.c_cv, hln_done)
            h2_t = None
            for hc in range(NHC):
                ws = hc % 2
                wsl = sb["ARW"][:, 4096 + ws * 2048: 4096 + (ws + 1) * 2048]
                wwait = []
                if w1slot_t[ws] is not None:
                    wwait.append((self.c_ct, w1slot_t[ws]))
                elif tq == 0 and hc < 2:
                    wwait.append((self.c_ct, self.wo_f_done))
                if tq == 0 and hc < 2:
                    wwait.append((self.c_pc, self.precast_t))
                wdma = self.dma("y", wsl, dr["w1s"][l][hc], waits=wwait,
                                inc=(self.c_w10 if ws == 0 else self.c_w11))
                ba = self.bank(2 * ws)
                bg = self.bank(2 * ws + 1)
                for kk in range(ND):
                    w = []
                    if kk == 0:
                        w += [self.bank_free[2 * ws], (self.c_cv, hln_done),
                              ((self.c_w10 if ws == 0 else self.c_w11), wdma)]
                        if tq > 0 and hc == 0:
                            w.append((self.c_ct, w2mm_prev))
                    self.op("t", lambda t, b=ba,
                            lh=wsl[:, kk * 256: kk * 256 + 128],
                            rh=sb["ARB"][:, hl_base + kk * 512: hl_base + (kk + 1) * 512],
                            st=(kk == 0), sp=(kk == ND - 1):
                            t.matmul(b, lh, rh, start=st, stop=sp),
                            waits=w, inc=self.c_ct)
                for kk in range(ND):
                    w = [self.bank_free[2 * ws + 1]] if kk == 0 else []
                    self.op("t", lambda t, b=bg,
                            lh=wsl[:, kk * 256 + 128: kk * 256 + 256],
                            rh=sb["ARB"][:, hl_base + kk * 512: hl_base + (kk + 1) * 512],
                            st=(kk == 0), sp=(kk == ND - 1):
                            t.matmul(b, lh, rh, start=st, stop=sp),
                            waits=w, inc=self.c_ct)
                w1mm = self.c_ct.v
                w1slot_t[ws] = w1mm
                wgl = [(self.c_ct, w1mm)]
                if tq == 0 and hc == 0:
                    wgl.append((self.c_prp, self.prep_t))
                if stg2_free[ws] is not None:
                    wgl.append((self.c_cv, stg2_free[ws]))
                gl = self.op("s", lambda ss, b=bg,
                             d=sb["STG"][:, ws * 512:(ws + 1) * 512],
                             bi=sb["COLS"][:, 54 + hc:55 + hc]:
                             ss.activation(out=d, in_=b, func=AF.Gelu,
                                           bias=bi),
                             waits=wgl, inc=self.c_cs)
                self.bank_free[2 * ws + 1] = (self.c_cs, gl)
                h2w = [(self.c_cs, gl)]
                if tq == 0 and hc < 2:
                    h2w.append((self.c_ct, self.attn_done_ct))
                if tq > 0 and hc == 0:
                    h2w.append((self.c_ct, w2mm_prev))
                h2_t = self.op("v", lambda v, b=ba, hc=hc:
                               v.scalar_tensor_tensor(
                                   out=H2C[:, hc * 512:(hc + 1) * 512],
                                   in0=b, scalar=sb["COLS"][:, 32 + hc:33 + hc],
                                   in1=sb["STG"][:, (hc % 2) * 512:((hc % 2) + 1) * 512],
                                   op0=ALU.add, op1=ALU.mult),
                               waits=h2w, inc=self.c_cv)
                self.bank_free[2 * ws] = (self.c_cv, h2_t)
                stg2_free[ws] = h2_t
            h2_done = h2_t
            svd = bass.AP(sb["COLS"].tensor, 80, [[88, 128], [1, 4], [1, 1]])
            svs = bass.AP(X.tensor, 4 * T + tq * 512 + 511,
                          [[ND * T, 128], [T, 4], [1, 1]])
            self.sv_t = self.op("v", lambda v, svd=svd, svs=svs:
                                v.tensor_copy(out=svd, in_=svs),
                                inc=self.c_cv)
            for o in range(ND):
                ws2 = o % 2
                wwait = []
                if w2slot_t[ws2] is not None:
                    wwait.append((self.c_ct, w2slot_t[ws2]))
                elif tq == 0 and o < 2:
                    wwait.append((self.c_ct, self.wo_f_done))
                    wwait.append((self.c_cg, self.qkv_prep_cg))
                    wwait.append((self.c_cv, self.qkv_prep_cv))
                if tq == 0 and o < 2:
                    wwait.append((self.c_pc, self.precast_t))
                wdma = self.dma("y", sb["ARC"][:, ws2 * 2816:(ws2 + 1) * 2816],
                                dr["w2s"][l][o], waits=wwait,
                                inc=(self.c_w20 if ws2 == 0 else self.c_w21))
                bidx = o % 4
                for kk in range(NHC):
                    w = []
                    if kk == 0:
                        w += [self.bank_free[bidx], (self.c_cv, h2_done),
                              ((self.c_w20 if ws2 == 0 else self.c_w21), wdma)]
                    self.op("t", lambda t, b=self.bank(bidx),
                            lh=sb["ARC"][:, ws2 * 2816 + kk * 128:
                                         ws2 * 2816 + (kk + 1) * 128],
                            rh=H2C[:, kk * 512:(kk + 1) * 512],
                            st=(kk == 0), sp=(kk == NHC - 1):
                            t.matmul(b, lh, rh, start=st, stop=sp),
                            waits=w, inc=self.c_ct)
                w2mm = self.c_ct.v
                w2slot_t[ws2] = w2mm
                ev = self.op("v", lambda v, b=self.bank(bidx),
                             d=X[:, o * T + tq * 512: o * T + tq * 512 + 512],
                             sc=sb["COLS"][:, 24 + o:25 + o]:
                             v.scalar_tensor_tensor(
                                 out=d, in0=b, scalar=sc,
                                 in1=d, op0=ALU.add, op1=ALU.add),
                             waits=[(self.c_ct, w2mm)], inc=self.c_cv)
                self.bank_free[bidx] = (self.c_cv, ev)
            w2mm_prev = self.c_ct.v
        self.w2mm_done = self.c_ct.v
        self.ffn_done_ct = self.c_ct.v
        self.ffn_done_cv = self.c_cv.v

    def st_final(self):
        sb, dr = self.sb, self.dr
        XA_lo = sb["ARA"].bitcast(F32)
        XA_hi = sb["ARB"].bitcast(F32)
        d1 = self.dma("y", XA_lo[:, 0:8192], dr["xa_spill"][:, 0:8192],
                      waits=[(self.c_ct, self.ffn_done_ct),
                             (self.c_cv, self.ffn_done_cv),
                             (self.c_cs, self.c_cs.v)], inc=self.c_rel)
        d2 = self.dma("y", XA_hi[:, 0:8192], dr["xa_spill"][:, 8192:16384],
                      inc=self.c_rel)
        last = None
        for i in range(16):
            src = (XA_lo if i < 8 else XA_hi)[:, (i % 8) * 1024:(i % 8) * 1024 + 1024]
            w = []
            if i == 0:
                w = [(self.c_rel, d2), (self.c_cv, self.ffn_done_cv)]
            last = self.op("v", lambda v, src=src,
                           d=sb["X"][:, i * 1024:(i + 1) * 1024]:
                           v.tensor_tensor(out=d, in0=d, in1=src, op=ALU.add),
                           waits=w, inc=self.c_cv)
        self.v_x0_done = last
        self.st_out(ln=True)


def VB_ap(sb, vb_base, jc, hb):
    return sb["ARB"][0:128, vb_base + jc * 130 + hb * 65:
                     vb_base + jc * 130 + hb * 65 + 65]



def build_nc(upto="full"):
    mk = MK(upto)
    mk.st_setup()
    if upto == "roundtrip":
        mk.st_out(ln=False)
        return mk.finalize()
    if upto == "proj0":
        mk.natk = 0
        mk.nat_free = [(mk.c_ct, 0), (mk.c_ct, 0)]
        mk.pair_ct = {}
        mk.proj_ready = {}
        mk.vb_ready = {}
        mk.wslot_ready = {}
        mk.proj_done_ct = {}
        mk.px_free = [(mk.c_cv, 0), (mk.c_cv, 0)]
        mk.a_wot()
        mk.a_pair_weights(0)
        for _ in mk.a_proj(0):
            pass
        mk.dbg("d_qt", mk.sb["ARA"][:, 0:2048], (128, 2048))
        mk.dbg("d_kt", mk.sb["ARA"][:, 2048:4096], (128, 2048))
        mk.dbg("d_vb", mk.sb["ARB"][:, 8192:8192 + 2080], (128, 2080))
        mk.dbg("d_wqt", mk.sb["ARW"][:, 2048:3072], (128, 1024))
        mk.dbg("d_wot", mk.sb["ARA"][:, 8192:16384], (128, 8192))
        mk.dbg("d_xb", mk.sb["ARC"][:, 0:2048], (128, 2048))
        return mk.finalize()
    if upto == "attn01":
        mk.natk = 0
        mk.nat_free = [(mk.c_ct, 0), (mk.c_ct, 0)]
        mk.pair_ct = {}
        mk.proj_ready = {}
        mk.vb_ready = {}
        mk.wslot_ready = {}
        mk.proj_done_ct = {}
        mk.et_free = [(mk.c_ct, 0), (mk.c_ct, 0)]
        mk.nb_free = [(mk.c_cs, 0), (mk.c_cs, 0)]
        mk.rc_free = [(mk.c_ct, 0), (mk.c_ct, 0)]
        mk.pn_free = [(mk.c_cv, 0), (mk.c_cv, 0)]
        mk.px_free = [(mk.c_cv, 0), (mk.c_cv, 0)]
        mk.wo_done = {}
        mk._gg = 0
        mk.a_wot()
        mk.a_pair_weights(0)
        for _ in mk.a_proj(0):
            pass
        mk.a_attn(0)
        mk.a_attn(1)
        mk.dbg("d_ao", mk.sb["ARB"][:, 0:2048], (128, 2048))
        return mk.finalize()
    mk.st_partA()
    if upto == "parta":
        mk.st_out(ln=False)
        return mk.finalize()
    if upto == "wtest":
        mk.dout("d_w1s", (NHC, 128, 2048), BF16)
        mk.dout("d_w2s", (ND, 128, 2816), BF16)
        mk.op("y", lambda y: y.nop(), waits=[(mk.c_sw, mk.precast_t)])
        mk.dma("y", mk.dr["d_w1s"], mk.dr["w1s"][0])
        t = mk.dma("y", mk.dr["d_w2s"], mk.dr["w2s"][0])
        mk.op("y", lambda y: y.nop(), waits=[(mk.c_sh, t)])
        return mk.finalize()
    if upto == "w2check":
        mk.st_partB(0)
        mk.st_partB(1)
        mk.st_final()
        mk.dout("d_w2s", (2, ND, 128, 2816), BF16)
        mk.op("y", lambda y: y.nop(), waits=[(mk.c_sh, mk.c_sh.v),
                                             (mk.c_sm, mk.c_sm.v)])
        t = mk.dma("y", mk.dr["d_w2s"], mk.dr["w2s"])
        mk.op("y", lambda y: y.nop(), waits=[(mk.c_sh, t)])
        return mk.finalize()
    if upto == "bqkv":
        mk.b_prep(0)
        mk.b_qkv(0)
        mk.dbg("d_qt2", mk.sb["ARA"][:, 0:8192], (128, 8192))
        mk.dbg("d_kt2", mk.sb["ARA"][:, 8192:16384], (128, 8192))
        mk.dbg("d_vb2", mk.sb["ARB"][:, 0:8320], (128, 8320))
        return mk.finalize()
    if upto == "battn":
        mk.b_prep(0)
        mk.b_qkv(0)
        mk.attn_state_init()
        for h in range(HB + 1):
            if h < HB:
                mk.b_pass1(0, h)
                mk.b_transport(0, h)
            if h >= 1:
                mk.b_pass2(0, h - 1)
        mk.dbg("d_ao2", mk.sb["ARC"][:, 8192:16384], (128, 8192))
        return mk.finalize()
    if upto == "bffn":
        mk.b_prep(0)
        mk.b_qkv(0)
        mk.wo_dma_t = mk.dma(
            "g", mk.sb["ARW"][:, 0:4096].rearrange("p (k m) -> p k m", k=4),
            mk.dr["wo_f"][0].rearrange("(k p) m -> p k m", p=128),
            waits=[(mk.c_ct, mk.qkv_wslot_done)], inc=mk.c_wod)
        mk.emit_precast()
        mk.attn_state_init()
        for h in range(HB + 1):
            feeder = mk.b_pass1_iter(0, h) if h < HB else None
            if h == 0:
                for _ in feeder:
                    pass
                feeder = None
            if h >= 1:
                for g in range(4):
                    mk.b_pass2_g(0, h - 1, g, feeder=feeder)
            if feeder is not None:
                for _ in feeder:
                    pass
            if h < HB:
                mk.b_transport(0, h)
        for fn in mk._den_pending:
            fn()
        mk._den_pending = []
        mk.ao2_ready = mk.c_cv.v
        mk.attn_done_ct = mk.c_ct.v
        mk.bank_free[0] = mk.pa_free[0] or mk.bank_free[0]
        mk.bank_free[1] = mk.pa_free[0] or mk.bank_free[1]
        mk.bank_free[2] = mk.pa_free[1] or mk.bank_free[2]
        mk.bank_free[3] = mk.pa_free[1] or mk.bank_free[3]
        mk.b_wo(0)
        mk.b_ffn(0)
        mk.dbg("d_hln", mk.sb["ARB"][:, 8320:12416], (128, 4096))
        mk.dbg("d_xb2", mk.sb["ARB"][:, 0:4096], (128, 4096))
        mk.dbg("d_sm", mk.sb["SM"][:, :], (128, 1536))
        mk.dbg("d_smb", mk.sb["SMB"][:, :], (128, 1280))
        mk.dbg("d_sqb", mk.sb["ARB"][:, 4096:8192], (128, 4096))
        mk.dbg("d_h2c", mk.sb["ARA"][:, 0:11264], (128, 11264))
        mk.st_out(ln=False)
        return mk.finalize()
    if upto == "bwo":
        mk.b_prep(0)
        mk.b_qkv(0)
        mk.wo_dma_t = mk.dma(
            "g", mk.sb["ARW"][:, 0:4096].rearrange("p (k m) -> p k m", k=4),
            mk.dr["wo_f"][0].rearrange("(k p) m -> p k m", p=128),
            waits=[(mk.c_ct, mk.qkv_wslot_done)], inc=mk.c_wod)
        mk.emit_precast()
        mk.attn_state_init()
        for h in range(HB + 1):
            feeder = mk.b_pass1_iter(0, h) if h < HB else None
            if h == 0:
                for _ in feeder:
                    pass
                feeder = None
            if h >= 1:
                for g in range(4):
                    mk.b_pass2_g(0, h - 1, g, feeder=feeder)
            if feeder is not None:
                for _ in feeder:
                    pass
            if h < HB:
                mk.b_transport(0, h)
        for fn in mk._den_pending:
            fn()
        mk._den_pending = []
        mk.ao2_ready = mk.c_cv.v
        mk.attn_done_ct = mk.c_ct.v
        mk.bank_free[0] = mk.pa_free[0] or mk.bank_free[0]
        mk.bank_free[1] = mk.pa_free[0] or mk.bank_free[1]
        mk.bank_free[2] = mk.pa_free[1] or mk.bank_free[2]
        mk.bank_free[3] = mk.pa_free[1] or mk.bank_free[3]
        mk.b_wo(0)
        mk.st_out(ln=False)
        return mk.finalize()
    mk.st_partB(0)
    if upto == "partb0":
        mk.st_out(ln=False)
        return mk.finalize()
    mk.st_partB(1)
    mk.st_final()
    return mk.nc if False else mk.finalize()


def make_inputs_per_core(inputs):
    c = host_constants()
    x = np.ascontiguousarray(inputs["L0_attn_x"], dtype=np.float32)
    b1 = np.asarray(inputs["L0_ffn_b1"], np.float32)
    b1a = np.zeros((2, HIDP), np.float32)
    b1g = np.zeros((2, HIDP), np.float32)
    b1a[:, :HID] = b1[:, :HID]
    b1g[:, :HID] = b1[:, HID:]
    w1 = np.asarray(inputs["L0_ffn_w1"], np.float32)
    w1i = np.zeros((2, D, NHC, 2, 128), np.float32)
    w1a = np.zeros((2, D, HIDP), np.float32)
    w1g = np.zeros((2, D, HIDP), np.float32)
    w1a[:, :, :HID] = w1[:, :, :HID]
    w1g[:, :, :HID] = w1[:, :, HID:]
    w1i[:, :, :, 0, :] = w1a.reshape(2, D, NHC, 128)
    w1i[:, :, :, 1, :] = w1g.reshape(2, D, NHC, 128)
    w1i = np.ascontiguousarray(w1i.reshape(2, D, 2 * HIDP))
    w2 = np.asarray(inputs["L0_ffn_w2"], np.float32)
    w2p = np.zeros((2, HIDP, D), np.float32)
    w2p[:, :HID, :] = w2
    qs = np.asarray(inputs["L0_ffn_q_scale"], np.float32)
    ks = np.asarray(inputs["L0_ffn_k_scale"], np.float32)
    qs_rep = np.ascontiguousarray(np.broadcast_to(
        np.tile(qs, (1, HB))[:, None, :], (2, 128, INNER))).astype(np.float32)
    ks_rep = np.ascontiguousarray(np.broadcast_to(
        np.tile(ks, (1, HB))[:, None, :], (2, 128, INNER))).astype(np.float32)
    import ml_dtypes
    shared = {
        "wq": np.ascontiguousarray(inputs["L0_attn_Wq"], np.float32),
        "wkv": np.ascontiguousarray(inputs["L0_attn_Wkv"], np.float32),
        "wo_a": np.ascontiguousarray(inputs["L0_attn_Wo"], np.float32),
        "wqkv": np.ascontiguousarray(inputs["L0_ffn_wqkv"], np.float32),
        "qs_rep": qs_rep,
        "ks_rep": ks_rep,
        "wo_f": np.ascontiguousarray(inputs["L0_ffn_wo"], np.float32),
        "lng_c": col_major(np.asarray(inputs["L0_ffn_ln_g"], np.float32)),
        "lnb_c": col_major(np.asarray(inputs["L0_ffn_ln_b"], np.float32)),
        "w1i": w1i,
        "b1a_c": col_major(b1a),
        "b1g_c": col_major(b1g),
        "w2p": w2p,
        "b2_c": col_major(np.asarray(inputs["L0_ffn_b2"], np.float32)),
        "ident_f": c["ident_f"],
        "ident_b": c["ident_b"].astype(ml_dtypes.bfloat16),
        "mask_tri": c["mask_tri"],
        "mask_low": c["mask_low"],
        "ones_b": c["ones_b"].astype(ml_dtypes.bfloat16),
        "rot_qc": c["rot_qc"],
        "rot_qs": c["rot_qs"],
        "rot_kc": c["rot_kc"],
        "rot_ks": c["rot_ks"],
    }
    in_maps = []
    for core in range(8):
        m = dict(shared)
        m["x0"] = np.ascontiguousarray(x[core])
        in_maps.append(m)
    return in_maps


def kernel(**inputs):
    in_maps = make_inputs_per_core(inputs)
    nc = build_nc("full")
    res = run_bass_kernel_spmd(nc, in_maps, core_ids=list(range(8)))
    out = np.stack([np.asarray(r["y"]) for r in res.results], axis=0)
    return out.astype(np.float32)


# version 24
# speedup vs baseline: 1.0067x; 1.0067x over previous
import numpy as np
import concourse.bass as bass
import concourse.mybir as mybir
from concourse.bass_utils import run_bass_kernel_spmd

F32 = mybir.dt.float32
BF16 = mybir.dt.bfloat16
AF = mybir.ActivationFunctionType
ALU = mybir.AluOpType
AX = mybir.AxisListType

T = 2048
D = 1024
NT = T // 128
ND = D // 128
HA = 16
DH = 64
HB = 8
INNER = 512
WIN = 512
HID = 2730
HIDP = 2816
NHC = HIDP // 128
NEG = -1.0e5


class Ctr:
    def __init__(self, sem):
        self.sem = sem
        self.v = 0

    def inc(self, n=1):
        self.v += n
        return self.v


class Em:

    def __init__(self):
        self.q = {"t": [], "v": [], "s": [], "g": [], "y": []}

    def t(self, f): self.q["t"].append(f)
    def v(self, f): self.q["v"].append(f)
    def s(self, f): self.q["s"].append(f)
    def g(self, f): self.q["g"].append(f)
    def y(self, f): self.q["y"].append(f)


def host_constants():
    c = {}
    c["ident_f"] = np.eye(128, dtype=np.float32)
    c["ident_b"] = np.eye(128, dtype=np.float32)
    tri = np.zeros((128, 128), dtype=np.float32)
    low = np.zeros((128, 128), dtype=np.float32)
    for p in range(128):
        tri[p, :p] = NEG
        low[p, p + 1:] = NEG
    c["mask_tri"] = tri
    c["mask_low"] = low
    c["ones_b"] = np.ones((128, 128), dtype=np.float32)
    half = np.arange(0, DH, 2, dtype=np.float64)
    inv_freq = 1.0 / 10000.0 ** (half / DH)
    t = np.arange(T, dtype=np.float64)
    freqs = t[:, None] * inv_freq[None, :]
    freqs = np.concatenate([freqs, freqs], axis=-1)
    base = (half + 0.4 * DH) / (1.4 * DH)
    power = (t - T // 2) / float(WIN // 2)
    scale = base[None, :] ** power[:, None]
    scale = np.concatenate([scale, scale], axis=-1)
    cos = np.cos(freqs)
    sin = np.sin(freqs)
    sign = np.ones((1, DH))
    sign[0, :DH // 2] = -1.0
    qc = 8.0 * cos * scale
    qs = 8.0 * sin * scale * sign
    kc = cos / scale
    ks = sin / scale * sign

    def tm(a):
        return np.ascontiguousarray(
            a.reshape(NT, 128, DH).transpose(1, 0, 2).reshape(128, NT * DH)
        ).astype(np.float32)

    c["rot_qc"] = tm(qc)
    c["rot_qs"] = tm(qs)
    c["rot_kc"] = tm(kc)
    c["rot_ks"] = tm(ks)
    return c


def col_major(a, npart=128):
    n = a.shape[-1] // npart
    return np.ascontiguousarray(
        a.reshape(a.shape[:-1] + (n, npart)).swapaxes(-1, -2))


class MK:

    def __init__(self, upto="full"):
        self.upto = upto
        nc = bass.Bass("TRN2", target_bir_lowering=False, debug=False)
        self.nc = nc
        self.em = Em()
        self.dr = {}
        self.sb = {}
        self.ps = {}
        self._semstack = []
        self._declare_dram()
        self._alloc_sbuf()
        self._alloc_psum()
        self._alloc_sems()
        self.bank_free = [(self.sems["ct"], 0)] * 4

    def din(self, name, shape, dt=F32):
        self.dr[name] = self.nc.dram_tensor(name, list(shape), dt,
                                            kind="ExternalInput").ap()

    def dout(self, name, shape, dt=F32):
        self.dr[name] = self.nc.dram_tensor(name, list(shape), dt,
                                            kind="ExternalOutput").ap()

    def dint(self, name, shape, dt=F32):
        self.dr[name] = self.nc.dram_tensor(name, list(shape), dt,
                                            kind="Internal").ap()

    def _declare_dram(self):
        self.din("x0", (T, D))
        self.din("wq", (D, D))
        self.din("wkv", (2 * D, D))
        self.din("wo_a", (D, D))
        self.din("wqkv", (2, D, 3 * INNER))
        self.din("qs_rep", (2, 128, INNER))
        self.din("ks_rep", (2, 128, INNER))
        self.din("wo_f", (2, INNER, D))
        self.din("lng_c", (2, 128, ND))
        self.din("lnb_c", (2, 128, ND))
        self.din("w1i", (2, D, 2 * HIDP))
        self.din("b1a_c", (2, 128, NHC))
        self.din("b1g_c", (2, 128, NHC))
        self.din("w2p", (2, HIDP, D))
        self.din("b2_c", (2, 128, ND))
        self.din("ident_f", (128, 128))
        self.din("ident_b", (128, 128), BF16)
        self.din("mask_tri", (128, 128))
        self.din("mask_low", (128, 128))
        self.din("ones_b", (128, 128), BF16)
        self.din("rot_qc", (128, NT * DH))
        self.din("rot_qs", (128, NT * DH))
        self.din("rot_kc", (128, NT * DH))
        self.din("rot_ks", (128, NT * DH))
        self.dout("y", (T, D))
        self.dint("xa_spill", (128, ND * T))
        self.dint("w1s", (2, NHC, 128, 2048), BF16)
        self.dint("w2s", (2, ND, 128, 2816), BF16)
        self.dint("mscr", (2 * HB, 16, 128), BF16)

    def _alloc_sbuf(self):
        def sbuf(name, shape, dt):
            self.sb[name] = self.nc.alloc_sbuf_tensor(name, list(shape), dt).ap()
        sbuf("X", (128, ND * T), F32)
        sbuf("ARA", (128, 16384), BF16)
        sbuf("ARB", (128, 16384), BF16)
        sbuf("ARC", (128, 16384), BF16)
        sbuf("ARW", (128, 8192), BF16)
        sbuf("SM", (128, 1536), F32)
        sbuf("QKS", (128, 2 * INNER), F32)
        sbuf("STG", (128, 2 * 512), BF16)
        sbuf("STK", (128, 2 * 128), BF16)
        sbuf("ET", (128, 4 * 512), BF16)
        sbuf("PRP", (128, 3 * 512), F32)
        sbuf("IDF", (128, 128), F32)
        sbuf("IDB", (128, 128), BF16)
        sbuf("MTRI", (128, 128), F32)
        sbuf("MLOW", (128, 128), F32)
        sbuf("ONE", (128, 128), BF16)
        sbuf("MTB", (128, 256), BF16)
        sbuf("COLS", (128, 88), F32)
        sbuf("HSJ", (128, 2048), BF16)
        sbuf("SMB", (128, 1280), BF16)

    def _alloc_psum(self):
        def psum(name, shape):
            self.ps[name] = self.nc.alloc_psum_tensor(name, list(shape), F32).ap()
        psum("PA", (128, 1024))
        psum("PB", (128, 1024))
        psum("PN0", (128, 512))
        psum("PN1", (128, 512))
        psum("PX", (128, 512))
        psum("PY", (128, 512))

    def bank(self, i):
        t = self.ps["PA"] if (i % 4) < 2 else self.ps["PB"]
        off = 512 * (i % 2)
        return t[:, off:off + 512]

    def _alloc_sems(self):
        self.sems = {}
        for name in ["sw", "sh", "sm", "ct", "cv", "cs", "cg", "fb", "fb2",
                     "x0", "nat0", "nat1", "qkv0", "qkv1", "wod", "w10", "w11",
                     "w20", "w21", "m10", "m11", "m20", "m21", "rel", "spl",
                     "pc", "prp", "st0", "st1"]:
            h = self.nc.semaphore(name)
            hh = h.__enter__()
            self._semstack.append(h)
            self.sems[name] = Ctr(hh)
        for k, v in self.sems.items():
            setattr(self, "c_" + k, v)

    def finalize(self):
        em = self.em
        with self.nc.allow_low_precision(
                reason="bf16 matmul operands w/ fp32 accumulation per stated "
                       "problem tolerance"), self.nc.Block() as block:
            @block.tensor
            def _(t):
                for f in em.q["t"]:
                    f(t)

            @block.vector
            def _(v):
                for f in em.q["v"]:
                    f(v)

            @block.scalar
            def _(s):
                for f in em.q["s"]:
                    f(s)

            @block.gpsimd
            def _(g):
                for f in em.q["g"]:
                    f(g)

            @block.sync
            def _(y):
                for f in em.q["y"]:
                    f(y)
        for h in self._semstack:
            h.__exit__(None, None, None)
        return self.nc

    def op(self, eng, fn, waits=(), inc=None, inc_n=1):
        tgt = inc.inc(inc_n) if inc is not None else None

        def closure(e, fn=fn, waits=tuple(waits), inc=inc, inc_n=inc_n):
            for (c, v) in waits:
                e.wait_ge(c.sem, v)
            r = fn(e)
            if inc is not None:
                r.then_inc(inc.sem, inc_n)
        self.em.q[eng].append(closure)
        return tgt

    def dma(self, eng, dst, src, waits=(), inc=None):
        inc = inc or (self.c_sw if eng == "g" else self.c_sh)
        return self.op(eng, lambda e, d=dst, s=src: e.dma_start(out=d, in_=s),
                       waits=waits, inc=inc, inc_n=16)

    def ld_sync(self, dst, src, wait=None):
        c = self.c_sh

        def f(y, dst=dst, src=src, wait=wait):
            if wait is not None:
                y.wait_ge(wait[0].sem, wait[1])
            y.dma_start(out=dst, in_=src).then_inc(c.sem, 16)
        self.em.y(f)
        return c.inc(16)

    def ld_gp(self, dst, src, wait=None):
        c = self.c_sw

        def f(g, dst=dst, src=src, wait=wait):
            if wait is not None:
                g.wait_ge(wait[0].sem, wait[1])
            g.dma_start(out=dst, in_=src).then_inc(c.sem, 16)
        self.em.g(f)
        return c.inc(16)

    def st_setup(self):
        sb, dr, ps = self.sb, self.dr, self.ps
        self.ld_sync(sb["IDF"][:, :], dr["ident_f"][:, :])
        self.ld_sync(sb["IDB"][:, :], dr["ident_b"][:, :])
        self.ld_sync(sb["MTRI"][:, :], dr["mask_tri"][:, :])
        self.ld_sync(sb["MLOW"][:, :], dr["mask_low"][:, :])
        vm = self.ld_sync(sb["ONE"][:, :], dr["ones_b"][:, :])
        self.const_t = vm
        self.op("s", lambda ss: ss.copy(out=sb["MTB"][:, 0:128],
                                        in_=sb["MTRI"][:, :]),
                waits=[(self.c_sh, vm)])
        self.mtb_ready = (self.c_cs,
                          self.op("s", lambda ss:
                                  ss.copy(out=sb["MTB"][:, 128:256],
                                          in_=sb["MLOW"][:, :]),
                                  inc=self.c_cs))

        xtm = sb["ARB"].bitcast(F32)
        X = sb["X"]
        XB = sb["ARC"]
        x0r = dr["x0"].rearrange("(n p) d -> p n d", p=128)
        tz = self.c_ct.v
        vz = self.c_cv.v
        k = 0
        for half in range(2):
            wait = None
            if half == 1:
                wait = (self.c_ct, tz + 64)
            v_x0 = self.dma("y", xtm[:, 0:8 * D],
                            x0r[:, half * 8:(half + 1) * 8, :],
                            waits=[wait] if wait else [], inc=self.c_x0)
            for idx in range(8 * ND):
                n, c = divmod(idx, ND)
                nn = half * 8 + n
                pt = self.bank(k)[:, 0:128]
                vt = self.c_ct.inc()

                def f(t, n=n, c=c, pt=pt, k=k, v_x0=v_x0, first=(idx == 0)):
                    if first:
                        t.wait_ge(self.c_sh.sem, self.const_t)
                        t.wait_ge(self.c_x0.sem, v_x0)
                    if k >= 4:
                        t.wait_ge(self.c_cv.sem, vz + 2 * (k - 3))
                    t.matmul(pt, xtm[:, n * D + c * 128: n * D + (c + 1) * 128],
                             sb["IDF"][:, :], start=True, stop=True
                             ).then_inc(self.c_ct.sem, 1)
                self.em.t(f)

                dst_f = X[:, c * T + nn * 128: c * T + (nn + 1) * 128]
                dst_b = XB[:, c * T + nn * 128: c * T + (nn + 1) * 128]
                self.c_cv.inc(2)

                def fv(v, pt=pt, dst_f=dst_f, dst_b=dst_b, vt=vt):
                    v.wait_ge(self.c_ct.sem, vt)
                    v.tensor_copy(out=dst_f, in_=pt).then_inc(self.c_cv.sem, 1)
                    v.tensor_copy(out=dst_b, in_=pt).then_inc(self.c_cv.sem, 1)
                self.em.v(fv)
                k += 1
        self.v_x0_done = self.c_cv.v
        self.bank_free = [(self.c_cv, self.v_x0_done)] * 4
        self.op("v", lambda v: v.memset(self.sb["COLS"][:, 4:5], 1e-5))


    def st_out(self, ln=False):
        sb, dr = self.sb, self.dr
        X = sb["X"]
        ytile = sb["ARW"].bitcast(F32)
        yout = dr["y"].rearrange("(n p) d -> p n d", p=128)
        x_ready = (self.c_cv, self.c_cv.v)
        store_t = {}
        for n in range(NT):
            dst = ytile[:, (n % 2) * D:(n % 2) * D + D]
            evs = []
            for c in range(ND):
                b = self.bank(c % 4)
                w = [self.bank_free[c % 4]]
                if n == 0 and c == 0:
                    w.append(x_ready)
                tt = self.op("t", lambda t, b=b[:, 0:128], src=X[:, c * T + n * 128: c * T + (n + 1) * 128]:
                             t.matmul(b, src, sb["IDF"][:, :], start=True, stop=True),
                             waits=w, inc=self.c_ct)
                wv = [(self.c_ct, tt)]
                if n >= 2 and c == 0:
                    wv.append(((self.c_st0 if n % 2 == 0 else self.c_st1),
                               store_t[n - 2]))
                ev = self.op("v", lambda v, b=b[:, 0:128], d=dst[:, c * 128:(c + 1) * 128]:
                             v.tensor_copy(out=d, in_=b),
                             waits=wv, inc=self.c_cv)
                self.bank_free[c % 4] = (self.c_cv, ev)
                evs.append(ev)
            vdone = (self.c_cv, evs[-1])
            if ln:
                vdone = self._ln_row_tile(dst, vdone)
            store_t[n] = self.dma("y", yout[:, n:n + 1, :], dst.unsqueeze(1),
                                  waits=[vdone],
                                  inc=(self.c_st0 if n % 2 == 0 else self.c_st1))
        self.op("y", lambda y: y.nop(),
                waits=[(self.c_sh, self.c_sh.v), (self.c_st0, self.c_st0.v),
                       (self.c_st1, self.c_st1.v)])

    def _ln_row_tile(self, dst, vdone):
        sb = self.sb
        sm = sb["SM"]
        s1, s2, mu, e2, mu2, var, rstd = (sm[:, i:i + 1] for i in
                                          (0, 4, 8, 12, 16, 20, 24))
        sq = sb["PRP"][:, 0:1024]
        fill = sm[:, 127:128]
        r1 = self.op("v", lambda v: v.reduce_sum(s1, dst, axis=AX.X),
                     waits=[vdone], inc=self.c_cv)
        r2 = self.op("v", lambda v: v.tensor_mul(sq, dst, dst),
                     inc=self.c_cv)
        r3 = self.op("v", lambda v: v.reduce_sum(s2, sq, axis=AX.X),
                     waits=[(self.c_cv, r2)], inc=self.c_cv)
        m1 = self.op("v", lambda v: v.tensor_scalar(out=mu, in0=s1,
                                                    scalar1=1.0 / D,
                                                    scalar2=None, op0=ALU.mult),
                     waits=[(self.c_cv, r1)], inc=self.c_cv)
        self.op("v", lambda v: v.tensor_scalar(out=e2, in0=s2,
                                               scalar1=1.0 / D,
                                               scalar2=None, op0=ALU.mult),
                waits=[(self.c_cv, r3)], inc=self.c_cv)
        m3 = self.op("v", lambda v: v.tensor_scalar(out=mu2, in0=mu,
                                                    scalar1=mu, scalar2=None,
                                                    op0=ALU.mult),
                     waits=[(self.c_cv, m1)], inc=self.c_cv)
        vv = self.op("v", lambda v: v.tensor_scalar(
            out=var, in0=e2, scalar1=mu2, scalar2=None, op0=ALU.subtract),
            waits=[(self.c_cv, m3)], inc=self.c_cv)
        st0 = self.op("s", lambda s: s.activation(
            out=sm[:, 28:29], in_=var, func=AF.Ln, bias=sb["COLS"][:, 4:5]),
            waits=[(self.c_cv, vv)], inc=self.c_cs)
        st = self.op("s", lambda s: s.activation(
            out=rstd, in_=sm[:, 28:29], func=AF.Exp, scale=-0.5),
            waits=[(self.c_cs, st0)], inc=self.c_cs)
        out = self.op("v", lambda v: v.tensor_scalar(
            out=dst, in0=dst, scalar1=mu, scalar2=rstd, op0=ALU.subtract,
            op1=ALU.mult), waits=[(self.c_cs, st), (self.c_cv, vv)],
            inc=self.c_cv)
        return (self.c_cv, out)

    def dbg(self, name, ap, shape, dt=None):
        dt = dt or ap.dtype
        self.dout(name, shape, dt)
        waits = [(self.c_ct, self.c_ct.v), (self.c_cv, self.c_cv.v),
                 (self.c_cs, self.c_cs.v)]
        self.dma("y", self.dr[name][:, :] if len(shape) == 2 else self.dr[name],
                 ap, waits=waits)
        self.op("y", lambda y: y.nop(), waits=[(self.c_sh, self.c_sh.v)])

    def st_partA(self):
        self.natk = 0
        self.nat_free = [(self.c_ct, 0), (self.c_ct, 0)]
        self.pair_ct = {}
        self.proj_ready = {}
        self.vb_ready = {}
        self.wslot_ready = {}
        self.proj_done_ct = {}
        self.et_free = [None, None, None, None]
        self.nb_free = [(self.c_cs, 0), (self.c_cs, 0)]
        self.rc_free = [(self.c_ct, 0), (self.c_ct, 0)]
        self.pn_free = [(self.c_cv, 0)] * 4
        self.ao_ready = {0: None, 1: None}
        self.wo_done = {}
        self._gg = 0
        self._den_pending = []
        self.a_wot()
        self._cv_ao = {0: [], 1: []}
        fd = None
        for p in range(9):
            if p < 8:
                self.a_pair_weights(p)
                fd = self.a_proj(p)
                if p == 0:
                    for _ in fd:
                        pass
                    fd = None
                else:
                    next(fd)
                    next(fd)
            else:
                fd = None
            if p >= 1:
                self.a_attn(2 * (p - 1), feeder=fd)
                self.a_attn(2 * (p - 1) + 1, feeder=fd)
                if fd is not None:
                    for _ in fd:
                        pass
            if p == 4:
                for fn in self._den_pending:
                    fn()
                self._den_pending = []
                self.ao_ready[0] = self.c_cv.v
                self.a_wo_half(0)
        for fn in self._den_pending:
            fn()
        self._den_pending = []
        self.ao_ready[1] = self.c_cv.v
        self.a_wo_half(1)
        self.a_ln1()


    def nat_slot(self):
        s = self.natk % 2
        self.natk += 1
        return s, self.sb["ARW"][:, s * 1024:(s + 1) * 1024]

    def nat_sem(self, s):
        return self.c_nat0 if s == 0 else self.c_nat1

    def a_wot(self):
        sb, dr = self.sb, self.dr
        WOT = sb["ARA"]
        for o in range(ND):
            s, nat = self.nat_slot()
            dt = self.dma("g", nat, dr["wo_a"][o * 128:(o + 1) * 128, :],
                          waits=[self.nat_free[s]], inc=self.nat_sem(s))
            last = None
            for c in range(ND):
                b = self.bank(c % 4)
                w = [self.bank_free[c % 4]]
                if c == 0:
                    w.append((self.nat_sem(s), dt))
                tt = self.op("t", lambda t, b=b[:, 0:128],
                             src=nat[:, c * 128:(c + 1) * 128]:
                             t.matmul(b, src, sb["IDB"][:, :], start=True, stop=True),
                             waits=w, inc=self.c_ct)
                ev = self.op("v", lambda v, b=b[:, 0:128],
                             d=WOT[:, 8192 + c * 1024 + o * 128: 8192 + c * 1024 + (o + 1) * 128]:
                             v.tensor_copy(out=d, in_=b),
                             waits=[(self.c_ct, tt)], inc=self.c_cv)
                self.bank_free[c % 4] = (self.c_cv, ev)
                last = tt
            self.nat_free[s] = (self.c_ct, last)
        self.wot_ready = self.c_cv.v

    def a_pair_weights(self, p):
        sb, dr = self.sb, self.dr
        s = p % 2
        base = 2048 + s * 3072
        srcs = [dr["wq"][p * 128:(p + 1) * 128, :],
                dr["wkv"][p * 128:(p + 1) * 128, :],
                dr["wkv"][D + p * 128: D + (p + 1) * 128, :]]
        slot_free = (self.c_ct, self.proj_done_ct.get(p - 2, 0))
        evs = []
        for wi, src in enumerate(srcs):
            ns, nat = self.nat_slot()
            dt = self.dma("g", nat, src, waits=[self.nat_free[ns]],
                          inc=self.nat_sem(ns))
            last = None
            for c in range(ND):
                b = self.bank((c + wi) % 4)
                w = [self.bank_free[(c + wi) % 4]]
                if c == 0:
                    w.append((self.nat_sem(ns), dt))
                    w.append(slot_free)
                tt = self.op("t", lambda t, b=b[:, 0:128],
                             src2=nat[:, c * 128:(c + 1) * 128]:
                             t.matmul(b, src2, sb["IDB"][:, :], start=True, stop=True),
                             waits=w, inc=self.c_ct)
                ev = self.op("v", lambda v, b=b[:, 0:128],
                             d=sb["ARW"][:, base + wi * 1024 + c * 128: base + wi * 1024 + (c + 1) * 128]:
                             v.tensor_copy(out=d, in_=b),
                             waits=[(self.c_ct, tt)], inc=self.c_cv)
                self.bank_free[(c + wi) % 4] = (self.c_cv, ev)
                last = tt
            self.nat_free[ns] = (self.c_ct, last)
            evs.append(self.c_cv.v)
        self.wslot_ready[p] = self.c_cv.v

    def a_proj(self, p):
        sb = self.sb
        s = p % 2
        QT = sb["ARA"][:, s * 4096: s * 4096 + 2048]
        KT = sb["ARA"][:, s * 4096 + 2048: s * 4096 + 4096]
        VB = sb["ARB"][:, 8192 + s * 2080: 8192 + (s + 1) * 2080]
        XB = sb["ARC"]
        WQ = sb["ARW"][:, 2048 + s * 3072: 2048 + s * 3072 + 1024]
        WK = sb["ARW"][:, 2048 + s * 3072 + 1024: 2048 + s * 3072 + 2048]
        WV = sb["ARW"][:, 2048 + s * 3072 + 2048: 2048 + s * 3072 + 3072]
        wready = (self.c_cv, self.wslot_ready[p])
        slot_free = (self.c_ct, self.pair_ct.get(p - 2, 0))
        vb_ones = bass.AP(sb["ARB"].tensor, 8192 + s * 2080 + 64,
                          [[16384, 128], [130, 16], [65, 2]])
        mz = self.op("v", lambda v: v.memset(vb_ones, 1.0),
                     waits=[slot_free], inc=self.c_cv)
        self.vb_ready[p] = mz
        for wi, (W, DST) in enumerate([(WQ, QT), (WK, KT)]):
            for c in range(ND):
                for tc in range(4):
                    w = []
                    if c == 0:
                        w.append(self.bank_free[tc])
                        if tc == 0:
                            w.append(wready)
                            w.append(slot_free)
                            if wi == 0 and p == 0:
                                w.append((self.c_cv, self.v_x0_done))
                    self.op("t", lambda t, b=self.bank(tc),
                            lh=W[:, c * 128:(c + 1) * 128],
                            rh=XB[:, c * T + tc * 512: c * T + tc * 512 + 512],
                            st=(c == 0), sp=(c == ND - 1):
                            t.matmul(b, lh, rh, start=st, stop=sp),
                            waits=w, inc=self.c_ct)
            mmt = self.c_ct.v
            for tc in range(4):
                ev = self.op("v", lambda v, b=self.bank(tc),
                             d=DST[:, tc * 512:(tc + 1) * 512]:
                             v.tensor_copy(out=d, in_=b),
                             waits=[(self.c_ct, mmt)], inc=self.c_cv)
                self.bank_free[tc] = (self.c_cv, ev)
            yield
        for n in range(NT):
            bidx = n % 4
            for c in range(ND):
                w = []
                if c == 0:
                    w.append(self.bank_free[bidx])
                tt = self.op("t", lambda t, b=self.bank(bidx)[:, 0:128],
                             lh=XB[:, c * T + n * 128: c * T + (n + 1) * 128],
                             rh=WV[:, c * 128:(c + 1) * 128],
                             st=(c == 0), sp=(c == ND - 1):
                             t.matmul(b, lh, rh, start=st, stop=sp),
                             waits=w, inc=self.c_ct)
            ev = None
            for hb in range(2):
                ev = self.op("v", lambda v, b=self.bank(bidx)[:, hb * 64:hb * 64 + 64],
                             d=VB[:, n * 130 + hb * 65: n * 130 + hb * 65 + 64]:
                             v.tensor_copy(out=d, in_=b),
                             waits=[(self.c_ct, tt)] if hb == 0 else [],
                             inc=self.c_cv)
            self.bank_free[bidx] = (self.c_cv, ev)
            yield
        self.proj_ready[p] = self.c_cv.v
        self.proj_done_ct[p] = self.c_ct.v

    def a_attn(self, h, feeder=None):
        sb = self.sb
        p, hb = divmod(h, 2)
        s = p % 2
        QT = sb["ARA"][:, s * 4096: s * 4096 + 2048]
        KT = sb["ARA"][:, s * 4096 + 2048: s * 4096 + 4096]
        vb_base = 8192 + s * 2080
        hh = h % 8
        ao_p0 = (hh % 2) * 64
        ao_cb = (hh // 2) * 2048
        pready = (self.c_cv, self.proj_ready[p])
        vbready = (self.c_cv, self.vb_ready[p])
        pnb = [self.ps["PN0"], self.ps["PN1"], self.ps["PX"], self.ps["PY"]]
        for g in range(4):
            gg = self._gg
            njc = 4 * g + 4
            pn = pnb[gg % 4]
            ndg = 4 * g
            units = [(jc, jc + 1) for jc in range(0, ndg - 1, 2)]
            units += [(jc,) for jc in range(ndg, njc)]
            for unit in units:
                simt = None
                for jc in unit:
                    diag = jc >= ndg
                    delta = max(0, (jc - ndg) * 128)
                    N = 512 - delta
                    sbk = jc % 4
                    w = [self.bank_free[sbk]]
                    if jc == 0:
                        w.append(pready)
                        w.append(self.mtb_ready)
                    if diag and N > 128:
                        qb = g * 512 + delta
                        self.op("t", lambda t, b=self.bank(sbk)[:, 128:N],
                                lh=KT[hb * 64: hb * 64 + 64,
                                      jc * 128:(jc + 1) * 128],
                                rh=QT[hb * 64: hb * 64 + 64,
                                      qb + 128:(g + 1) * 512]:
                                t.matmul(b, lh, rh, start=True, stop=False),
                                waits=w, inc=self.c_ct)
                        self.op("t", lambda t, b=self.bank(sbk)[:, 0:128]:
                                t.matmul(b, sb["IDB"][:, :],
                                         sb["MTB"][:, 0:128],
                                         start=False, stop=False),
                                inc=self.c_ct)
                        simt = self.op("t", lambda t, b=self.bank(sbk)[:, 0:128],
                                       lh=KT[hb * 64: hb * 64 + 64,
                                             jc * 128:(jc + 1) * 128],
                                       rh=QT[hb * 64: hb * 64 + 64,
                                             qb:qb + 128]:
                                       t.matmul(b, lh, rh, start=False,
                                                stop=True),
                                       inc=self.c_ct)
                    else:
                        if diag:
                            self.op("t", lambda t, b=self.bank(sbk)[:, 0:128]:
                                    t.matmul(b, sb["IDB"][:, :],
                                             sb["MTB"][:, 0:128],
                                             start=True, stop=False),
                                    waits=w, inc=self.c_ct)
                            w = []
                        simt = self.op("t", lambda t, b=self.bank(sbk)[:, 0:N],
                                       lh=KT[hb * 64: hb * 64 + 64,
                                             jc * 128:(jc + 1) * 128],
                                       rh=QT[hb * 64: hb * 64 + 64,
                                             g * 512 + delta: (g + 1) * 512],
                                       st=(not diag):
                                       t.matmul(b, lh, rh, start=st, stop=True),
                                       waits=w, inc=self.c_ct)
                we = [(self.c_ct, simt)]
                if len(unit) == 2:
                    sbkA = unit[0] % 4
                    for sbx in (sbkA, sbkA + 1):
                        if self.et_free[sbx] is not None:
                            we.append(self.et_free[sbx])
                    pt = self.ps["PA"] if sbkA < 2 else self.ps["PB"]
                    et = self.op("s", lambda ss, b=pt[:, 0:1024],
                                 d=sb["ET"][0:128, sbkA * 512:
                                            sbkA * 512 + 1024]:
                                 ss.activation(out=d, in_=b, func=AF.Exp,
                                               scale=1.0 / 8.0),
                                 waits=we, inc=self.c_cs)
                    self.bank_free[sbkA] = (self.c_cs, et)
                    self.bank_free[sbkA + 1] = (self.c_cs, et)
                else:
                    jc = unit[0]
                    sbk = jc % 4
                    N = 512 - max(0, (jc - ndg) * 128)
                    if self.et_free[sbk] is not None:
                        we.append(self.et_free[sbk])
                    et = self.op("s", lambda ss, b=self.bank(sbk)[:, 0:N],
                                 d=sb["ET"][0:128, sbk * 512: sbk * 512 + N]:
                                 ss.activation(out=d, in_=b, func=AF.Exp,
                                               scale=1.0 / 8.0),
                                 waits=we, inc=self.c_cs)
                    self.bank_free[sbk] = (self.c_cs, et)
                for jc in unit:
                    sbk = jc % 4
                    delta = max(0, (jc - ndg) * 128)
                    N = 512 - delta
                    wn = [(self.c_cs, et)] if jc == unit[0] else []
                    if jc == 0:
                        wn.append(self.pn_free[gg % 4])
                        wn.append(vbready)
                    nstop = self.op("t", lambda t, b=pn[0:65, delta:512],
                                    lh=VB_ap(sb, vb_base, jc, hb),
                                    rh=sb["ET"][0:128, sbk * 512:
                                                sbk * 512 + N],
                                    st=(jc == 0), sp=(jc == njc - 1):
                                    t.matmul(b, lh, rh, start=st, stop=sp),
                                    waits=wn, inc=self.c_ct)
                    self.et_free[sbk] = (self.c_ct, nstop)
                if feeder is not None:
                    next(feeder, None)
                if any(jc == 1 for jc in unit) and self._den_pending:
                    for fn in self._den_pending:
                        fn()
                    self._den_pending = []
            numt = self.c_ct.v
            self._den_pending.append(
                lambda gg=gg, pn=pn, numt=numt, h=h, g=g, ao_p0=ao_p0,
                ao_cb=ao_cb: self._a_den(gg, pn, numt, h, g, ao_p0, ao_cb))
            self._gg += 1
        self.pair_ct[p] = self.c_ct.v

    def _a_den(self, gg, pn, numt, h, g, ao_p0, ao_cb):
            sb = self.sb
            rcslot = gg % 2
            rc = sb["SMB"][0:1, rcslot * 512: rcslot * 512 + 512]
            ri = sb["SM"][0:1, 512 + rcslot * 512: 1024 + rcslot * 512]
            lnt = self.op("s", lambda ss, ri=ri, pn=pn:
                          ss.activation(out=ri, in_=pn[64:65, 0:512],
                                        func=AF.Ln),
                          waits=[(self.c_ct, numt), self.rc_free[rcslot]],
                          inc=self.c_cs)
            rt = self.op("s", lambda ss, ri=ri, rc=rc:
                         ss.activation(out=rc, in_=ri, func=AF.Exp,
                                       scale=-1.0),
                         waits=[(self.c_cs, lnt)], inc=self.c_cs)
            nbslot = gg % 2
            nb = sb["SMB"][64:128, nbslot * 512: nbslot * 512 + 512]
            nbt = self.op("s", lambda ss, nb=nb, pn=pn:
                          ss.copy(out=nb, in_=pn[0:64, 0:512]),
                          waits=[self.nb_free[nbslot]],
                          inc=self.c_cs)
            rpt = self.op("t", lambda t, pn=pn, rc=rc:
                          t.matmul(pn[64:128, 0:512], sb["ONE"][0:1, 0:64], rc,
                                   start=True, stop=True),
                          waits=[(self.c_cs, nbt)], inc=self.c_ct)
            self.rc_free[rcslot] = (self.c_ct, rpt)
            wao = [(self.c_ct, rpt)]
            if h >= 8 and self.wo_done.get(0) is not None:
                wao.append((self.c_ct, self.wo_done[0]))
            ev = self.op("v", lambda v, pn=pn, nb=nb,
                         d=sb["ARB"][ao_p0:ao_p0 + 64,
                                     ao_cb + g * 512: ao_cb + (g + 1) * 512]:
                         v.scalar_tensor_tensor(out=d, in0=pn[64:128, 0:512],
                                                scalar=1.0, in1=nb,
                                                op0=ALU.mult, op1=ALU.mult),
                         waits=wao, inc=self.c_cv)
            self.pn_free[gg % 4] = (self.c_cv, ev)
            self.nb_free[nbslot] = (self.c_cv, ev)

    def a_wo_half(self, half):
        sb = self.sb
        WOT = sb["ARA"]
        aoready = (self.c_cv, self.ao_ready[half])
        k = 0
        for o in range(ND):
            for tc in range(4):
                bidx = k % 4
                for f in range(4):
                    w = []
                    if f == 0:
                        w.append(self.bank_free[bidx])
                        if k == 0:
                            w.append(aoready)
                            w.append((self.c_cv, self.wot_ready))
                    self.op("t", lambda t, b=self.bank(bidx),
                            lh=WOT[:, 8192 + (half * 4 + f) * 1024 + o * 128:
                                   8192 + (half * 4 + f) * 1024 + (o + 1) * 128],
                            rh=sb["ARB"][:, f * 2048 + tc * 512: f * 2048 + tc * 512 + 512],
                            st=(f == 0), sp=(f == 3):
                            t.matmul(b, lh, rh, start=st, stop=sp),
                            waits=w, inc=self.c_ct)
                mmt = self.c_ct.v
                ev = self.op("v", lambda v, b=self.bank(bidx),
                             d=sb["X"][:, o * T + tc * 512: o * T + tc * 512 + 512]:
                             v.scalar_tensor_tensor(out=d, in0=b, scalar=1.0,
                                                    in1=d, op0=ALU.mult,
                                                    op1=ALU.add),
                             waits=[(self.c_ct, mmt)], inc=self.c_cv)
                self.bank_free[bidx] = (self.c_cv, ev)
                k += 1
        self.wo_done[half] = self.c_ct.v
        self.x1_ready = self.c_cv.v

    def a_ln1(self):
        sb = self.sb
        X = sb["X"]
        XBL = sb["ARB"][:, 0:4096]
        SQB = sb["ARB"][:, 4096:8192]
        wo1 = (self.c_ct, self.wo_done[1])
        xready = (self.c_cv, self.x1_ready)
        PX, PY = self.ps["PX"], self.ps["PY"]
        PN0, PN1 = self.ps["PN0"], self.ps["PN1"]
        apply_done = 0
        for tc in range(4):
            xr = X[:, :].rearrange("q (c t) -> q c t", c=ND)[:, :, tc * 512:(tc + 1) * 512]
            cst = self.op("s", lambda ss, xr=xr,
                          d=XBL.rearrange("q (c t) -> q c t", c=ND):
                          ss.copy(out=d, in_=xr),
                          waits=[xready, wo1,
                                 (self.c_cv, apply_done)], inc=self.c_cs)
            sqt = self.op("v", lambda v: v.tensor_mul(SQB, XBL, XBL),
                          waits=[(self.c_cs, cst)], inc=self.c_cv)
            for c in range(ND):
                self.op("t", lambda t, rh=XBL[:, c * 512:(c + 1) * 512],
                        st=(c == 0), sp=(c == ND - 1):
                        t.matmul(PX[0:1, 0:512], sb["ONE"][0:128, 0:1], rh,
                                 start=st, stop=sp),
                        waits=[(self.c_cs, cst), self.pn_free[2]] if c == 0 else [],
                        inc=self.c_ct)
            for c in range(ND):
                self.op("t", lambda t, rh=SQB[:, c * 512:(c + 1) * 512],
                        st=(c == 0), sp=(c == ND - 1):
                        t.matmul(PY[0:1, 0:512], sb["ONE"][0:128, 0:1], rh,
                                 start=st, stop=sp),
                        waits=[(self.c_cv, sqt), self.pn_free[3]] if c == 0 else [],
                        inc=self.c_ct)
            sums = self.c_ct.v
            sm = sb["SM"]
            murow = sm[0:1, 0:512]
            e2row = sm[0:1, 512:1024]
            varrow = sm[0:1, 1024:1536]
            rrowb = sb["SMB"][0:1, 0:512]
            srowb = sb["SMB"][0:1, 512:1024]
            mu1 = self.op("v", lambda v: v.tensor_scalar(
                out=murow, in0=PX[0:1, 0:512], scalar1=1.0 / D, scalar2=None,
                op0=ALU.mult), waits=[(self.c_ct, sums)], inc=self.c_cv)
            self.op("v", lambda v: v.tensor_scalar(
                out=e2row, in0=PY[0:1, 0:512], scalar1=1.0 / D, scalar2=None,
                op0=ALU.mult), inc=self.c_cv)
            m3 = self.op("v", lambda v: v.tensor_mul(varrow, murow, murow),
                         waits=[(self.c_cv, mu1)], inc=self.c_cv)
            vv = self.op("v", lambda v: v.tensor_tensor(
                out=varrow, in0=e2row, in1=varrow, op=ALU.subtract),
                waits=[(self.c_cv, m3)], inc=self.c_cv)
            st0 = self.op("s", lambda s2: s2.activation(
                out=sm[0:1, 1024:1536], in_=varrow, func=AF.Ln,
                bias=sb["COLS"][0:1, 4:5]), waits=[(self.c_cv, vv)],
                inc=self.c_cs)
            st1 = self.op("s", lambda s2: s2.activation(
                out=sm[0:1, 512:1024], in_=sm[0:1, 1024:1536], func=AF.Exp,
                scale=-0.5), waits=[(self.c_cs, st0)], inc=self.c_cs)
            vv2 = self.op("v", lambda v: v.tensor_copy(out=rrowb,
                                                       in_=sm[0:1, 512:1024]),
                          waits=[(self.c_cs, st1), self.rc_free[0],
                                 self.rc_free[1]])
            vv2 = self.op("v", lambda v: v.tensor_tensor(
                out=srowb, in0=murow, in1=sm[0:1, 512:1024], op=ALU.mult),
                waits=[(self.c_cv, mu1)], inc=self.c_cv)
            rep1 = self.op("t", lambda t: t.matmul(
                PN0[0:128, 0:512], sb["ONE"][0:1, 0:128], rrowb,
                start=True, stop=True),
                waits=[(self.c_cv, vv2), self.pn_free[0]], inc=self.c_ct)
            rep2 = self.op("t", lambda t: t.matmul(
                PN1[0:128, 0:512], sb["ONE"][0:1, 0:128], srowb,
                start=True, stop=True),
                waits=[self.pn_free[1]], inc=self.c_ct)
            mts = None
            for c in range(ND):
                mts = self.op("v", lambda v,
                              d=X[:, c * T + tc * 512: c * T + tc * 512 + 512]:
                              v.tensor_tensor(out=d, in0=d,
                                              in1=PN0[0:128, 0:512],
                                              op=ALU.mult),
                              waits=[(self.c_ct, rep2)] if c == 0 else [],
                              inc=self.c_cv if c == ND - 1 else None)
            ap = None
            for c in range(ND):
                ap = self.op("v", lambda v,
                             d=X[:, c * T + tc * 512: c * T + tc * 512 + 512]:
                             v.tensor_tensor(out=d, in0=d,
                                             in1=PN1[0:128, 0:512],
                                             op=ALU.subtract),
                             waits=[(self.c_cv, mts)] if c == 0 else [],
                             inc=self.c_cv)
            self.pn_free = [(self.c_cv, ap)] * 4
            apply_done = ap
        self.xa_ready = apply_done
        self.xa_spilled = self.dma("y", self.dr["xa_spill"][:, :], X[:, :],
                                   waits=[(self.c_cv, apply_done)],
                                   inc=self.c_spl)


    def emit_precast(self):
        W1ROW = 2 * HIDP
        for li in range(2):
            for hc in range(NHC):
                s1 = bass.AP(self.dr["w1i"].tensor, li * D * W1ROW + hc * 256,
                             [[W1ROW, 128], [128 * W1ROW, 8], [1, 256]])
                d1 = bass.AP(self.dr["w1s"].tensor,
                             (li * NHC + hc) * 128 * 2048,
                             [[2048, 128], [256, 8], [1, 256]])
                self.dma("g", d1, s1, inc=self.c_pc)
            for o in range(ND):
                s2 = bass.AP(self.dr["w2p"].tensor, li * HIDP * D + o * 128,
                             [[D, 128], [128 * D, NHC], [1, 128]])
                d2 = bass.AP(self.dr["w2s"].tensor, (li * ND + o) * 128 * 2816,
                             [[2816, 128], [128, NHC], [1, 128]])
                self.dma("g", d2, s2, inc=self.c_pc)
        self.precast_t = self.c_pc.v

    def st_partB(self, l):
        self.b_prep(l)
        self.b_qkv(l)
        wsrc = self.dr["wo_f"][l].rearrange("(k p) m -> p k m", p=128)
        self.wo_dma_t = self.dma(
            "g", self.sb["ARW"][:, 0:4096].rearrange("p (k m) -> p k m", k=4),
            wsrc, waits=[(self.c_ct, self.qkv_wslot_done)], inc=self.c_wod)
        if l == 0:
            self.emit_precast()
        self.attn_state_init()
        for h in range(HB + 1):
            feeder = self.b_pass1_iter(l, h) if h < HB else None
            if h == 0:
                for _ in feeder:
                    pass
                feeder = None
            if h >= 1:
                for g in range(4):
                    self.b_pass2_g(l, h - 1, g, feeder=feeder)
            if feeder is not None:
                for _ in feeder:
                    pass
            if h < HB:
                self.b_transport(l, h)
        for fn in self._den_pending:
            fn()
        self._den_pending = []
        self.ao2_ready = self.c_cv.v
        self.attn_done_ct = self.c_ct.v
        self.bank_free[0] = self.pa_free[0] or self.bank_free[0]
        self.bank_free[1] = self.pa_free[0] or self.bank_free[1]
        self.bank_free[2] = self.pa_free[1] or self.bank_free[2]
        self.bank_free[3] = self.pa_free[1] or self.bank_free[3]
        self.b_wo(l)
        self.b_ffn(l)

    def b_prep(self, l):
        sb, dr = self.sb, self.dr
        rot = sb["ARC"].bitcast(F32)
        if l == 0:
            wct = [(self.c_ct, self.proj_done_ct[7])]
        else:
            wct = [(self.c_ct, self.w2mm_done), (self.c_cv, self.ffn_done_cv),
                   (self.c_cs, self.c_cs.v)]
        for i, nm in enumerate(["rot_qc", "rot_qs", "rot_kc", "rot_ks"]):
            self.dma("y", rot[:, i * 1024:(i + 1) * 1024], dr[nm][:, :],
                     waits=wct if i == 0 else [], inc=self.c_prp)
        if l == 0:
            self.stk_ones = self.op("v", lambda v: v.memset(
                sb["STK"][64:65, :], 1.0), inc=self.c_cv)
        self.dma("y", sb["QKS"][:, 0:512], dr["qs_rep"][l], inc=self.c_prp)
        self.dma("y", sb["QKS"][:, 512:1024], dr["ks_rep"][l], inc=self.c_prp)
        self.dma("y", sb["COLS"][:, 8:16], dr["lng_c"][l], inc=self.c_prp)
        self.dma("y", sb["COLS"][:, 16:24], dr["lnb_c"][l], inc=self.c_prp)
        self.dma("y", sb["COLS"][:, 24:32], dr["b2_c"][l], inc=self.c_prp)
        self.dma("y", sb["COLS"][:, 32:54], dr["b1a_c"][l], inc=self.c_prp)
        self.prep_t = self.dma("y", sb["COLS"][:, 54:76], dr["b1g_c"][l],
                               inc=self.c_prp)

    def b_qkv(self, l):
        sb, dr = self.sb, self.dr
        X = sb["X"]
        rot = sb["ARC"].bitcast(F32)
        PRP = sb["PRP"]
        arbf = sb["ARB"].bitcast(F32)
        PSET = [
            dict(P0=PRP[:, 0:512], P1=PRP[:, 512:1024], P2=PRP[:, 1024:1536],
                 gt=sb["PRP"].tensor, gstride=1536, gbase=0),
            dict(P0=arbf[:, 4160:4672], P1=arbf[:, 4672:5184],
                 P2=arbf[:, 5184:5696], gt=arbf.tensor, gstride=8192,
                 gbase=4160),
        ]
        sm = sb["SM"]
        fill = sm[:, 127:128]
        if l == 0:
            wx = (self.c_cv, self.xa_ready)
            wfence = (self.c_ct, self.wo_done[1])
        else:
            wx = (self.c_cv, self.ffn_done_cv)
            wfence = (self.c_ct, self.w2mm_done)
        vb1 = bass.AP(sb["ARB"].tensor, 64, [[16384, 128], [520, 16], [65, 8]])
        self.vb2_ones = self.op("v", lambda v: v.memset(vb1, 1.0),
                                waits=[wx, wfence], inc=self.c_cv)
        hsj_free = [None, None]
        wslot_t = [None, None]
        stg_free = [None, None]
        prp_free = [[], []]
        sm_free = [None, None]

        for f in range(3):
            ws = f % 2
            wsl = sb["ARW"][:, ws * 4096:(ws + 1) * 4096]
            src = dr["wqkv"][l][:, f * 512:(f + 1) * 512].rearrange(
                "(k p) f -> p k f", p=128)
            wwait = list(prp_free) and []
            wwait = []
            if wslot_t[ws] is not None:
                wwait.append((self.c_ct, wslot_t[ws]))
            if f == 0:
                wwait.append(wfence)
            wdma = self.dma("g", wsl.rearrange("p (k f) -> p k f", k=8),
                            src, waits=wwait,
                            inc=(self.c_qkv0 if ws == 0 else self.c_qkv1))
            for n in range(NT):
                hs = n % 2
                hbase = hs * 1024
                PS = PSET[hs]
                P0, P1, P2 = PS["P0"], PS["P1"], PS["P2"]
                wcast = [wx]
                if hsj_free[hs] is not None:
                    wcast.append((self.c_ct, hsj_free[hs]))
                d_lo = bass.AP(sb["HSJ"].tensor, hbase,
                               [[2048, 128], [128, 4], [1, 128]])
                s_lo = bass.AP(X.tensor, n * 128,
                               [[ND * T, 128], [T, 4], [1, 128]])
                self.op("s", lambda ss, d=d_lo, s2=s_lo:
                        ss.copy(out=d, in_=s2), waits=wcast)
                wz = []
                if n == 0:
                    ed = bass.AP(sb["HSJ"].tensor, hbase + 512,
                                 [[2048, 128], [128, 4], [1, 2]])
                    mz = self.op("s", lambda ss, ed=ed: ss.memzero(ed),
                                 inc=self.c_cs)
                    wz = [(self.c_cs, mz)]
                    d_hi = bass.AP(sb["HSJ"].tensor, hbase + 513,
                                   [[2048, 128], [128, 4], [1, 127]])
                    s_hi = bass.AP(X.tensor, 4 * T,
                                   [[ND * T, 128], [T, 4], [1, 127]])
                else:
                    d_hi = bass.AP(sb["HSJ"].tensor, hbase + 512,
                                   [[2048, 128], [128, 4], [1, 128]])
                    s_hi = bass.AP(X.tensor, 4 * T + n * 128 - 1,
                                   [[ND * T, 128], [T, 4], [1, 128]])
                ct_cast = self.op("s", lambda ss, d=d_hi, s2=s_hi:
                                  ss.copy(out=d, in_=s2), waits=wz,
                                  inc=self.c_cs)
                bidx = n % 4
                for c in range(ND):
                    w = []
                    if c == 0:
                        w += [self.bank_free[bidx], (self.c_cs, ct_cast)]
                        if n == 0:
                            w.append(((self.c_qkv0 if ws == 0 else self.c_qkv1),
                                      wdma))
                    self.op("t", lambda t, b=self.bank(bidx),
                            lh=sb["HSJ"][:, hbase + c * 128: hbase + (c + 1) * 128],
                            rh=wsl[:, c * 512:(c + 1) * 512],
                            st=(c == 0), sp=(c == ND - 1):
                            t.matmul(b, lh, rh, start=st, stop=sp),
                            waits=w, inc=self.c_ct)
                mmt = self.c_ct.v
                hsj_free[hs] = mmt
                if f == 2:
                    dv = bass.AP(sb["ARB"].tensor, n * 520,
                                 [[16384, 128], [65, 8], [1, 64]])
                    ev = self.op("s", lambda ss, b=self.bank(bidx), dv=dv:
                                 ss.copy(out=dv, in_=b.rearrange(
                                     "p (h e) -> p h e", h=8)),
                                 waits=[(self.c_ct, mmt), wfence],
                                 inc=self.c_cs)
                    self.bank_free[bidx] = (self.c_cs, ev)
                    continue
                qoff = 0 if f == 0 else 512
                cosb = 0 if f == 0 else 2048
                sinb = 1024 if f == 0 else 3072
                smq = sm[:, 1024 + hs * 64: 1024 + hs * 64 + 8]
                smc = sm[:, 1040 + hs * 64: 1040 + hs * 64 + 8]
                sml = sm[:, 1056 + hs * 64: 1056 + hs * 64 + 8]
                smr = sm[:, 1072 + hs * 64: 1072 + hs * 64 + 8]
                wa1 = [(self.c_ct, mmt)] + prp_free[hs]
                a1 = self.op("s", lambda ss, b=self.bank(bidx), P0=P0:
                             ss.copy(out=P0, in_=b), waits=wa1, inc=self.c_cs)
                sq2 = self.op("v", lambda v, P0=P0, P1=P1:
                              v.tensor_mul(P1, P0, P0),
                              waits=[(self.c_cs, a1)], inc=self.c_cv)
                wsm = [(self.c_cv, sq2)]
                if sm_free[hs] is not None:
                    wsm.append((self.c_cs, sm_free[hs]))
                rs3 = self.op("v", lambda v, smq=smq, P1=P1: v.reduce_sum(
                    smq, P1.rearrange("p (h e) -> p h e", h=8), axis=AX.X),
                    waits=wsm, inc=self.c_cv)
                self.op("v", lambda v, qo=qoff, P0=P0, P2=P2: v.tensor_mul(
                    P2, P0, sb["QKS"][:, qo:qo + 512]),
                    waits=[(self.c_prp, self.prep_t)] if n == 0 else [])
                v4 = self.op("v", lambda v, smq=smq, smc=smc:
                             v.tensor_scalar_max(smc, smq, 1e-24),
                             waits=[(self.c_cv, rs3)], inc=self.c_cv)
                ln6 = self.op("s", lambda ss, smc=smc, sml=sml:
                              ss.activation(out=sml, in_=smc, func=AF.Ln),
                              waits=[(self.c_cv, v4)], inc=self.c_cs)
                s3 = self.op("s", lambda ss, sml=sml, smr=smr:
                             ss.activation(out=smr, in_=sml, func=AF.Exp,
                                           scale=-0.5),
                             waits=[(self.c_cs, ln6)], inc=self.c_cs)
                sm_free[hs] = s3
                hl8 = None
                for hh in range(HB):
                    hl8 = self.op("v", lambda v, hh=hh, smr=smr, P0=P0, P2=P2:
                                  v.tensor_scalar(
                        out=P0[:, hh * 64:(hh + 1) * 64],
                        in0=P2[:, hh * 64:(hh + 1) * 64],
                        scalar1=smr[:, hh:hh + 1], scalar2=None, op0=ALU.mult),
                        waits=[(self.c_cs, s3)] if hh == 0 else [],
                        inc=self.c_cv if hh == HB - 1 else None)
                gsrc = bass.AP(PS["gt"], PS["gbase"] + 32,
                               [[PS["gstride"], 128], [64, 8], [-32, 2], [1, 32]])
                gdst = bass.AP(PS["gt"], PS["gbase"] + 512,
                               [[PS["gstride"], 128], [64, 8], [32, 2], [1, 32]])
                sin4 = rot[:, sinb + n * 64: sinb + (n + 1) * 64].rearrange(
                    "p (two e) -> p two e", two=2).unsqueeze(
                    1).broadcast_to((128, 8, 2, 32))
                g1 = self.op("g", lambda g, gsrc=gsrc, gdst=gdst, s4=sin4:
                             g.tensor_tensor(out=gdst, in0=gsrc, in1=s4,
                                             op=ALU.mult),
                             waits=[(self.c_cv, hl8)], inc=self.c_cg)
                cos3 = rot[:, cosb + n * 64: cosb + (n + 1) * 64].unsqueeze(
                    1).broadcast_to((128, 8, 64))
                c15 = self.op("v", lambda v, c3=cos3, P0=P0, P2=P2:
                              v.tensor_tensor(
                    out=P2.rearrange("p (h e) -> p h e", h=8),
                    in0=P0.rearrange("p (h e) -> p h e", h=8),
                    in1=c3, op=ALU.mult),
                    waits=[(self.c_cv, hl8)], inc=self.c_cv)
                wst = [(self.c_cg, g1), (self.c_cv, c15)]
                if stg_free[hs] is not None:
                    wst.append((self.c_ct, stg_free[hs]))
                v14 = self.op("v", lambda v, d=sb["STG"][:, hs * 512:(hs + 1) * 512],
                              P1=P1, P2=P2:
                              v.tensor_add(d, P2, P1),
                              waits=wst, inc=self.c_cv)
                prp_free[hs] = [(self.c_cv, v14), (self.c_cg, g1)]
                dbase = 0 if f == 0 else 8192
                ltt = None
                for tp in range(4):
                    b2 = (n + tp) % 4
                    w = [self.bank_free[b2]]
                    if tp == 0:
                        w.append((self.c_cv, v14))
                    ltt = self.op("t", lambda t, b=self.bank(b2)[:, 0:128],
                                  lh=sb["STG"][:, hs * 512 + tp * 128: hs * 512 + (tp + 1) * 128]:
                                  t.matmul(b, lh, sb["IDB"][:, :],
                                           start=True, stop=True),
                                  waits=w, inc=self.c_ct)
                    wev = [(self.c_ct, ltt)]
                    if f == 0 and n == 0 and tp == 0:
                        wev.append(wfence)
                    ev = self.op("s", lambda ss, b=self.bank(b2)[:, 0:128],
                                 d=sb["ARA"][:, dbase + tp * 2048 + n * 128:
                                             dbase + tp * 2048 + (n + 1) * 128]:
                                 ss.copy(out=d, in_=b),
                                 waits=wev, inc=self.c_cs)
                    self.bank_free[b2] = (self.c_cs, ev)
                stg_free[hs] = ltt
            wslot_t[ws] = self.c_ct.v
        self.qkv_wslot_done = self.c_ct.v
        self.qt2_ready = self.c_cs.v
        self.stgk_init = self.c_ct.v
        self.qkv_prep_cg = self.c_cg.v
        self.qkv_prep_cv = self.c_cv.v
        self.hln_last = None

    def attn_state_init(self):
        self._gg2 = 0
        self.pa_pidx = 0
        self.pa_free = [None, None]
        self.mslot_free = [None, None]
        self.stq_free = [None, None]
        self.stk_free = [None, None]
        self.sbk_free = [None, None]
        self.hop1_t = {}
        self.mred_t = {}
        self.pn_free = [self.pn_free[0], self.pn_free[1],
                        self.bank_free[2], self.bank_free[3]]

    def b_pass1_iter(self, l, h):
        sb = self.sb
        j, hb = divmod(h, 2)
        QTp = sb["ARA"][:, j * 2048:(j + 1) * 2048]
        KTp = sb["ARA"][:, 8192 + j * 2048: 8192 + (j + 1) * 2048]
        hs2 = h % 2
        mA = sb["SM"][:, hs2 * 32: hs2 * 32 + 16]
        mC = sb["SM"][:, 128 + hs2 * 32: 128 + hs2 * 32 + 16]
        for c in range(NT):
            jlo = max(0, (c - 4) * 128)
            jhi = c * 128 + 128
            w = jhi - jlo
            par = self.pa_pidx % 2
            reg = self.ps["PA"] if par == 0 else self.ps["PB"]
            wts = []
            if self.pa_free[par] is not None:
                wts.append(self.pa_free[par])
            else:
                wts += [self.bank_free[2 * par], self.bank_free[2 * par + 1]]
            if c == 0:
                wts.append((self.c_cs, self.qt2_ready))
            lt = self.op("t", lambda t, b=reg[:, 0:min(w, 512)],
                         lh=QTp[hb * 64: hb * 64 + 64, c * 128:(c + 1) * 128],
                         rh=KTp[hb * 64: hb * 64 + 64, jlo:jlo + min(w, 512)]:
                         t.matmul(b, lh, rh, start=True, stop=True),
                         waits=wts, inc=self.c_ct)
            if w > 512:
                lt = self.op("t", lambda t, b=reg[:, 512:w],
                             lh=QTp[hb * 64: hb * 64 + 64, c * 128:(c + 1) * 128],
                             rh=KTp[hb * 64: hb * 64 + 64, jlo + 512:jhi]:
                             t.matmul(b, lh, rh, start=True, stop=True),
                             inc=self.c_ct)
            wrd = [(self.c_ct, lt)]
            if c == 0 and self.mslot_free[hs2] is not None:
                wrd.append((self.c_ct, self.mslot_free[hs2]))
            rd = self.op("v", lambda v, b=reg[:, 0:w], m=mA[:, c:c + 1]:
                         v.reduce_max(m, b, axis=AX.X, negate=True),
                         waits=wrd, inc=self.c_cv)
            self.pa_free[par] = (self.c_cv, rd)
            self.pa_pidx += 1
            yield
        wcmb = [(self.c_cv, rd)]
        if self.mslot_free[hs2] is not None:
            wcmb.append((self.c_ct, self.mslot_free[hs2]))
        cmb = self.op("v", lambda v, mA=mA, mC=mC:
                      v.tensor_copy(out=mC, in_=mA),
                      waits=wcmb, inc=self.c_cv)
        self.mred_t[h] = cmb

    def b_transport(self, l, h):
        sb, dr = self.sb, self.dr
        hs2 = h % 2
        mC = sb["SM"][:, 128 + hs2 * 32: 128 + hs2 * 32 + 16]
        PA = self.ps["PA"]
        tp = self.op("t", lambda t, mC=mC: t.matmul(
            PA[0:16, 0:128], mC, sb["IDF"][:, :], start=True, stop=True),
            waits=[(self.c_cv, self.mred_t[h]), self.pa_free[0]],
            inc=self.c_ct)
        self.mslot_free[hs2] = tp
        m16 = sb["SMB"][0:16, 1024 + hs2 * 128: 1024 + hs2 * 128 + 128]
        wev = [(self.c_ct, tp)]
        if h >= 2 and (h - 2) in self.hop1_t:
            wev.append(((self.c_m10 if hs2 == 0 else self.c_m11),
                        self.hop1_t[h - 2]))
        ev = self.op("v", lambda v, m16=m16: v.tensor_copy(
            out=m16, in_=PA[0:16, 0:128]),
            waits=wev, inc=self.c_cv)
        self.pa_free[0] = (self.c_cv, ev)
        self.hop1_t[h] = self.dma("s", dr["mscr"][l * HB + h], m16,
                                  waits=[(self.c_cv, ev)],
                                  inc=(self.c_m10 if hs2 == 0 else self.c_m11))

    def b_pass2_g(self, l, h, g, feeder=None):
        sb, dr = self.sb, self.dr
        j, hb = divmod(h, 2)
        QTp = sb["ARA"][:, j * 2048:(j + 1) * 2048]
        KTp = sb["ARA"][:, 8192 + j * 2048: 8192 + (j + 1) * 2048]
        mrow = dr["mscr"][l * HB + h].rearrange("p f -> (p f)")
        ao_p0 = (h % 2) * 64
        ao_cb = 8192 + (h // 2) * 2048
        pnb = [self.ps["PN0"], self.ps["PN1"]]
        if True:
            gg2 = self._gg2
            gs = g % 2
            pn = pnb[gg2 % 2]
            wq = [(self.c_ct, self.stgk_init)]
            if self.stq_free[gs] is not None:
                wq = [(self.c_ct, self.stq_free[gs])]
            sA = self.op("s", lambda ss, d=sb["STG"][0:64, gs * 512:(gs + 1) * 512],
                         s2=QTp[hb * 64: hb * 64 + 64, g * 512:(g + 1) * 512]:
                         ss.copy(out=d, in_=s2), waits=wq, inc=self.c_cs)
            hp2 = self.dma("s", sb["STG"][64:65, gs * 512:(gs + 1) * 512],
                           mrow[g * 512:(g + 1) * 512].unsqueeze(0),
                           waits=[((self.c_m10 if h % 2 == 0 else self.c_m11),
                                   self.hop1_t[h])] + wq,
                           inc=(self.c_m20 if gs == 0 else self.c_m21))
            jc0 = max(0, 4 * g - 4)
            cc = 0
            first = True
            nstop = None
            jc_order = [4 * g] + [j_ for j_ in range(jc0, 4 * g + 4)
                                  if j_ != 4 * g]
            for jc in jc_order:
                if jc < 4 * g:
                    a = jc - (4 * g - 4)
                    N = 128 * (a + 1)
                    ocol = 0
                    roff = 0
                    mcol = 128 * a
                    moff = 128
                else:
                    b_ = jc - 4 * g
                    delta = 128 * b_
                    N = 512 - delta
                    ocol = delta
                    roff = delta
                    mcol = 0
                    moff = 0
                ks = cc % 2
                wk = [(self.c_ct, self.stgk_init),
                      (self.c_cv, self.stk_ones)]
                if self.stk_free[ks] is not None:
                    wk = [(self.c_ct, self.stk_free[ks])]
                kA = self.op("s", lambda ss, d=sb["STK"][0:64, ks * 128:(ks + 1) * 128],
                             s2=KTp[hb * 64: hb * 64 + 64, jc * 128:(jc + 1) * 128]:
                             ss.copy(out=d, in_=s2), waits=wk, inc=self.c_cs)
                sbi = cc % 2
                sbank = self.ps["PX"] if sbi == 0 else self.ps["PY"]
                wsim = [(self.c_cs, kA)]
                if self.sbk_free[sbi] is not None:
                    wsim.append(self.sbk_free[sbi])
                if cc == 0:
                    wsim += [(self.c_cs, sA),
                             ((self.c_m20 if gs == 0 else self.c_m21), hp2)]
                if N > 128:
                    rlo, rhi = (128, N) if mcol == 0 else (0, mcol)
                    self.op("t", lambda t, b=sbank[0:128, rlo:rhi],
                            lh=sb["STK"][0:65, ks * 128:(ks + 1) * 128],
                            rh=sb["STG"][0:65, gs * 512 + roff + rlo:
                                         gs * 512 + roff + rhi]:
                            t.matmul(b, lh, rh, start=True, stop=False),
                            waits=wsim, inc=self.c_ct)
                    self.op("t", lambda t, b=sbank[:, mcol:mcol + 128],
                            moff=moff:
                            t.matmul(b, sb["IDB"][:, :],
                                     sb["MTB"][:, moff:moff + 128],
                                     start=False, stop=False),
                            inc=self.c_ct)
                    st_ = self.op("t", lambda t, b=sbank[0:128, mcol:mcol + 128],
                                  lh=sb["STK"][0:65, ks * 128:(ks + 1) * 128],
                                  rh=sb["STG"][0:65, gs * 512 + roff + mcol:
                                               gs * 512 + roff + mcol + 128]:
                                  t.matmul(b, lh, rh, start=False, stop=True),
                                  inc=self.c_ct)
                else:
                    self.op("t", lambda t, b=sbank[:, mcol:mcol + 128],
                            moff=moff:
                            t.matmul(b, sb["IDB"][:, :],
                                     sb["MTB"][:, moff:moff + 128],
                                     start=True, stop=False),
                            waits=wsim, inc=self.c_ct)
                    st_ = self.op("t", lambda t, b=sbank[0:128, 0:N],
                                  lh=sb["STK"][0:65, ks * 128:(ks + 1) * 128],
                                  rh=sb["STG"][0:65, gs * 512 + roff:
                                               gs * 512 + roff + N]:
                                  t.matmul(b, lh, rh, start=False, stop=True),
                                  inc=self.c_ct)
                self.stk_free[ks] = st_
                we = [(self.c_ct, st_)]
                es = cc % 4
                if self.et_free[es] is not None:
                    we.append(self.et_free[es])
                et = self.op("s", lambda ss, b=sbank[0:128, 0:N],
                             d=sb["ET"][0:128, es * 512: es * 512 + N]:
                             ss.activation(out=d, in_=b, func=AF.Exp),
                             waits=we, inc=self.c_cs)
                self.sbk_free[sbi] = (self.c_cs, et)
                wn = [(self.c_cs, et)]
                if first:
                    wn.append(self.pn_free[gg2 % 2])
                    wn.append((self.c_cv, self.vb2_ones))
                    if l == 0 and h == 0 and g == 0:
                        wn.append((self.c_ct, self.proj_done_ct[7]))
                nstop = self.op("t", lambda t, b=pn[0:65, ocol:ocol + N],
                                lh=sb["ARB"][0:128, jc * 520 + h * 65:
                                             jc * 520 + h * 65 + 65],
                                rh=sb["ET"][0:128, es * 512: es * 512 + N],
                                st=first, sp=(jc == 4 * g + 3):
                                t.matmul(b, lh, rh, start=st, stop=sp),
                                waits=wn, inc=self.c_ct)
                self.et_free[es] = (self.c_ct, nstop)
                if feeder is not None:
                    next(feeder, None)
                if cc == 1 and self._den_pending:
                    for fn in self._den_pending:
                        fn()
                    self._den_pending = []
                first = False
                cc += 1
            self.stq_free[gs] = nstop
            self._den_pending.append(
                lambda gg2=gg2, pn=pn, nstop=nstop, l=l, h=h, g=g,
                ao_p0=ao_p0, ao_cb=ao_cb:
                self._b_den(gg2, pn, nstop, l, h, g, ao_p0, ao_cb))
            self._gg2 += 1

    def _b_den(self, gg2, pn, nstop, l, h, g, ao_p0, ao_cb):
            sb = self.sb
            rs = gg2 % 2
            rc = sb["SMB"][0:1, rs * 512: rs * 512 + 512]
            ri = sb["SM"][0:1, 512 + rs * 512: 1024 + rs * 512]
            lnt = self.op("s", lambda ss, ri=ri, pn=pn:
                          ss.activation(out=ri, in_=pn[64:65, 0:512],
                                        func=AF.Ln),
                          waits=[(self.c_ct, nstop), self.rc_free[rs]],
                          inc=self.c_cs)
            rt = self.op("s", lambda ss, ri=ri, rc=rc:
                         ss.activation(out=rc, in_=ri, func=AF.Exp,
                                       scale=-1.0),
                         waits=[(self.c_cs, lnt)], inc=self.c_cs)
            nb = sb["SMB"][64:128, rs * 512: rs * 512 + 512]
            nbt = self.op("s", lambda ss, nb=nb, pn=pn:
                          ss.copy(out=nb, in_=pn[0:64, 0:512]),
                          waits=[self.nb_free[rs]],
                          inc=self.c_cs)
            rpt = self.op("t", lambda t, pn=pn, rc=rc:
                          t.matmul(pn[64:128, 0:512], sb["ONE"][0:1, 0:64], rc,
                                   start=True, stop=True),
                          waits=[(self.c_cs, nbt)], inc=self.c_ct)
            self.rc_free[rs] = (self.c_ct, rpt)
            wao = [(self.c_ct, rpt)]
            if l == 1:
                wao.append((self.c_ct, self.wo_f_done))
            ev = self.op("v", lambda v, pn=pn, nb=nb,
                         d=sb["ARC"][ao_p0:ao_p0 + 64,
                                     ao_cb + g * 512: ao_cb + (g + 1) * 512]:
                         v.scalar_tensor_tensor(out=d, in0=pn[64:128, 0:512],
                                                scalar=1.0, in1=nb,
                                                op0=ALU.mult, op1=ALU.mult),
                         waits=wao, inc=self.c_cv)
            self.pn_free[gg2 % 2] = (self.c_cv, ev)
            self.nb_free[rs] = (self.c_cv, ev)

    def b_wo(self, l):
        sb, dr = self.sb, self.dr
        wdma = self.wo_dma_t
        k = 0
        for o in range(ND):
            for tc in range(4):
                bidx = k % 4
                for kk in range(4):
                    w = []
                    if kk == 0:
                        w.append(self.bank_free[bidx])
                        if k == 0:
                            w += [(self.c_cv, self.ao2_ready),
                                  (self.c_wod, wdma)]
                    self.op("t", lambda t, b=self.bank(bidx),
                            lh=sb["ARW"][:, kk * 1024 + o * 128: kk * 1024 + (o + 1) * 128],
                            rh=sb["ARC"][0:128, 8192 + kk * 2048 + tc * 512:
                                         8192 + kk * 2048 + tc * 512 + 512],
                            st=(kk == 0), sp=(kk == 3):
                            t.matmul(b, lh, rh, start=st, stop=sp),
                            waits=w, inc=self.c_ct)
                mmt = self.c_ct.v
                wev = [(self.c_ct, mmt)]
                if k == 0 and l == 0:
                    wev.append((self.c_spl, self.xa_spilled))
                ev = self.op("v", lambda v, b=self.bank(bidx),
                             d=sb["X"][:, o * T + tc * 512: o * T + tc * 512 + 512]:
                             v.scalar_tensor_tensor(out=d, in0=b, scalar=1.0,
                                                    in1=d, op0=ALU.mult,
                                                    op1=ALU.add),
                             waits=wev, inc=self.c_cv)
                self.bank_free[bidx] = (self.c_cv, ev)
                k += 1
        self.wo_f_done = self.c_ct.v
        self.x2_ready = self.c_cv.v

    def b_ffn(self, l):
        sb, dr = self.sb, self.dr
        X = sb["X"]
        XB2 = sb["ARB"][:, 0:4096]
        SQB = sb["ARB"][:, 4096:8192]
        hl_base = 8320
        H2C = sb["ARA"]
        PX, PY = self.ps["PX"], self.ps["PY"]
        PN0, PN1 = self.ps["PN0"], self.ps["PN1"]
        sm = sb["SM"]
        fill = sm[:, 127:128]
        w2mm_prev = None
        w1slot_t = [None, None]
        w2slot_t = [None, None]
        stg2_free = [None, None]
        for tq in range(4):
            wc = [(self.c_cv, self.x2_ready)]
            if tq == 0:
                wc += [(self.c_ct, self.attn_done_ct)]
            else:
                wc += [(self.c_ct, self.stats_mm_t)]
            d_lo = bass.AP(sb["ARB"].tensor, 0, [[16384, 128], [512, 4], [1, 512]])
            s_lo = bass.AP(X.tensor, tq * 512, [[ND * T, 128], [T, 4], [1, 512]])
            self.op("s", lambda ss, d=d_lo, s2=s_lo: ss.copy(out=d, in_=s2),
                    waits=wc)
            wz2 = []
            if tq == 0:
                ed = bass.AP(sb["ARB"].tensor, 2048, [[16384, 128], [512, 4], [1, 2]])
                mz2 = self.op("s", lambda ss, ed=ed: ss.memzero(ed),
                              inc=self.c_cs)
                wz2 = [(self.c_cs, mz2)]
            else:
                ed = bass.AP(sb["ARB"].tensor, 2048, [[16384, 128], [512, 4], [1, 1]])
                sv = bass.AP(sb["COLS"].tensor, 80, [[88, 128], [1, 4], [1, 1]])
                self.op("s", lambda ss, ed=ed, sv=sv: ss.copy(out=ed, in_=sv),
                        waits=[(self.c_cv, self.sv_t)])
            d_hi = bass.AP(sb["ARB"].tensor, 2049, [[16384, 128], [512, 4], [1, 511]])
            s_hi = bass.AP(X.tensor, 4 * T + tq * 512, [[ND * T, 128], [T, 4], [1, 511]])
            cst = self.op("s", lambda ss, d=d_hi, s2=s_hi: ss.copy(out=d, in_=s2),
                          waits=wz2, inc=self.c_cs)
            sqt = self.op("v", lambda v: v.tensor_mul(SQB, XB2, XB2),
                          waits=[(self.c_cs, cst)], inc=self.c_cv)
            for c in range(ND):
                self.op("t", lambda t, rh=XB2[:, c * 512:(c + 1) * 512],
                        st=(c == 0), sp=(c == ND - 1):
                        t.matmul(PX[0:1, 0:512], sb["ONE"][0:128, 0:1], rh,
                                 start=st, stop=sp),
                        waits=[(self.c_cs, cst), self.sbk_free[0] or (self.c_ct, 0)]
                        if c == 0 else [], inc=self.c_ct)
            for c in range(ND):
                self.op("t", lambda t, rh=SQB[:, c * 512:(c + 1) * 512],
                        st=(c == 0), sp=(c == ND - 1):
                        t.matmul(PY[0:1, 0:512], sb["ONE"][0:128, 0:1], rh,
                                 start=st, stop=sp),
                        waits=[(self.c_cv, sqt), self.sbk_free[1] or (self.c_ct, 0)]
                        if c == 0 else [], inc=self.c_ct)
            self.sbk_free = [None, None]
            sums = self.c_ct.v
            self.stats_mm_t = sums
            murow = sm[0:1, 0:512]
            e2row = sm[0:1, 512:1024]
            varrow = sm[0:1, 1024:1536]
            rrowb = sb["SMB"][0:1, 0:512]
            srowb = sb["SMB"][0:1, 512:1024]
            mu1 = self.op("v", lambda v: v.tensor_scalar(
                out=murow, in0=PX[0:1, 0:512], scalar1=1.0 / D, scalar2=None,
                op0=ALU.mult), waits=[(self.c_ct, sums)], inc=self.c_cv)
            self.op("v", lambda v: v.tensor_scalar(
                out=e2row, in0=PY[0:1, 0:512], scalar1=1.0 / D, scalar2=None,
                op0=ALU.mult), inc=self.c_cv)
            m3 = self.op("v", lambda v: v.tensor_mul(varrow, murow, murow),
                         waits=[(self.c_cv, mu1)], inc=self.c_cv)
            vv = self.op("v", lambda v: v.tensor_tensor(
                out=varrow, in0=e2row, in1=varrow, op=ALU.subtract),
                waits=[(self.c_cv, m3)], inc=self.c_cv)
            self.sbk_free = [(self.c_cv, vv), (self.c_cv, vv)]
            st0 = self.op("s", lambda s2: s2.activation(
                out=sm[0:1, 1024:1536], in_=varrow, func=AF.Ln,
                bias=sb["COLS"][0:1, 4:5]), waits=[(self.c_cv, vv)],
                inc=self.c_cs)
            st1 = self.op("s", lambda s2: s2.activation(
                out=sm[0:1, 512:1024], in_=sm[0:1, 1024:1536], func=AF.Exp,
                scale=-0.5), waits=[(self.c_cs, st0)], inc=self.c_cs)
            vv2 = self.op("v", lambda v: v.tensor_copy(out=rrowb,
                                                       in_=sm[0:1, 512:1024]),
                          waits=[(self.c_cs, st1)])
            vv2 = self.op("v", lambda v: v.tensor_tensor(
                out=srowb, in0=murow, in1=sm[0:1, 512:1024], op=ALU.mult),
                waits=[(self.c_cv, mu1)], inc=self.c_cv)
            rep1 = self.op("t", lambda t: t.matmul(
                PN0[0:128, 0:512], sb["ONE"][0:1, 0:128], rrowb,
                start=True, stop=True),
                waits=[(self.c_cv, vv2), self.pn_free[0]], inc=self.c_ct)
            rep2 = self.op("t", lambda t: t.matmul(
                PN1[0:128, 0:512], sb["ONE"][0:1, 0:128], srowb,
                start=True, stop=True),
                waits=[self.pn_free[1]], inc=self.c_ct)
            hln_t = None
            for p_ in range(4):
                stg_t = None
                for c in (2 * p_, 2 * p_ + 1):
                    Pc = sb["PRP"][:, (c % 2) * 512:(c % 2) * 512 + 512]
                    shift = 1 if c >= 4 else 0
                    w = [(self.c_ct, rep2)] if c == 0 else []
                    if c == 0 and tq == 0:
                        w.append((self.c_prp, self.prep_t))
                    if c >= 2 and c % 2 == 0:
                        w.append((self.c_cv, hln_t))
                    if c == 0 and tq == 0:
                        w.append((self.c_cg, self.qkv_prep_cg))
                        if self.hln_last is not None:
                            w.append((self.c_cv, self.hln_last))
                    elif c == 0 and self.hln_last is not None:
                        w.append((self.c_cv, self.hln_last))
                    if shift:
                        if tq == 0:
                            self.op("v", lambda v, Pc=Pc:
                                    v.memset(Pc[:, 0:1], 0.0), waits=w)
                        else:
                            self.op("v", lambda v, Pc=Pc,
                                    sv=sb["COLS"][:, 76 + c:77 + c]:
                                    v.tensor_tensor(out=Pc[:, 0:1], in0=sv,
                                                    in1=PN0[0:128, 0:1],
                                                    op=ALU.mult),
                                    waits=w)
                        stg_t = self.op("v", lambda v, Pc=Pc,
                                        xs=X[:, c * T + tq * 512:
                                             c * T + tq * 512 + 511]:
                                        v.tensor_tensor(out=Pc[:, 1:512],
                                                        in0=xs,
                                                        in1=PN0[0:128, 1:512],
                                                        op=ALU.mult),
                                        inc=self.c_cv)
                    else:
                        stg_t = self.op("v", lambda v, Pc=Pc,
                                        xs=X[:, c * T + tq * 512:
                                             c * T + tq * 512 + 512]:
                                        v.tensor_tensor(out=Pc, in0=xs,
                                                        in1=PN0[0:128, 0:512],
                                                        op=ALU.mult),
                                        waits=w, inc=self.c_cv)
                sub_t = None
                for i_, c in enumerate((2 * p_, 2 * p_ + 1)):
                    Pc = sb["PRP"][:, (c % 2) * 512:(c % 2) * 512 + 512]
                    sub_t = self.op("v", lambda v, Pc=Pc: v.tensor_tensor(
                        out=Pc, in0=Pc, in1=PN1[0:128, 0:512],
                        op=ALU.subtract),
                        waits=[(self.c_cv, stg_t)] if i_ == 0 else [],
                        inc=self.c_cv)
                for i_, c in enumerate((2 * p_, 2 * p_ + 1)):
                    Pc = sb["PRP"][:, (c % 2) * 512:(c % 2) * 512 + 512]
                    hln_t = self.op("v", lambda v, Pc=Pc,
                                    d=sb["ARB"][:, hl_base + c * 512:
                                                hl_base + (c + 1) * 512],
                                    sc=sb["COLS"][:, 8 + c:9 + c],
                                    bi=sb["COLS"][:, 16 + c:17 + c]:
                                    v.tensor_scalar(out=d, in0=Pc, scalar1=sc,
                                                    scalar2=bi, op0=ALU.mult,
                                                    op1=ALU.add),
                                    waits=[(self.c_cv, sub_t)] if i_ == 0
                                    else [], inc=self.c_cv)
            hln_done = hln_t
            self.hln_last = hln_done
            self.pn_free[0] = (self.c_cv, hln_done)
            self.pn_free[1] = (self.c_cv, hln_done)
            h2_t = None
            for hc in range(NHC):
                ws = hc % 2
                wsl = sb["ARW"][:, 4096 + ws * 2048: 4096 + (ws + 1) * 2048]
                wwait = []
                if w1slot_t[ws] is not None:
                    wwait.append((self.c_ct, w1slot_t[ws]))
                elif tq == 0 and hc < 2:
                    wwait.append((self.c_ct, self.wo_f_done))
                if tq == 0 and hc < 2:
                    wwait.append((self.c_pc, self.precast_t))
                wdma = self.dma("y", wsl, dr["w1s"][l][hc], waits=wwait,
                                inc=(self.c_w10 if ws == 0 else self.c_w11))
                ba = self.bank(2 * ws)
                bg = self.bank(2 * ws + 1)
                for kk in range(ND):
                    w = []
                    if kk == 0:
                        w += [self.bank_free[2 * ws], (self.c_cv, hln_done),
                              ((self.c_w10 if ws == 0 else self.c_w11), wdma)]
                        if tq > 0 and hc == 0:
                            w.append((self.c_ct, w2mm_prev))
                    self.op("t", lambda t, b=ba,
                            lh=wsl[:, kk * 256: kk * 256 + 128],
                            rh=sb["ARB"][:, hl_base + kk * 512: hl_base + (kk + 1) * 512],
                            st=(kk == 0), sp=(kk == ND - 1):
                            t.matmul(b, lh, rh, start=st, stop=sp),
                            waits=w, inc=self.c_ct)
                for kk in range(ND):
                    w = [self.bank_free[2 * ws + 1]] if kk == 0 else []
                    self.op("t", lambda t, b=bg,
                            lh=wsl[:, kk * 256 + 128: kk * 256 + 256],
                            rh=sb["ARB"][:, hl_base + kk * 512: hl_base + (kk + 1) * 512],
                            st=(kk == 0), sp=(kk == ND - 1):
                            t.matmul(b, lh, rh, start=st, stop=sp),
                            waits=w, inc=self.c_ct)
                w1mm = self.c_ct.v
                w1slot_t[ws] = w1mm
                wgl = [(self.c_ct, w1mm)]
                if tq == 0 and hc == 0:
                    wgl.append((self.c_prp, self.prep_t))
                if stg2_free[ws] is not None:
                    wgl.append((self.c_cv, stg2_free[ws]))
                gl = self.op("s", lambda ss, b=bg,
                             d=sb["STG"][:, ws * 512:(ws + 1) * 512],
                             bi=sb["COLS"][:, 54 + hc:55 + hc]:
                             ss.activation(out=d, in_=b, func=AF.Gelu,
                                           bias=bi),
                             waits=wgl, inc=self.c_cs)
                self.bank_free[2 * ws + 1] = (self.c_cs, gl)
                h2w = [(self.c_cs, gl)]
                if tq == 0 and hc < 2:
                    h2w.append((self.c_ct, self.attn_done_ct))
                if tq > 0 and hc == 0:
                    h2w.append((self.c_ct, w2mm_prev))
                h2_t = self.op("v", lambda v, b=ba, hc=hc:
                               v.scalar_tensor_tensor(
                                   out=H2C[:, hc * 512:(hc + 1) * 512],
                                   in0=b, scalar=sb["COLS"][:, 32 + hc:33 + hc],
                                   in1=sb["STG"][:, (hc % 2) * 512:((hc % 2) + 1) * 512],
                                   op0=ALU.add, op1=ALU.mult),
                               waits=h2w, inc=self.c_cv)
                self.bank_free[2 * ws] = (self.c_cv, h2_t)
                stg2_free[ws] = h2_t
            h2_done = h2_t
            svd = bass.AP(sb["COLS"].tensor, 80, [[88, 128], [1, 4], [1, 1]])
            svs = bass.AP(X.tensor, 4 * T + tq * 512 + 511,
                          [[ND * T, 128], [T, 4], [1, 1]])
            self.sv_t = self.op("v", lambda v, svd=svd, svs=svs:
                                v.tensor_copy(out=svd, in_=svs),
                                inc=self.c_cv)
            for o in range(ND):
                ws2 = o % 2
                wwait = []
                if w2slot_t[ws2] is not None:
                    wwait.append((self.c_ct, w2slot_t[ws2]))
                elif tq == 0 and o < 2:
                    wwait.append((self.c_ct, self.wo_f_done))
                    wwait.append((self.c_cg, self.qkv_prep_cg))
                    wwait.append((self.c_cv, self.qkv_prep_cv))
                if tq == 0 and o < 2:
                    wwait.append((self.c_pc, self.precast_t))
                wdma = self.dma("y", sb["ARC"][:, ws2 * 2816:(ws2 + 1) * 2816],
                                dr["w2s"][l][o], waits=wwait,
                                inc=(self.c_w20 if ws2 == 0 else self.c_w21))
                bidx = o % 4
                for kk in range(NHC):
                    w = []
                    if kk == 0:
                        w += [self.bank_free[bidx], (self.c_cv, h2_done),
                              ((self.c_w20 if ws2 == 0 else self.c_w21), wdma)]
                    self.op("t", lambda t, b=self.bank(bidx),
                            lh=sb["ARC"][:, ws2 * 2816 + kk * 128:
                                         ws2 * 2816 + (kk + 1) * 128],
                            rh=H2C[:, kk * 512:(kk + 1) * 512],
                            st=(kk == 0), sp=(kk == NHC - 1):
                            t.matmul(b, lh, rh, start=st, stop=sp),
                            waits=w, inc=self.c_ct)
                w2mm = self.c_ct.v
                w2slot_t[ws2] = w2mm
                ev = self.op("v", lambda v, b=self.bank(bidx),
                             d=X[:, o * T + tq * 512: o * T + tq * 512 + 512],
                             sc=sb["COLS"][:, 24 + o:25 + o]:
                             v.scalar_tensor_tensor(
                                 out=d, in0=b, scalar=sc,
                                 in1=d, op0=ALU.add, op1=ALU.add),
                             waits=[(self.c_ct, w2mm)], inc=self.c_cv)
                self.bank_free[bidx] = (self.c_cv, ev)
            w2mm_prev = self.c_ct.v
        self.w2mm_done = self.c_ct.v
        self.ffn_done_ct = self.c_ct.v
        self.ffn_done_cv = self.c_cv.v

    def st_final(self):
        sb, dr = self.sb, self.dr
        XA_lo = sb["ARA"].bitcast(F32)
        XA_hi = sb["ARB"].bitcast(F32)
        d1 = self.dma("y", XA_lo[:, 0:8192], dr["xa_spill"][:, 0:8192],
                      waits=[(self.c_ct, self.ffn_done_ct),
                             (self.c_cv, self.ffn_done_cv),
                             (self.c_cs, self.c_cs.v)], inc=self.c_rel)
        d2 = self.dma("y", XA_hi[:, 0:8192], dr["xa_spill"][:, 8192:16384],
                      inc=self.c_rel)
        last = None
        for i in range(16):
            src = (XA_lo if i < 8 else XA_hi)[:, (i % 8) * 1024:(i % 8) * 1024 + 1024]
            w = []
            if i == 0:
                w = [(self.c_rel, d2), (self.c_cv, self.ffn_done_cv)]
            last = self.op("v", lambda v, src=src,
                           d=sb["X"][:, i * 1024:(i + 1) * 1024]:
                           v.tensor_tensor(out=d, in0=d, in1=src, op=ALU.add),
                           waits=w, inc=self.c_cv)
        self.v_x0_done = last
        self.st_out(ln=True)


def VB_ap(sb, vb_base, jc, hb):
    return sb["ARB"][0:128, vb_base + jc * 130 + hb * 65:
                     vb_base + jc * 130 + hb * 65 + 65]



def build_nc(upto="full"):
    mk = MK(upto)
    mk.st_setup()
    if upto == "roundtrip":
        mk.st_out(ln=False)
        return mk.finalize()
    if upto == "proj0":
        mk.natk = 0
        mk.nat_free = [(mk.c_ct, 0), (mk.c_ct, 0)]
        mk.pair_ct = {}
        mk.proj_ready = {}
        mk.vb_ready = {}
        mk.wslot_ready = {}
        mk.proj_done_ct = {}
        mk.px_free = [(mk.c_cv, 0), (mk.c_cv, 0)]
        mk.a_wot()
        mk.a_pair_weights(0)
        for _ in mk.a_proj(0):
            pass
        mk.dbg("d_qt", mk.sb["ARA"][:, 0:2048], (128, 2048))
        mk.dbg("d_kt", mk.sb["ARA"][:, 2048:4096], (128, 2048))
        mk.dbg("d_vb", mk.sb["ARB"][:, 8192:8192 + 2080], (128, 2080))
        mk.dbg("d_wqt", mk.sb["ARW"][:, 2048:3072], (128, 1024))
        mk.dbg("d_wot", mk.sb["ARA"][:, 8192:16384], (128, 8192))
        mk.dbg("d_xb", mk.sb["ARC"][:, 0:2048], (128, 2048))
        return mk.finalize()
    if upto == "attn01":
        mk.natk = 0
        mk.nat_free = [(mk.c_ct, 0), (mk.c_ct, 0)]
        mk.pair_ct = {}
        mk.proj_ready = {}
        mk.vb_ready = {}
        mk.wslot_ready = {}
        mk.proj_done_ct = {}
        mk.et_free = [(mk.c_ct, 0), (mk.c_ct, 0)]
        mk.nb_free = [(mk.c_cs, 0), (mk.c_cs, 0)]
        mk.rc_free = [(mk.c_ct, 0), (mk.c_ct, 0)]
        mk.pn_free = [(mk.c_cv, 0), (mk.c_cv, 0)]
        mk.px_free = [(mk.c_cv, 0), (mk.c_cv, 0)]
        mk.wo_done = {}
        mk._gg = 0
        mk.a_wot()
        mk.a_pair_weights(0)
        for _ in mk.a_proj(0):
            pass
        mk.a_attn(0)
        mk.a_attn(1)
        mk.dbg("d_ao", mk.sb["ARB"][:, 0:2048], (128, 2048))
        return mk.finalize()
    mk.st_partA()
    if upto == "parta":
        mk.st_out(ln=False)
        return mk.finalize()
    if upto == "wtest":
        mk.dout("d_w1s", (NHC, 128, 2048), BF16)
        mk.dout("d_w2s", (ND, 128, 2816), BF16)
        mk.op("y", lambda y: y.nop(), waits=[(mk.c_sw, mk.precast_t)])
        mk.dma("y", mk.dr["d_w1s"], mk.dr["w1s"][0])
        t = mk.dma("y", mk.dr["d_w2s"], mk.dr["w2s"][0])
        mk.op("y", lambda y: y.nop(), waits=[(mk.c_sh, t)])
        return mk.finalize()
    if upto == "w2check":
        mk.st_partB(0)
        mk.st_partB(1)
        mk.st_final()
        mk.dout("d_w2s", (2, ND, 128, 2816), BF16)
        mk.op("y", lambda y: y.nop(), waits=[(mk.c_sh, mk.c_sh.v),
                                             (mk.c_sm, mk.c_sm.v)])
        t = mk.dma("y", mk.dr["d_w2s"], mk.dr["w2s"])
        mk.op("y", lambda y: y.nop(), waits=[(mk.c_sh, t)])
        return mk.finalize()
    if upto == "bqkv":
        mk.b_prep(0)
        mk.b_qkv(0)
        mk.dbg("d_qt2", mk.sb["ARA"][:, 0:8192], (128, 8192))
        mk.dbg("d_kt2", mk.sb["ARA"][:, 8192:16384], (128, 8192))
        mk.dbg("d_vb2", mk.sb["ARB"][:, 0:8320], (128, 8320))
        return mk.finalize()
    if upto == "battn":
        mk.b_prep(0)
        mk.b_qkv(0)
        mk.attn_state_init()
        for h in range(HB + 1):
            if h < HB:
                mk.b_pass1(0, h)
                mk.b_transport(0, h)
            if h >= 1:
                mk.b_pass2(0, h - 1)
        mk.dbg("d_ao2", mk.sb["ARC"][:, 8192:16384], (128, 8192))
        return mk.finalize()
    if upto == "bffn":
        mk.b_prep(0)
        mk.b_qkv(0)
        mk.wo_dma_t = mk.dma(
            "g", mk.sb["ARW"][:, 0:4096].rearrange("p (k m) -> p k m", k=4),
            mk.dr["wo_f"][0].rearrange("(k p) m -> p k m", p=128),
            waits=[(mk.c_ct, mk.qkv_wslot_done)], inc=mk.c_wod)
        mk.emit_precast()
        mk.attn_state_init()
        for h in range(HB + 1):
            feeder = mk.b_pass1_iter(0, h) if h < HB else None
            if h == 0:
                for _ in feeder:
                    pass
                feeder = None
            if h >= 1:
                for g in range(4):
                    mk.b_pass2_g(0, h - 1, g, feeder=feeder)
            if feeder is not None:
                for _ in feeder:
                    pass
            if h < HB:
                mk.b_transport(0, h)
        for fn in mk._den_pending:
            fn()
        mk._den_pending = []
        mk.ao2_ready = mk.c_cv.v
        mk.attn_done_ct = mk.c_ct.v
        mk.bank_free[0] = mk.pa_free[0] or mk.bank_free[0]
        mk.bank_free[1] = mk.pa_free[0] or mk.bank_free[1]
        mk.bank_free[2] = mk.pa_free[1] or mk.bank_free[2]
        mk.bank_free[3] = mk.pa_free[1] or mk.bank_free[3]
        mk.b_wo(0)
        mk.b_ffn(0)
        mk.dbg("d_hln", mk.sb["ARB"][:, 8320:12416], (128, 4096))
        mk.dbg("d_xb2", mk.sb["ARB"][:, 0:4096], (128, 4096))
        mk.dbg("d_sm", mk.sb["SM"][:, :], (128, 1536))
        mk.dbg("d_smb", mk.sb["SMB"][:, :], (128, 1280))
        mk.dbg("d_sqb", mk.sb["ARB"][:, 4096:8192], (128, 4096))
        mk.dbg("d_h2c", mk.sb["ARA"][:, 0:11264], (128, 11264))
        mk.st_out(ln=False)
        return mk.finalize()
    if upto == "bwo":
        mk.b_prep(0)
        mk.b_qkv(0)
        mk.wo_dma_t = mk.dma(
            "g", mk.sb["ARW"][:, 0:4096].rearrange("p (k m) -> p k m", k=4),
            mk.dr["wo_f"][0].rearrange("(k p) m -> p k m", p=128),
            waits=[(mk.c_ct, mk.qkv_wslot_done)], inc=mk.c_wod)
        mk.emit_precast()
        mk.attn_state_init()
        for h in range(HB + 1):
            feeder = mk.b_pass1_iter(0, h) if h < HB else None
            if h == 0:
                for _ in feeder:
                    pass
                feeder = None
            if h >= 1:
                for g in range(4):
                    mk.b_pass2_g(0, h - 1, g, feeder=feeder)
            if feeder is not None:
                for _ in feeder:
                    pass
            if h < HB:
                mk.b_transport(0, h)
        for fn in mk._den_pending:
            fn()
        mk._den_pending = []
        mk.ao2_ready = mk.c_cv.v
        mk.attn_done_ct = mk.c_ct.v
        mk.bank_free[0] = mk.pa_free[0] or mk.bank_free[0]
        mk.bank_free[1] = mk.pa_free[0] or mk.bank_free[1]
        mk.bank_free[2] = mk.pa_free[1] or mk.bank_free[2]
        mk.bank_free[3] = mk.pa_free[1] or mk.bank_free[3]
        mk.b_wo(0)
        mk.st_out(ln=False)
        return mk.finalize()
    mk.st_partB(0)
    if upto == "partb0":
        mk.st_out(ln=False)
        return mk.finalize()
    mk.st_partB(1)
    mk.st_final()
    return mk.nc if False else mk.finalize()


def make_inputs_per_core(inputs):
    c = host_constants()
    x = np.ascontiguousarray(inputs["L0_attn_x"], dtype=np.float32)
    b1 = np.asarray(inputs["L0_ffn_b1"], np.float32)
    b1a = np.zeros((2, HIDP), np.float32)
    b1g = np.zeros((2, HIDP), np.float32)
    b1a[:, :HID] = b1[:, :HID]
    b1g[:, :HID] = b1[:, HID:]
    w1 = np.asarray(inputs["L0_ffn_w1"], np.float32)
    w1i = np.zeros((2, D, NHC, 2, 128), np.float32)
    w1a = np.zeros((2, D, HIDP), np.float32)
    w1g = np.zeros((2, D, HIDP), np.float32)
    w1a[:, :, :HID] = w1[:, :, :HID]
    w1g[:, :, :HID] = w1[:, :, HID:]
    w1i[:, :, :, 0, :] = w1a.reshape(2, D, NHC, 128)
    w1i[:, :, :, 1, :] = w1g.reshape(2, D, NHC, 128)
    w1i = np.ascontiguousarray(w1i.reshape(2, D, 2 * HIDP))
    w2 = np.asarray(inputs["L0_ffn_w2"], np.float32)
    w2p = np.zeros((2, HIDP, D), np.float32)
    w2p[:, :HID, :] = w2
    qs = np.asarray(inputs["L0_ffn_q_scale"], np.float32)
    ks = np.asarray(inputs["L0_ffn_k_scale"], np.float32)
    qs_rep = np.ascontiguousarray(np.broadcast_to(
        np.tile(qs, (1, HB))[:, None, :], (2, 128, INNER))).astype(np.float32)
    ks_rep = np.ascontiguousarray(np.broadcast_to(
        np.tile(ks, (1, HB))[:, None, :], (2, 128, INNER))).astype(np.float32)
    import ml_dtypes
    shared = {
        "wq": np.ascontiguousarray(inputs["L0_attn_Wq"], np.float32),
        "wkv": np.ascontiguousarray(inputs["L0_attn_Wkv"], np.float32),
        "wo_a": np.ascontiguousarray(inputs["L0_attn_Wo"], np.float32),
        "wqkv": np.ascontiguousarray(inputs["L0_ffn_wqkv"], np.float32),
        "qs_rep": qs_rep,
        "ks_rep": ks_rep,
        "wo_f": np.ascontiguousarray(inputs["L0_ffn_wo"], np.float32),
        "lng_c": col_major(np.asarray(inputs["L0_ffn_ln_g"], np.float32)),
        "lnb_c": col_major(np.asarray(inputs["L0_ffn_ln_b"], np.float32)),
        "w1i": w1i,
        "b1a_c": col_major(b1a),
        "b1g_c": col_major(b1g),
        "w2p": w2p,
        "b2_c": col_major(np.asarray(inputs["L0_ffn_b2"], np.float32)),
        "ident_f": c["ident_f"],
        "ident_b": c["ident_b"].astype(ml_dtypes.bfloat16),
        "mask_tri": c["mask_tri"],
        "mask_low": c["mask_low"],
        "ones_b": c["ones_b"].astype(ml_dtypes.bfloat16),
        "rot_qc": c["rot_qc"],
        "rot_qs": c["rot_qs"],
        "rot_kc": c["rot_kc"],
        "rot_ks": c["rot_ks"],
    }
    in_maps = []
    for core in range(8):
        m = dict(shared)
        m["x0"] = np.ascontiguousarray(x[core])
        in_maps.append(m)
    return in_maps


def kernel(**inputs):
    in_maps = make_inputs_per_core(inputs)
    nc = build_nc("full")
    res = run_bass_kernel_spmd(nc, in_maps, core_ids=list(range(8)))
    out = np.stack([np.asarray(r["y"]) for r in res.results], axis=0)
    return out.astype(np.float32)


# version 25 (final)
# speedup vs baseline: 1.0174x; 1.0107x over previous
import numpy as np
import concourse.bass as bass
import concourse.mybir as mybir
from concourse.bass_utils import run_bass_kernel_spmd

F32 = mybir.dt.float32
BF16 = mybir.dt.bfloat16
AF = mybir.ActivationFunctionType
ALU = mybir.AluOpType
AX = mybir.AxisListType

T = 2048
D = 1024
NT = T // 128
ND = D // 128
HA = 16
DH = 64
HB = 8
INNER = 512
WIN = 512
HID = 2730
HIDP = 2816
NHC = HIDP // 128
NEG = -1.0e5


class Ctr:
    def __init__(self, sem):
        self.sem = sem
        self.v = 0

    def inc(self, n=1):
        self.v += n
        return self.v


class Em:

    def __init__(self):
        self.q = {"t": [], "v": [], "s": [], "g": [], "y": []}

    def t(self, f): self.q["t"].append(f)
    def v(self, f): self.q["v"].append(f)
    def s(self, f): self.q["s"].append(f)
    def g(self, f): self.q["g"].append(f)
    def y(self, f): self.q["y"].append(f)


def host_constants():
    c = {}
    c["ident_f"] = np.eye(128, dtype=np.float32)
    c["ident_b"] = np.eye(128, dtype=np.float32)
    tri = np.zeros((128, 128), dtype=np.float32)
    low = np.zeros((128, 128), dtype=np.float32)
    for p in range(128):
        tri[p, :p] = NEG
        low[p, p + 1:] = NEG
    c["mask_tri"] = tri
    c["mask_low"] = low
    c["ones_b"] = np.ones((128, 128), dtype=np.float32)
    half = np.arange(0, DH, 2, dtype=np.float64)
    inv_freq = 1.0 / 10000.0 ** (half / DH)
    t = np.arange(T, dtype=np.float64)
    freqs = t[:, None] * inv_freq[None, :]
    freqs = np.concatenate([freqs, freqs], axis=-1)
    base = (half + 0.4 * DH) / (1.4 * DH)
    power = (t - T // 2) / float(WIN // 2)
    scale = base[None, :] ** power[:, None]
    scale = np.concatenate([scale, scale], axis=-1)
    cos = np.cos(freqs)
    sin = np.sin(freqs)
    sign = np.ones((1, DH))
    sign[0, :DH // 2] = -1.0
    qc = 8.0 * cos * scale
    qs = 8.0 * sin * scale * sign
    kc = cos / scale
    ks = sin / scale * sign

    def tm(a):
        return np.ascontiguousarray(
            a.reshape(NT, 128, DH).transpose(1, 0, 2).reshape(128, NT * DH)
        ).astype(np.float32)

    c["rot_qc"] = tm(qc)
    c["rot_qs"] = tm(qs)
    c["rot_kc"] = tm(kc)
    c["rot_ks"] = tm(ks)
    return c


def col_major(a, npart=128):
    n = a.shape[-1] // npart
    return np.ascontiguousarray(
        a.reshape(a.shape[:-1] + (n, npart)).swapaxes(-1, -2))


class MK:

    def __init__(self, upto="full"):
        self.upto = upto
        nc = bass.Bass("TRN2", target_bir_lowering=False, debug=False)
        self.nc = nc
        self.em = Em()
        self.dr = {}
        self.sb = {}
        self.ps = {}
        self._semstack = []
        self._declare_dram()
        self._alloc_sbuf()
        self._alloc_psum()
        self._alloc_sems()
        self.bank_free = [(self.sems["ct"], 0)] * 4

    def din(self, name, shape, dt=F32):
        self.dr[name] = self.nc.dram_tensor(name, list(shape), dt,
                                            kind="ExternalInput").ap()

    def dout(self, name, shape, dt=F32):
        self.dr[name] = self.nc.dram_tensor(name, list(shape), dt,
                                            kind="ExternalOutput").ap()

    def dint(self, name, shape, dt=F32):
        self.dr[name] = self.nc.dram_tensor(name, list(shape), dt,
                                            kind="Internal").ap()

    def _declare_dram(self):
        self.din("x0", (T, D))
        self.din("wq", (D, D))
        self.din("wkv", (2 * D, D))
        self.din("wo_a", (D, D))
        self.din("wqkv", (2, D, 3 * INNER))
        self.din("qs_rep", (2, 128, INNER))
        self.din("ks_rep", (2, 128, INNER))
        self.din("wo_f", (2, INNER, D))
        self.din("lng_c", (2, 128, ND))
        self.din("lnb_c", (2, 128, ND))
        self.din("w1i", (2, D, 2 * HIDP))
        self.din("b1a_c", (2, 128, NHC))
        self.din("b1g_c", (2, 128, NHC))
        self.din("w2p", (2, HIDP, D))
        self.din("b2_c", (2, 128, ND))
        self.din("ident_f", (128, 128))
        self.din("ident_b", (128, 128), BF16)
        self.din("mask_tri", (128, 128))
        self.din("mask_low", (128, 128))
        self.din("ones_b", (128, 128), BF16)
        self.din("rot_qc", (128, NT * DH))
        self.din("rot_qs", (128, NT * DH))
        self.din("rot_kc", (128, NT * DH))
        self.din("rot_ks", (128, NT * DH))
        self.dout("y", (T, D))
        self.dint("xa_spill", (128, ND * T))
        self.dint("w1s", (2, NHC, 128, 2048), BF16)
        self.dint("w2s", (2, ND, 128, 2816), BF16)
        self.dint("mscr", (2 * HB, 16, 128), BF16)

    def _alloc_sbuf(self):
        def sbuf(name, shape, dt):
            self.sb[name] = self.nc.alloc_sbuf_tensor(name, list(shape), dt).ap()
        sbuf("X", (128, ND * T), F32)
        sbuf("ARA", (128, 16384), BF16)
        sbuf("ARB", (128, 16384), BF16)
        sbuf("ARC", (128, 16384), BF16)
        sbuf("ARW", (128, 8192), BF16)
        sbuf("SM", (128, 1536), F32)
        sbuf("QKS", (128, 2 * INNER), F32)
        sbuf("STG", (128, 2 * 512), BF16)
        sbuf("STK", (128, 2 * 128), BF16)
        sbuf("ET", (128, 4 * 512), BF16)
        sbuf("PRP", (128, 3 * 512), F32)
        sbuf("IDF", (128, 128), F32)
        sbuf("IDB", (128, 128), BF16)
        sbuf("MTRI", (128, 128), F32)
        sbuf("MLOW", (128, 128), F32)
        sbuf("ONE", (128, 128), BF16)
        sbuf("MTB", (128, 256), BF16)
        sbuf("COLS", (128, 88), F32)
        sbuf("HSJ", (128, 2048), BF16)
        sbuf("SMB", (128, 1280), BF16)

    def _alloc_psum(self):
        def psum(name, shape):
            self.ps[name] = self.nc.alloc_psum_tensor(name, list(shape), F32).ap()
        psum("PA", (128, 1024))
        psum("PB", (128, 1024))
        psum("PN0", (128, 512))
        psum("PN1", (128, 512))
        psum("PX", (128, 512))
        psum("PY", (128, 512))

    def bank(self, i):
        t = self.ps["PA"] if (i % 4) < 2 else self.ps["PB"]
        off = 512 * (i % 2)
        return t[:, off:off + 512]

    def _alloc_sems(self):
        self.sems = {}
        for name in ["sw", "sh", "sm", "ct", "cv", "cs", "cg", "fb", "fb2",
                     "x0", "nat0", "nat1", "qkv0", "qkv1", "wod", "w10", "w11",
                     "w20", "w21", "m10", "m11", "m20", "m21", "rel", "spl",
                     "pc", "prp", "st0", "st1"]:
            h = self.nc.semaphore(name)
            hh = h.__enter__()
            self._semstack.append(h)
            self.sems[name] = Ctr(hh)
        for k, v in self.sems.items():
            setattr(self, "c_" + k, v)

    def finalize(self):
        em = self.em
        with self.nc.allow_low_precision(
                reason="bf16 matmul operands w/ fp32 accumulation per stated "
                       "problem tolerance"), self.nc.Block() as block:
            @block.tensor
            def _(t):
                for f in em.q["t"]:
                    f(t)

            @block.vector
            def _(v):
                for f in em.q["v"]:
                    f(v)

            @block.scalar
            def _(s):
                for f in em.q["s"]:
                    f(s)

            @block.gpsimd
            def _(g):
                for f in em.q["g"]:
                    f(g)

            @block.sync
            def _(y):
                for f in em.q["y"]:
                    f(y)
        for h in self._semstack:
            h.__exit__(None, None, None)
        return self.nc

    def op(self, eng, fn, waits=(), inc=None, inc_n=1):
        tgt = inc.inc(inc_n) if inc is not None else None

        def closure(e, fn=fn, waits=tuple(waits), inc=inc, inc_n=inc_n):
            for (c, v) in waits:
                e.wait_ge(c.sem, v)
            r = fn(e)
            if inc is not None:
                r.then_inc(inc.sem, inc_n)
        self.em.q[eng].append(closure)
        return tgt

    def dma(self, eng, dst, src, waits=(), inc=None):
        inc = inc or (self.c_sw if eng == "g" else self.c_sh)
        return self.op(eng, lambda e, d=dst, s=src: e.dma_start(out=d, in_=s),
                       waits=waits, inc=inc, inc_n=16)

    def ld_sync(self, dst, src, wait=None):
        c = self.c_sh

        def f(y, dst=dst, src=src, wait=wait):
            if wait is not None:
                y.wait_ge(wait[0].sem, wait[1])
            y.dma_start(out=dst, in_=src).then_inc(c.sem, 16)
        self.em.y(f)
        return c.inc(16)

    def ld_gp(self, dst, src, wait=None):
        c = self.c_sw

        def f(g, dst=dst, src=src, wait=wait):
            if wait is not None:
                g.wait_ge(wait[0].sem, wait[1])
            g.dma_start(out=dst, in_=src).then_inc(c.sem, 16)
        self.em.g(f)
        return c.inc(16)

    def st_setup(self):
        sb, dr, ps = self.sb, self.dr, self.ps
        self.ld_sync(sb["IDF"][:, :], dr["ident_f"][:, :])
        self.ld_sync(sb["IDB"][:, :], dr["ident_b"][:, :])
        self.ld_sync(sb["MTRI"][:, :], dr["mask_tri"][:, :])
        self.ld_sync(sb["MLOW"][:, :], dr["mask_low"][:, :])
        vm = self.ld_sync(sb["ONE"][:, :], dr["ones_b"][:, :])
        self.const_t = vm
        self.op("s", lambda ss: ss.copy(out=sb["MTB"][:, 0:128],
                                        in_=sb["MTRI"][:, :]),
                waits=[(self.c_sh, vm)])
        self.mtb_ready = (self.c_cs,
                          self.op("s", lambda ss:
                                  ss.copy(out=sb["MTB"][:, 128:256],
                                          in_=sb["MLOW"][:, :]),
                                  inc=self.c_cs))

        xtm = sb["ARB"].bitcast(F32)
        X = sb["X"]
        XB = sb["ARC"]
        x0r = dr["x0"].rearrange("(n p) d -> p n d", p=128)
        tz = self.c_ct.v
        vz = self.c_cv.v
        k = 0
        for half in range(2):
            wait = None
            if half == 1:
                wait = (self.c_ct, tz + 64)
            v_x0 = self.dma("y", xtm[:, 0:8 * D],
                            x0r[:, half * 8:(half + 1) * 8, :],
                            waits=[wait] if wait else [], inc=self.c_x0)
            for idx in range(8 * ND):
                n, c = divmod(idx, ND)
                nn = half * 8 + n
                pt = self.bank(k)[:, 0:128]
                vt = self.c_ct.inc()

                def f(t, n=n, c=c, pt=pt, k=k, v_x0=v_x0, first=(idx == 0)):
                    if first:
                        t.wait_ge(self.c_sh.sem, self.const_t)
                        t.wait_ge(self.c_x0.sem, v_x0)
                    if k >= 4:
                        t.wait_ge(self.c_cv.sem, vz + 2 * (k - 3))
                    t.matmul(pt, xtm[:, n * D + c * 128: n * D + (c + 1) * 128],
                             sb["IDF"][:, :], start=True, stop=True
                             ).then_inc(self.c_ct.sem, 1)
                self.em.t(f)

                dst_f = X[:, c * T + nn * 128: c * T + (nn + 1) * 128]
                dst_b = XB[:, c * T + nn * 128: c * T + (nn + 1) * 128]
                self.c_cv.inc(2)

                def fv(v, pt=pt, dst_f=dst_f, dst_b=dst_b, vt=vt):
                    v.wait_ge(self.c_ct.sem, vt)
                    v.tensor_copy(out=dst_f, in_=pt).then_inc(self.c_cv.sem, 1)
                    v.tensor_copy(out=dst_b, in_=pt).then_inc(self.c_cv.sem, 1)
                self.em.v(fv)
                k += 1
        self.v_x0_done = self.c_cv.v
        self.bank_free = [(self.c_cv, self.v_x0_done)] * 4
        self.op("v", lambda v: v.memset(self.sb["COLS"][:, 4:5], 1e-5))


    def st_out(self, ln=False):
        sb, dr = self.sb, self.dr
        X = sb["X"]
        ytile = sb["ARW"].bitcast(F32)
        yout = dr["y"].rearrange("(n p) d -> p n d", p=128)
        x_ready = (self.c_cv, self.c_cv.v)
        store_t = {}
        for n in range(NT):
            dst = ytile[:, (n % 2) * D:(n % 2) * D + D]
            evs = []
            for c in range(ND):
                b = self.bank(c % 4)
                w = [self.bank_free[c % 4]]
                if n == 0 and c == 0:
                    w.append(x_ready)
                tt = self.op("t", lambda t, b=b[:, 0:128], src=X[:, c * T + n * 128: c * T + (n + 1) * 128]:
                             t.matmul(b, src, sb["IDF"][:, :], start=True, stop=True),
                             waits=w, inc=self.c_ct)
                wv = [(self.c_ct, tt)]
                if n >= 2 and c == 0:
                    wv.append(((self.c_st0 if n % 2 == 0 else self.c_st1),
                               store_t[n - 2]))
                ev = self.op("v", lambda v, b=b[:, 0:128], d=dst[:, c * 128:(c + 1) * 128]:
                             v.tensor_copy(out=d, in_=b),
                             waits=wv, inc=self.c_cv)
                self.bank_free[c % 4] = (self.c_cv, ev)
                evs.append(ev)
            vdone = (self.c_cv, evs[-1])
            if ln:
                vdone = self._ln_row_tile(dst, vdone)
            store_t[n] = self.dma("y", yout[:, n:n + 1, :], dst.unsqueeze(1),
                                  waits=[vdone],
                                  inc=(self.c_st0 if n % 2 == 0 else self.c_st1))
        self.op("y", lambda y: y.nop(),
                waits=[(self.c_sh, self.c_sh.v), (self.c_st0, self.c_st0.v),
                       (self.c_st1, self.c_st1.v)])

    def _ln_row_tile(self, dst, vdone):
        sb = self.sb
        sm = sb["SM"]
        s1, s2, mu, e2, mu2, var, rstd = (sm[:, i:i + 1] for i in
                                          (0, 4, 8, 12, 16, 20, 24))
        sq = sb["PRP"][:, 0:1024]
        fill = sm[:, 127:128]
        r1 = self.op("v", lambda v: v.reduce_sum(s1, dst, axis=AX.X),
                     waits=[vdone], inc=self.c_cv)
        r2 = self.op("v", lambda v: v.tensor_mul(sq, dst, dst),
                     inc=self.c_cv)
        r3 = self.op("v", lambda v: v.reduce_sum(s2, sq, axis=AX.X),
                     waits=[(self.c_cv, r2)], inc=self.c_cv)
        m1 = self.op("v", lambda v: v.tensor_scalar(out=mu, in0=s1,
                                                    scalar1=1.0 / D,
                                                    scalar2=None, op0=ALU.mult),
                     waits=[(self.c_cv, r1)], inc=self.c_cv)
        self.op("v", lambda v: v.tensor_scalar(out=e2, in0=s2,
                                               scalar1=1.0 / D,
                                               scalar2=None, op0=ALU.mult),
                waits=[(self.c_cv, r3)], inc=self.c_cv)
        m3 = self.op("v", lambda v: v.tensor_scalar(out=mu2, in0=mu,
                                                    scalar1=mu, scalar2=None,
                                                    op0=ALU.mult),
                     waits=[(self.c_cv, m1)], inc=self.c_cv)
        vv = self.op("v", lambda v: v.tensor_scalar(
            out=var, in0=e2, scalar1=mu2, scalar2=None, op0=ALU.subtract),
            waits=[(self.c_cv, m3)], inc=self.c_cv)
        st0 = self.op("s", lambda s: s.activation(
            out=sm[:, 28:29], in_=var, func=AF.Ln, bias=sb["COLS"][:, 4:5]),
            waits=[(self.c_cv, vv)], inc=self.c_cs)
        st = self.op("s", lambda s: s.activation(
            out=rstd, in_=sm[:, 28:29], func=AF.Exp, scale=-0.5),
            waits=[(self.c_cs, st0)], inc=self.c_cs)
        out = self.op("v", lambda v: v.tensor_scalar(
            out=dst, in0=dst, scalar1=mu, scalar2=rstd, op0=ALU.subtract,
            op1=ALU.mult), waits=[(self.c_cs, st), (self.c_cv, vv)],
            inc=self.c_cv)
        return (self.c_cv, out)

    def dbg(self, name, ap, shape, dt=None):
        dt = dt or ap.dtype
        self.dout(name, shape, dt)
        waits = [(self.c_ct, self.c_ct.v), (self.c_cv, self.c_cv.v),
                 (self.c_cs, self.c_cs.v)]
        self.dma("y", self.dr[name][:, :] if len(shape) == 2 else self.dr[name],
                 ap, waits=waits)
        self.op("y", lambda y: y.nop(), waits=[(self.c_sh, self.c_sh.v)])

    def st_partA(self):
        self.natk = 0
        self.nat_free = [(self.c_ct, 0), (self.c_ct, 0)]
        self.pair_ct = {}
        self.proj_ready = {}
        self.vb_ready = {}
        self.wslot_ready = {}
        self.proj_done_ct = {}
        self.et_free = [None, None, None, None]
        self.nb_free = [(self.c_cs, 0), (self.c_cs, 0)]
        self.rc_free = [(self.c_ct, 0), (self.c_ct, 0)]
        self.pn_free = [(self.c_cv, 0)] * 4
        self.ao_ready = {0: None, 1: None}
        self.wo_done = {}
        self._gg = 0
        self._den_pending = []
        self.a_wot()
        self._cv_ao = {0: [], 1: []}
        for p in range(9):
            if p < 8:
                self.a_pair_weights(p)
                self.a_proj(p)
            if p >= 1:
                self.a_attn(2 * (p - 1))
                self.a_attn(2 * (p - 1) + 1)
            if p == 4:
                for fn in self._den_pending:
                    fn()
                self._den_pending = []
                self.ao_ready[0] = self.c_cv.v
                self.a_wo_half(0)
        for fn in self._den_pending:
            fn()
        self._den_pending = []
        self.ao_ready[1] = self.c_cv.v
        self.a_wo_half(1)
        self.a_ln1()


    def nat_slot(self):
        s = self.natk % 2
        self.natk += 1
        return s, self.sb["ARW"][:, s * 1024:(s + 1) * 1024]

    def nat_sem(self, s):
        return self.c_nat0 if s == 0 else self.c_nat1

    def a_wot(self):
        sb, dr = self.sb, self.dr
        WOT = sb["ARA"]
        for o in range(ND):
            s, nat = self.nat_slot()
            dt = self.dma("g", nat, dr["wo_a"][o * 128:(o + 1) * 128, :],
                          waits=[self.nat_free[s]], inc=self.nat_sem(s))
            last = None
            for c in range(ND):
                b = self.bank(c % 4)
                w = [self.bank_free[c % 4]]
                if c == 0:
                    w.append((self.nat_sem(s), dt))
                tt = self.op("t", lambda t, b=b[:, 0:128],
                             src=nat[:, c * 128:(c + 1) * 128]:
                             t.matmul(b, src, sb["IDB"][:, :], start=True, stop=True),
                             waits=w, inc=self.c_ct)
                ev = self.op("v", lambda v, b=b[:, 0:128],
                             d=WOT[:, 8192 + c * 1024 + o * 128: 8192 + c * 1024 + (o + 1) * 128]:
                             v.tensor_copy(out=d, in_=b),
                             waits=[(self.c_ct, tt)], inc=self.c_cv)
                self.bank_free[c % 4] = (self.c_cv, ev)
                last = tt
            self.nat_free[s] = (self.c_ct, last)
        self.wot_ready = self.c_cv.v

    def a_pair_weights(self, p):
        sb, dr = self.sb, self.dr
        s = p % 2
        base = 2048 + s * 3072
        srcs = [dr["wq"][p * 128:(p + 1) * 128, :],
                dr["wkv"][p * 128:(p + 1) * 128, :],
                dr["wkv"][D + p * 128: D + (p + 1) * 128, :]]
        slot_free = (self.c_ct, self.proj_done_ct.get(p - 2, 0))
        evs = []
        for wi, src in enumerate(srcs):
            ns, nat = self.nat_slot()
            dt = self.dma("g", nat, src, waits=[self.nat_free[ns]],
                          inc=self.nat_sem(ns))
            last = None
            for c in range(ND):
                b = self.bank((c + wi) % 4)
                w = [self.bank_free[(c + wi) % 4]]
                if c == 0:
                    w.append((self.nat_sem(ns), dt))
                    w.append(slot_free)
                tt = self.op("t", lambda t, b=b[:, 0:128],
                             src2=nat[:, c * 128:(c + 1) * 128]:
                             t.matmul(b, src2, sb["IDB"][:, :], start=True, stop=True),
                             waits=w, inc=self.c_ct)
                ev = self.op("v", lambda v, b=b[:, 0:128],
                             d=sb["ARW"][:, base + wi * 1024 + c * 128: base + wi * 1024 + (c + 1) * 128]:
                             v.tensor_copy(out=d, in_=b),
                             waits=[(self.c_ct, tt)], inc=self.c_cv)
                self.bank_free[(c + wi) % 4] = (self.c_cv, ev)
                last = tt
            self.nat_free[ns] = (self.c_ct, last)
            evs.append(self.c_cv.v)
        self.wslot_ready[p] = self.c_cv.v

    def a_proj(self, p):
        sb = self.sb
        s = p % 2
        QT = sb["ARA"][:, s * 4096: s * 4096 + 2048]
        KT = sb["ARA"][:, s * 4096 + 2048: s * 4096 + 4096]
        VB = sb["ARB"][:, 8192 + s * 2080: 8192 + (s + 1) * 2080]
        XB = sb["ARC"]
        WQ = sb["ARW"][:, 2048 + s * 3072: 2048 + s * 3072 + 1024]
        WK = sb["ARW"][:, 2048 + s * 3072 + 1024: 2048 + s * 3072 + 2048]
        WV = sb["ARW"][:, 2048 + s * 3072 + 2048: 2048 + s * 3072 + 3072]
        wready = (self.c_cv, self.wslot_ready[p])
        slot_free = (self.c_ct, self.pair_ct.get(p - 2, 0))
        vb_ones = bass.AP(sb["ARB"].tensor, 8192 + s * 2080 + 64,
                          [[16384, 128], [130, 16], [65, 2]])
        mz = self.op("v", lambda v: v.memset(vb_ones, 1.0),
                     waits=[slot_free], inc=self.c_cv)
        self.vb_ready[p] = mz
        for wi, (W, DST) in enumerate([(WQ, QT), (WK, KT)]):
            for c in range(ND):
                for tc in range(4):
                    w = []
                    if c == 0:
                        w.append(self.bank_free[tc])
                        if tc == 0:
                            w.append(wready)
                            w.append(slot_free)
                            if wi == 0 and p == 0:
                                w.append((self.c_cv, self.v_x0_done))
                    self.op("t", lambda t, b=self.bank(tc),
                            lh=W[:, c * 128:(c + 1) * 128],
                            rh=XB[:, c * T + tc * 512: c * T + tc * 512 + 512],
                            st=(c == 0), sp=(c == ND - 1):
                            t.matmul(b, lh, rh, start=st, stop=sp),
                            waits=w, inc=self.c_ct)
            mmt = self.c_ct.v
            for tc in range(4):
                ev = self.op("v", lambda v, b=self.bank(tc),
                             d=DST[:, tc * 512:(tc + 1) * 512]:
                             v.tensor_copy(out=d, in_=b),
                             waits=[(self.c_ct, mmt)], inc=self.c_cv)
                self.bank_free[tc] = (self.c_cv, ev)
        for n in range(NT):
            bidx = n % 4
            for c in range(ND):
                w = []
                if c == 0:
                    w.append(self.bank_free[bidx])
                tt = self.op("t", lambda t, b=self.bank(bidx)[:, 0:128],
                             lh=XB[:, c * T + n * 128: c * T + (n + 1) * 128],
                             rh=WV[:, c * 128:(c + 1) * 128],
                             st=(c == 0), sp=(c == ND - 1):
                             t.matmul(b, lh, rh, start=st, stop=sp),
                             waits=w, inc=self.c_ct)
            ev = None
            for hb in range(2):
                ev = self.op("v", lambda v, b=self.bank(bidx)[:, hb * 64:hb * 64 + 64],
                             d=VB[:, n * 130 + hb * 65: n * 130 + hb * 65 + 64]:
                             v.tensor_copy(out=d, in_=b),
                             waits=[(self.c_ct, tt)] if hb == 0 else [],
                             inc=self.c_cv)
            self.bank_free[bidx] = (self.c_cv, ev)
        self.proj_ready[p] = self.c_cv.v
        self.proj_done_ct[p] = self.c_ct.v

    def a_attn(self, h):
        sb = self.sb
        p, hb = divmod(h, 2)
        s = p % 2
        QT = sb["ARA"][:, s * 4096: s * 4096 + 2048]
        KT = sb["ARA"][:, s * 4096 + 2048: s * 4096 + 4096]
        vb_base = 8192 + s * 2080
        hh = h % 8
        ao_p0 = (hh % 2) * 64
        ao_cb = (hh // 2) * 2048
        pready = (self.c_cv, self.proj_ready[p])
        vbready = (self.c_cv, self.vb_ready[p])
        pnb = [self.ps["PN0"], self.ps["PN1"], self.ps["PX"], self.ps["PY"]]
        for g in range(4):
            gg = self._gg
            njc = 4 * g + 4
            pn = pnb[gg % 4]
            ndg = 4 * g
            units = [(jc, jc + 1) for jc in range(0, ndg - 1, 2)]
            units += [(jc,) for jc in range(ndg, njc)]
            for unit in units:
                simt = None
                for jc in unit:
                    diag = jc >= ndg
                    delta = max(0, (jc - ndg) * 128)
                    N = 512 - delta
                    sbk = jc % 4
                    w = [self.bank_free[sbk]]
                    if jc == 0:
                        w.append(pready)
                        w.append(self.mtb_ready)
                    if diag and N > 128:
                        qb = g * 512 + delta
                        self.op("t", lambda t, b=self.bank(sbk)[:, 128:N],
                                lh=KT[hb * 64: hb * 64 + 64,
                                      jc * 128:(jc + 1) * 128],
                                rh=QT[hb * 64: hb * 64 + 64,
                                      qb + 128:(g + 1) * 512]:
                                t.matmul(b, lh, rh, start=True, stop=False),
                                waits=w, inc=self.c_ct)
                        self.op("t", lambda t, b=self.bank(sbk)[:, 0:128]:
                                t.matmul(b, sb["IDB"][:, :],
                                         sb["MTB"][:, 0:128],
                                         start=False, stop=False),
                                inc=self.c_ct)
                        simt = self.op("t", lambda t, b=self.bank(sbk)[:, 0:128],
                                       lh=KT[hb * 64: hb * 64 + 64,
                                             jc * 128:(jc + 1) * 128],
                                       rh=QT[hb * 64: hb * 64 + 64,
                                             qb:qb + 128]:
                                       t.matmul(b, lh, rh, start=False,
                                                stop=True),
                                       inc=self.c_ct)
                    else:
                        if diag:
                            self.op("t", lambda t, b=self.bank(sbk)[:, 0:128]:
                                    t.matmul(b, sb["IDB"][:, :],
                                             sb["MTB"][:, 0:128],
                                             start=True, stop=False),
                                    waits=w, inc=self.c_ct)
                            w = []
                        simt = self.op("t", lambda t, b=self.bank(sbk)[:, 0:N],
                                       lh=KT[hb * 64: hb * 64 + 64,
                                             jc * 128:(jc + 1) * 128],
                                       rh=QT[hb * 64: hb * 64 + 64,
                                             g * 512 + delta: (g + 1) * 512],
                                       st=(not diag):
                                       t.matmul(b, lh, rh, start=st, stop=True),
                                       waits=w, inc=self.c_ct)
                we = [(self.c_ct, simt)]
                if len(unit) == 2:
                    sbkA = unit[0] % 4
                    for sbx in (sbkA, sbkA + 1):
                        if self.et_free[sbx] is not None:
                            we.append(self.et_free[sbx])
                    pt = self.ps["PA"] if sbkA < 2 else self.ps["PB"]
                    et = self.op("s", lambda ss, b=pt[:, 0:1024],
                                 d=sb["ET"][0:128, sbkA * 512:
                                            sbkA * 512 + 1024]:
                                 ss.activation(out=d, in_=b, func=AF.Exp,
                                               scale=1.0 / 8.0),
                                 waits=we, inc=self.c_cs)
                    self.bank_free[sbkA] = (self.c_cs, et)
                    self.bank_free[sbkA + 1] = (self.c_cs, et)
                else:
                    jc = unit[0]
                    sbk = jc % 4
                    N = 512 - max(0, (jc - ndg) * 128)
                    if self.et_free[sbk] is not None:
                        we.append(self.et_free[sbk])
                    et = self.op("s", lambda ss, b=self.bank(sbk)[:, 0:N],
                                 d=sb["ET"][0:128, sbk * 512: sbk * 512 + N]:
                                 ss.activation(out=d, in_=b, func=AF.Exp,
                                               scale=1.0 / 8.0),
                                 waits=we, inc=self.c_cs)
                    self.bank_free[sbk] = (self.c_cs, et)
                for jc in unit:
                    sbk = jc % 4
                    delta = max(0, (jc - ndg) * 128)
                    N = 512 - delta
                    wn = [(self.c_cs, et)] if jc == unit[0] else []
                    if jc == 0:
                        wn.append(self.pn_free[gg % 4])
                        wn.append(vbready)
                    nstop = self.op("t", lambda t, b=pn[0:65, delta:512],
                                    lh=VB_ap(sb, vb_base, jc, hb),
                                    rh=sb["ET"][0:128, sbk * 512:
                                                sbk * 512 + N],
                                    st=(jc == 0), sp=(jc == njc - 1):
                                    t.matmul(b, lh, rh, start=st, stop=sp),
                                    waits=wn, inc=self.c_ct)
                    self.et_free[sbk] = (self.c_ct, nstop)
                if any(jc == 1 for jc in unit) and self._den_pending:
                    for fn in self._den_pending:
                        fn()
                    self._den_pending = []
            numt = self.c_ct.v
            self._den_pending.append(
                lambda gg=gg, pn=pn, numt=numt, h=h, g=g, ao_p0=ao_p0,
                ao_cb=ao_cb: self._a_den(gg, pn, numt, h, g, ao_p0, ao_cb))
            self._gg += 1
        self.pair_ct[p] = self.c_ct.v

    def _a_den(self, gg, pn, numt, h, g, ao_p0, ao_cb):
            sb = self.sb
            rcslot = gg % 2
            rc = sb["SMB"][0:1, rcslot * 512: rcslot * 512 + 512]
            ri = sb["SM"][0:1, 512 + rcslot * 512: 1024 + rcslot * 512]
            lnt = self.op("s", lambda ss, ri=ri, pn=pn:
                          ss.activation(out=ri, in_=pn[64:65, 0:512],
                                        func=AF.Ln),
                          waits=[(self.c_ct, numt), self.rc_free[rcslot]],
                          inc=self.c_cs)
            rt = self.op("s", lambda ss, ri=ri, rc=rc:
                         ss.activation(out=rc, in_=ri, func=AF.Exp,
                                       scale=-1.0),
                         waits=[(self.c_cs, lnt)], inc=self.c_cs)
            nbslot = gg % 2
            nb = sb["SMB"][64:128, nbslot * 512: nbslot * 512 + 512]
            nbt = self.op("v", lambda v, nb=nb, pn=pn:
                          v.tensor_copy(out=nb, in_=pn[0:64, 0:512]),
                          waits=[self.nb_free[nbslot], (self.c_ct, numt)],
                          inc=self.c_cv)
            rpt = self.op("t", lambda t, pn=pn, rc=rc:
                          t.matmul(pn[64:128, 0:512], sb["ONE"][0:1, 0:64], rc,
                                   start=True, stop=True),
                          waits=[(self.c_cs, rt), (self.c_cv, nbt)],
                          inc=self.c_ct)
            self.rc_free[rcslot] = (self.c_ct, rpt)
            wao = [(self.c_ct, rpt)]
            if h >= 8 and self.wo_done.get(0) is not None:
                wao.append((self.c_ct, self.wo_done[0]))
            ev = self.op("v", lambda v, pn=pn, nb=nb,
                         d=sb["ARB"][ao_p0:ao_p0 + 64,
                                     ao_cb + g * 512: ao_cb + (g + 1) * 512]:
                         v.scalar_tensor_tensor(out=d, in0=pn[64:128, 0:512],
                                                scalar=1.0, in1=nb,
                                                op0=ALU.mult, op1=ALU.mult),
                         waits=wao, inc=self.c_cv)
            self.pn_free[gg % 4] = (self.c_cv, ev)
            self.nb_free[nbslot] = (self.c_cv, ev)

    def a_wo_half(self, half):
        sb = self.sb
        WOT = sb["ARA"]
        aoready = (self.c_cv, self.ao_ready[half])
        k = 0
        for o in range(ND):
            for tc in range(4):
                bidx = k % 4
                for f in range(4):
                    w = []
                    if f == 0:
                        w.append(self.bank_free[bidx])
                        if k == 0:
                            w.append(aoready)
                            w.append((self.c_cv, self.wot_ready))
                    self.op("t", lambda t, b=self.bank(bidx),
                            lh=WOT[:, 8192 + (half * 4 + f) * 1024 + o * 128:
                                   8192 + (half * 4 + f) * 1024 + (o + 1) * 128],
                            rh=sb["ARB"][:, f * 2048 + tc * 512: f * 2048 + tc * 512 + 512],
                            st=(f == 0), sp=(f == 3):
                            t.matmul(b, lh, rh, start=st, stop=sp),
                            waits=w, inc=self.c_ct)
                mmt = self.c_ct.v
                ev = self.op("v", lambda v, b=self.bank(bidx),
                             d=sb["X"][:, o * T + tc * 512: o * T + tc * 512 + 512]:
                             v.scalar_tensor_tensor(out=d, in0=b, scalar=1.0,
                                                    in1=d, op0=ALU.mult,
                                                    op1=ALU.add),
                             waits=[(self.c_ct, mmt)], inc=self.c_cv)
                self.bank_free[bidx] = (self.c_cv, ev)
                k += 1
        self.wo_done[half] = self.c_ct.v
        self.x1_ready = self.c_cv.v

    def a_ln1(self):
        sb = self.sb
        X = sb["X"]
        XBL = sb["ARB"][:, 0:4096]
        SQB = sb["ARB"][:, 4096:8192]
        wo1 = (self.c_ct, self.wo_done[1])
        xready = (self.c_cv, self.x1_ready)
        PX, PY = self.ps["PX"], self.ps["PY"]
        PN0, PN1 = self.ps["PN0"], self.ps["PN1"]
        apply_done = 0
        for tc in range(4):
            xr = X[:, :].rearrange("q (c t) -> q c t", c=ND)[:, :, tc * 512:(tc + 1) * 512]
            cst = self.op("s", lambda ss, xr=xr,
                          d=XBL.rearrange("q (c t) -> q c t", c=ND):
                          ss.copy(out=d, in_=xr),
                          waits=[xready, wo1,
                                 (self.c_cv, apply_done)], inc=self.c_cs)
            sqt = self.op("v", lambda v: v.tensor_mul(SQB, XBL, XBL),
                          waits=[(self.c_cs, cst)], inc=self.c_cv)
            for c in range(ND):
                self.op("t", lambda t, rh=XBL[:, c * 512:(c + 1) * 512],
                        st=(c == 0), sp=(c == ND - 1):
                        t.matmul(PX[0:1, 0:512], sb["ONE"][0:128, 0:1], rh,
                                 start=st, stop=sp),
                        waits=[(self.c_cs, cst), self.pn_free[2]] if c == 0 else [],
                        inc=self.c_ct)
            for c in range(ND):
                self.op("t", lambda t, rh=SQB[:, c * 512:(c + 1) * 512],
                        st=(c == 0), sp=(c == ND - 1):
                        t.matmul(PY[0:1, 0:512], sb["ONE"][0:128, 0:1], rh,
                                 start=st, stop=sp),
                        waits=[(self.c_cv, sqt), self.pn_free[3]] if c == 0 else [],
                        inc=self.c_ct)
            sums = self.c_ct.v
            sm = sb["SM"]
            murow = sm[0:1, 0:512]
            e2row = sm[0:1, 512:1024]
            varrow = sm[0:1, 1024:1536]
            rrowb = sb["SMB"][0:1, 0:512]
            srowb = sb["SMB"][0:1, 512:1024]
            mu1 = self.op("v", lambda v: v.tensor_scalar(
                out=murow, in0=PX[0:1, 0:512], scalar1=1.0 / D, scalar2=None,
                op0=ALU.mult), waits=[(self.c_ct, sums)], inc=self.c_cv)
            self.op("v", lambda v: v.tensor_scalar(
                out=e2row, in0=PY[0:1, 0:512], scalar1=1.0 / D, scalar2=None,
                op0=ALU.mult), inc=self.c_cv)
            m3 = self.op("v", lambda v: v.tensor_mul(varrow, murow, murow),
                         waits=[(self.c_cv, mu1)], inc=self.c_cv)
            vv = self.op("v", lambda v: v.tensor_tensor(
                out=varrow, in0=e2row, in1=varrow, op=ALU.subtract),
                waits=[(self.c_cv, m3)], inc=self.c_cv)
            st0 = self.op("s", lambda s2: s2.activation(
                out=sm[0:1, 1024:1536], in_=varrow, func=AF.Ln,
                bias=sb["COLS"][0:1, 4:5]), waits=[(self.c_cv, vv)],
                inc=self.c_cs)
            st1 = self.op("s", lambda s2: s2.activation(
                out=sm[0:1, 512:1024], in_=sm[0:1, 1024:1536], func=AF.Exp,
                scale=-0.5), waits=[(self.c_cs, st0)], inc=self.c_cs)
            vv2 = self.op("v", lambda v: v.tensor_copy(out=rrowb,
                                                       in_=sm[0:1, 512:1024]),
                          waits=[(self.c_cs, st1), self.rc_free[0],
                                 self.rc_free[1]])
            vv2 = self.op("v", lambda v: v.tensor_tensor(
                out=srowb, in0=murow, in1=sm[0:1, 512:1024], op=ALU.mult),
                waits=[(self.c_cv, mu1)], inc=self.c_cv)
            rep1 = self.op("t", lambda t: t.matmul(
                PN0[0:128, 0:512], sb["ONE"][0:1, 0:128], rrowb,
                start=True, stop=True),
                waits=[(self.c_cv, vv2), self.pn_free[0]], inc=self.c_ct)
            rep2 = self.op("t", lambda t: t.matmul(
                PN1[0:128, 0:512], sb["ONE"][0:1, 0:128], srowb,
                start=True, stop=True),
                waits=[self.pn_free[1]], inc=self.c_ct)
            mts = None
            for c in range(ND):
                mts = self.op("v", lambda v,
                              d=X[:, c * T + tc * 512: c * T + tc * 512 + 512]:
                              v.tensor_tensor(out=d, in0=d,
                                              in1=PN0[0:128, 0:512],
                                              op=ALU.mult),
                              waits=[(self.c_ct, rep2)] if c == 0 else [],
                              inc=self.c_cv if c == ND - 1 else None)
            ap = None
            for c in range(ND):
                ap = self.op("v", lambda v,
                             d=X[:, c * T + tc * 512: c * T + tc * 512 + 512]:
                             v.tensor_tensor(out=d, in0=d,
                                             in1=PN1[0:128, 0:512],
                                             op=ALU.subtract),
                             waits=[(self.c_cv, mts)] if c == 0 else [],
                             inc=self.c_cv)
            self.pn_free = [(self.c_cv, ap)] * 4
            apply_done = ap
        self.xa_ready = apply_done
        self.xa_spilled = self.dma("y", self.dr["xa_spill"][:, :], X[:, :],
                                   waits=[(self.c_cv, apply_done)],
                                   inc=self.c_spl)


    def emit_precast(self):
        W1ROW = 2 * HIDP
        for li in range(2):
            for hc in range(NHC):
                s1 = bass.AP(self.dr["w1i"].tensor, li * D * W1ROW + hc * 256,
                             [[W1ROW, 128], [128 * W1ROW, 8], [1, 256]])
                d1 = bass.AP(self.dr["w1s"].tensor,
                             (li * NHC + hc) * 128 * 2048,
                             [[2048, 128], [256, 8], [1, 256]])
                self.dma("g", d1, s1, inc=self.c_pc)
            for o in range(ND):
                s2 = bass.AP(self.dr["w2p"].tensor, li * HIDP * D + o * 128,
                             [[D, 128], [128 * D, NHC], [1, 128]])
                d2 = bass.AP(self.dr["w2s"].tensor, (li * ND + o) * 128 * 2816,
                             [[2816, 128], [128, NHC], [1, 128]])
                self.dma("g", d2, s2, inc=self.c_pc)
        self.precast_t = self.c_pc.v

    def st_partB(self, l):
        self.b_prep(l)
        self.b_qkv(l)
        wsrc = self.dr["wo_f"][l].rearrange("(k p) m -> p k m", p=128)
        self.wo_dma_t = self.dma(
            "g", self.sb["ARW"][:, 0:4096].rearrange("p (k m) -> p k m", k=4),
            wsrc, waits=[(self.c_ct, self.qkv_wslot_done)], inc=self.c_wod)
        if l == 0:
            self.emit_precast()
        self.attn_state_init()
        for h in range(HB + 1):
            feeder = self.b_pass1_iter(l, h) if h < HB else None
            if h == 0:
                for _ in feeder:
                    pass
                feeder = None
            if h >= 1:
                for g in range(4):
                    self.b_pass2_g(l, h - 1, g, feeder=feeder)
            if feeder is not None:
                for _ in feeder:
                    pass
            if h < HB:
                self.b_transport(l, h)
        for fn in self._den_pending:
            fn()
        self._den_pending = []
        self.ao2_ready = self.c_cv.v
        self.attn_done_ct = self.c_ct.v
        self.bank_free[0] = self.pa_free[0] or self.bank_free[0]
        self.bank_free[1] = self.pa_free[0] or self.bank_free[1]
        self.bank_free[2] = self.pa_free[1] or self.bank_free[2]
        self.bank_free[3] = self.pa_free[1] or self.bank_free[3]
        self.b_wo(l)
        self.b_ffn(l)

    def b_prep(self, l):
        sb, dr = self.sb, self.dr
        rot = sb["ARC"].bitcast(F32)
        if l == 0:
            wct = [(self.c_ct, self.proj_done_ct[7])]
        else:
            wct = [(self.c_ct, self.w2mm_done), (self.c_cv, self.ffn_done_cv),
                   (self.c_cs, self.c_cs.v)]
        for i, nm in enumerate(["rot_qc", "rot_qs", "rot_kc", "rot_ks"]):
            self.dma("y", rot[:, i * 1024:(i + 1) * 1024], dr[nm][:, :],
                     waits=wct if i == 0 else [], inc=self.c_prp)
        if l == 0:
            self.stk_ones = self.op("v", lambda v: v.memset(
                sb["STK"][64:65, :], 1.0), inc=self.c_cv)
        self.dma("y", sb["QKS"][:, 0:512], dr["qs_rep"][l], inc=self.c_prp)
        self.dma("y", sb["QKS"][:, 512:1024], dr["ks_rep"][l], inc=self.c_prp)
        self.dma("y", sb["COLS"][:, 8:16], dr["lng_c"][l], inc=self.c_prp)
        self.dma("y", sb["COLS"][:, 16:24], dr["lnb_c"][l], inc=self.c_prp)
        self.dma("y", sb["COLS"][:, 24:32], dr["b2_c"][l], inc=self.c_prp)
        self.dma("y", sb["COLS"][:, 32:54], dr["b1a_c"][l], inc=self.c_prp)
        self.prep_t = self.dma("y", sb["COLS"][:, 54:76], dr["b1g_c"][l],
                               inc=self.c_prp)

    def b_qkv(self, l):
        sb, dr = self.sb, self.dr
        X = sb["X"]
        rot = sb["ARC"].bitcast(F32)
        PRP = sb["PRP"]
        arbf = sb["ARB"].bitcast(F32)
        PSET = [
            dict(P0=PRP[:, 0:512], P1=PRP[:, 512:1024], P2=PRP[:, 1024:1536],
                 gt=sb["PRP"].tensor, gstride=1536, gbase=0),
            dict(P0=arbf[:, 4160:4672], P1=arbf[:, 4672:5184],
                 P2=arbf[:, 5184:5696], gt=arbf.tensor, gstride=8192,
                 gbase=4160),
        ]
        sm = sb["SM"]
        fill = sm[:, 127:128]
        if l == 0:
            wx = (self.c_cv, self.xa_ready)
            wfence = (self.c_ct, self.wo_done[1])
        else:
            wx = (self.c_cv, self.ffn_done_cv)
            wfence = (self.c_ct, self.w2mm_done)
        vb1 = bass.AP(sb["ARB"].tensor, 64, [[16384, 128], [520, 16], [65, 8]])
        self.vb2_ones = self.op("v", lambda v: v.memset(vb1, 1.0),
                                waits=[wx, wfence], inc=self.c_cv)
        hsj_free = [None, None]
        wslot_t = [None, None]
        stg_free = [None, None]
        prp_free = [[], []]
        sm_free = [None, None]

        for f in range(3):
            ws = f % 2
            wsl = sb["ARW"][:, ws * 4096:(ws + 1) * 4096]
            src = dr["wqkv"][l][:, f * 512:(f + 1) * 512].rearrange(
                "(k p) f -> p k f", p=128)
            wwait = list(prp_free) and []
            wwait = []
            if wslot_t[ws] is not None:
                wwait.append((self.c_ct, wslot_t[ws]))
            if f == 0:
                wwait.append(wfence)
            wdma = self.dma("g", wsl.rearrange("p (k f) -> p k f", k=8),
                            src, waits=wwait,
                            inc=(self.c_qkv0 if ws == 0 else self.c_qkv1))
            for n in range(NT):
                hs = n % 2
                hbase = hs * 1024
                PS = PSET[hs]
                P0, P1, P2 = PS["P0"], PS["P1"], PS["P2"]
                wcast = [wx]
                if hsj_free[hs] is not None:
                    wcast.append((self.c_ct, hsj_free[hs]))
                d_lo = bass.AP(sb["HSJ"].tensor, hbase,
                               [[2048, 128], [128, 4], [1, 128]])
                s_lo = bass.AP(X.tensor, n * 128,
                               [[ND * T, 128], [T, 4], [1, 128]])
                self.op("s", lambda ss, d=d_lo, s2=s_lo:
                        ss.copy(out=d, in_=s2), waits=wcast)
                wz = []
                if n == 0:
                    ed = bass.AP(sb["HSJ"].tensor, hbase + 512,
                                 [[2048, 128], [128, 4], [1, 2]])
                    mz = self.op("s", lambda ss, ed=ed: ss.memzero(ed),
                                 inc=self.c_cs)
                    wz = [(self.c_cs, mz)]
                    d_hi = bass.AP(sb["HSJ"].tensor, hbase + 513,
                                   [[2048, 128], [128, 4], [1, 127]])
                    s_hi = bass.AP(X.tensor, 4 * T,
                                   [[ND * T, 128], [T, 4], [1, 127]])
                else:
                    d_hi = bass.AP(sb["HSJ"].tensor, hbase + 512,
                                   [[2048, 128], [128, 4], [1, 128]])
                    s_hi = bass.AP(X.tensor, 4 * T + n * 128 - 1,
                                   [[ND * T, 128], [T, 4], [1, 128]])
                ct_cast = self.op("s", lambda ss, d=d_hi, s2=s_hi:
                                  ss.copy(out=d, in_=s2), waits=wz,
                                  inc=self.c_cs)
                bidx = n % 4
                for c in range(ND):
                    w = []
                    if c == 0:
                        w += [self.bank_free[bidx], (self.c_cs, ct_cast)]
                        if n == 0:
                            w.append(((self.c_qkv0 if ws == 0 else self.c_qkv1),
                                      wdma))
                    self.op("t", lambda t, b=self.bank(bidx),
                            lh=sb["HSJ"][:, hbase + c * 128: hbase + (c + 1) * 128],
                            rh=wsl[:, c * 512:(c + 1) * 512],
                            st=(c == 0), sp=(c == ND - 1):
                            t.matmul(b, lh, rh, start=st, stop=sp),
                            waits=w, inc=self.c_ct)
                mmt = self.c_ct.v
                hsj_free[hs] = mmt
                if f == 2:
                    dv = bass.AP(sb["ARB"].tensor, n * 520,
                                 [[16384, 128], [65, 8], [1, 64]])
                    ev = self.op("s", lambda ss, b=self.bank(bidx), dv=dv:
                                 ss.copy(out=dv, in_=b.rearrange(
                                     "p (h e) -> p h e", h=8)),
                                 waits=[(self.c_ct, mmt), wfence],
                                 inc=self.c_cs)
                    self.bank_free[bidx] = (self.c_cs, ev)
                    continue
                qoff = 0 if f == 0 else 512
                cosb = 0 if f == 0 else 2048
                sinb = 1024 if f == 0 else 3072
                smq = sm[:, 1024 + hs * 64: 1024 + hs * 64 + 8]
                smc = sm[:, 1040 + hs * 64: 1040 + hs * 64 + 8]
                sml = sm[:, 1056 + hs * 64: 1056 + hs * 64 + 8]
                smr = sm[:, 1072 + hs * 64: 1072 + hs * 64 + 8]
                wa1 = [(self.c_ct, mmt)] + prp_free[hs]
                a1 = self.op("s", lambda ss, b=self.bank(bidx), P0=P0:
                             ss.copy(out=P0, in_=b), waits=wa1, inc=self.c_cs)
                sq2 = self.op("v", lambda v, P0=P0, P1=P1:
                              v.tensor_mul(P1, P0, P0),
                              waits=[(self.c_cs, a1)], inc=self.c_cv)
                wsm = [(self.c_cv, sq2)]
                if sm_free[hs] is not None:
                    wsm.append((self.c_cs, sm_free[hs]))
                rs3 = self.op("v", lambda v, smq=smq, P1=P1: v.reduce_sum(
                    smq, P1.rearrange("p (h e) -> p h e", h=8), axis=AX.X),
                    waits=wsm, inc=self.c_cv)
                self.op("v", lambda v, qo=qoff, P0=P0, P2=P2: v.tensor_mul(
                    P2, P0, sb["QKS"][:, qo:qo + 512]),
                    waits=[(self.c_prp, self.prep_t)] if n == 0 else [])
                v4 = self.op("v", lambda v, smq=smq, smc=smc:
                             v.tensor_scalar_max(smc, smq, 1e-24),
                             waits=[(self.c_cv, rs3)], inc=self.c_cv)
                ln6 = self.op("s", lambda ss, smc=smc, sml=sml:
                              ss.activation(out=sml, in_=smc, func=AF.Ln),
                              waits=[(self.c_cv, v4)], inc=self.c_cs)
                s3 = self.op("s", lambda ss, sml=sml, smr=smr:
                             ss.activation(out=smr, in_=sml, func=AF.Exp,
                                           scale=-0.5),
                             waits=[(self.c_cs, ln6)], inc=self.c_cs)
                sm_free[hs] = s3
                hl8 = None
                for hh in range(HB):
                    hl8 = self.op("v", lambda v, hh=hh, smr=smr, P0=P0, P2=P2:
                                  v.tensor_scalar(
                        out=P0[:, hh * 64:(hh + 1) * 64],
                        in0=P2[:, hh * 64:(hh + 1) * 64],
                        scalar1=smr[:, hh:hh + 1], scalar2=None, op0=ALU.mult),
                        waits=[(self.c_cs, s3)] if hh == 0 else [],
                        inc=self.c_cv if hh == HB - 1 else None)
                gsrc = bass.AP(PS["gt"], PS["gbase"] + 32,
                               [[PS["gstride"], 128], [64, 8], [-32, 2], [1, 32]])
                gdst = bass.AP(PS["gt"], PS["gbase"] + 512,
                               [[PS["gstride"], 128], [64, 8], [32, 2], [1, 32]])
                sin4 = rot[:, sinb + n * 64: sinb + (n + 1) * 64].rearrange(
                    "p (two e) -> p two e", two=2).unsqueeze(
                    1).broadcast_to((128, 8, 2, 32))
                g1 = self.op("g", lambda g, gsrc=gsrc, gdst=gdst, s4=sin4:
                             g.tensor_tensor(out=gdst, in0=gsrc, in1=s4,
                                             op=ALU.mult),
                             waits=[(self.c_cv, hl8)], inc=self.c_cg)
                cos3 = rot[:, cosb + n * 64: cosb + (n + 1) * 64].unsqueeze(
                    1).broadcast_to((128, 8, 64))
                c15 = self.op("v", lambda v, c3=cos3, P0=P0, P2=P2:
                              v.tensor_tensor(
                    out=P2.rearrange("p (h e) -> p h e", h=8),
                    in0=P0.rearrange("p (h e) -> p h e", h=8),
                    in1=c3, op=ALU.mult),
                    waits=[(self.c_cv, hl8)], inc=self.c_cv)
                wst = [(self.c_cg, g1), (self.c_cv, c15)]
                if stg_free[hs] is not None:
                    wst.append((self.c_ct, stg_free[hs]))
                v14 = self.op("v", lambda v, d=sb["STG"][:, hs * 512:(hs + 1) * 512],
                              P1=P1, P2=P2:
                              v.tensor_add(d, P2, P1),
                              waits=wst, inc=self.c_cv)
                prp_free[hs] = [(self.c_cv, v14), (self.c_cg, g1)]
                dbase = 0 if f == 0 else 8192
                ltt = None
                for tp in range(4):
                    b2 = (n + tp) % 4
                    w = [self.bank_free[b2]]
                    if tp == 0:
                        w.append((self.c_cv, v14))
                    ltt = self.op("t", lambda t, b=self.bank(b2)[:, 0:128],
                                  lh=sb["STG"][:, hs * 512 + tp * 128: hs * 512 + (tp + 1) * 128]:
                                  t.matmul(b, lh, sb["IDB"][:, :],
                                           start=True, stop=True),
                                  waits=w, inc=self.c_ct)
                    wev = [(self.c_ct, ltt)]
                    if f == 0 and n == 0 and tp == 0:
                        wev.append(wfence)
                    ev = self.op("s", lambda ss, b=self.bank(b2)[:, 0:128],
                                 d=sb["ARA"][:, dbase + tp * 2048 + n * 128:
                                             dbase + tp * 2048 + (n + 1) * 128]:
                                 ss.copy(out=d, in_=b),
                                 waits=wev, inc=self.c_cs)
                    self.bank_free[b2] = (self.c_cs, ev)
                stg_free[hs] = ltt
            wslot_t[ws] = self.c_ct.v
        self.qkv_wslot_done = self.c_ct.v
        self.qt2_ready = self.c_cs.v
        self.stgk_init = self.c_ct.v
        self.qkv_prep_cg = self.c_cg.v
        self.qkv_prep_cv = self.c_cv.v
        self.hln_last = None

    def attn_state_init(self):
        self._gg2 = 0
        self.pa_pidx = 0
        self.pa_free = [None, None]
        self.mslot_free = [None, None]
        self.stq_free = [None, None]
        self.stk_free = [None, None]
        self.sbk_free = [None, None]
        self.hop1_t = {}
        self.mred_t = {}
        self.pn_free = [self.pn_free[0], self.pn_free[1],
                        self.bank_free[2], self.bank_free[3]]

    def b_pass1_iter(self, l, h):
        sb = self.sb
        j, hb = divmod(h, 2)
        QTp = sb["ARA"][:, j * 2048:(j + 1) * 2048]
        KTp = sb["ARA"][:, 8192 + j * 2048: 8192 + (j + 1) * 2048]
        hs2 = h % 2
        mA = sb["SM"][:, hs2 * 32: hs2 * 32 + 16]
        mC = sb["SM"][:, 128 + hs2 * 32: 128 + hs2 * 32 + 16]
        for c in range(NT):
            jlo = max(0, (c - 4) * 128)
            jhi = c * 128 + 128
            w = jhi - jlo
            par = self.pa_pidx % 2
            reg = self.ps["PA"] if par == 0 else self.ps["PB"]
            wts = []
            if self.pa_free[par] is not None:
                wts.append(self.pa_free[par])
            else:
                wts += [self.bank_free[2 * par], self.bank_free[2 * par + 1]]
            if c == 0:
                wts.append((self.c_cs, self.qt2_ready))
            lt = self.op("t", lambda t, b=reg[:, 0:min(w, 512)],
                         lh=QTp[hb * 64: hb * 64 + 64, c * 128:(c + 1) * 128],
                         rh=KTp[hb * 64: hb * 64 + 64, jlo:jlo + min(w, 512)]:
                         t.matmul(b, lh, rh, start=True, stop=True),
                         waits=wts, inc=self.c_ct)
            if w > 512:
                lt = self.op("t", lambda t, b=reg[:, 512:w],
                             lh=QTp[hb * 64: hb * 64 + 64, c * 128:(c + 1) * 128],
                             rh=KTp[hb * 64: hb * 64 + 64, jlo + 512:jhi]:
                             t.matmul(b, lh, rh, start=True, stop=True),
                             inc=self.c_ct)
            wrd = [(self.c_ct, lt)]
            if c == 0 and self.mslot_free[hs2] is not None:
                wrd.append((self.c_ct, self.mslot_free[hs2]))
            rd = self.op("v", lambda v, b=reg[:, 0:w], m=mA[:, c:c + 1]:
                         v.reduce_max(m, b, axis=AX.X, negate=True),
                         waits=wrd, inc=self.c_cv)
            self.pa_free[par] = (self.c_cv, rd)
            self.pa_pidx += 1
            yield
        wcmb = [(self.c_cv, rd)]
        if self.mslot_free[hs2] is not None:
            wcmb.append((self.c_ct, self.mslot_free[hs2]))
        cmb = self.op("v", lambda v, mA=mA, mC=mC:
                      v.tensor_copy(out=mC, in_=mA),
                      waits=wcmb, inc=self.c_cv)
        self.mred_t[h] = cmb

    def b_transport(self, l, h):
        sb, dr = self.sb, self.dr
        hs2 = h % 2
        mC = sb["SM"][:, 128 + hs2 * 32: 128 + hs2 * 32 + 16]
        PA = self.ps["PA"]
        tp = self.op("t", lambda t, mC=mC: t.matmul(
            PA[0:16, 0:128], mC, sb["IDF"][:, :], start=True, stop=True),
            waits=[(self.c_cv, self.mred_t[h]), self.pa_free[0]],
            inc=self.c_ct)
        self.mslot_free[hs2] = tp
        m16 = sb["SMB"][0:16, 1024 + hs2 * 128: 1024 + hs2 * 128 + 128]
        wev = [(self.c_ct, tp)]
        if h >= 2 and (h - 2) in self.hop1_t:
            wev.append(((self.c_m10 if hs2 == 0 else self.c_m11),
                        self.hop1_t[h - 2]))
        ev = self.op("v", lambda v, m16=m16: v.tensor_copy(
            out=m16, in_=PA[0:16, 0:128]),
            waits=wev, inc=self.c_cv)
        self.pa_free[0] = (self.c_cv, ev)
        self.hop1_t[h] = self.dma("s", dr["mscr"][l * HB + h], m16,
                                  waits=[(self.c_cv, ev)],
                                  inc=(self.c_m10 if hs2 == 0 else self.c_m11))

    def b_pass2_g(self, l, h, g, feeder=None):
        sb, dr = self.sb, self.dr
        j, hb = divmod(h, 2)
        QTp = sb["ARA"][:, j * 2048:(j + 1) * 2048]
        KTp = sb["ARA"][:, 8192 + j * 2048: 8192 + (j + 1) * 2048]
        mrow = dr["mscr"][l * HB + h].rearrange("p f -> (p f)")
        ao_p0 = (h % 2) * 64
        ao_cb = 8192 + (h // 2) * 2048
        pnb = [self.ps["PN0"], self.ps["PN1"]]
        if True:
            gg2 = self._gg2
            gs = g % 2
            pn = pnb[gg2 % 2]
            wq = [(self.c_ct, self.stgk_init)]
            if self.stq_free[gs] is not None:
                wq = [(self.c_ct, self.stq_free[gs])]
            sA = self.op("s", lambda ss, d=sb["STG"][0:64, gs * 512:(gs + 1) * 512],
                         s2=QTp[hb * 64: hb * 64 + 64, g * 512:(g + 1) * 512]:
                         ss.copy(out=d, in_=s2), waits=wq, inc=self.c_cs)
            hp2 = self.dma("s", sb["STG"][64:65, gs * 512:(gs + 1) * 512],
                           mrow[g * 512:(g + 1) * 512].unsqueeze(0),
                           waits=[((self.c_m10 if h % 2 == 0 else self.c_m11),
                                   self.hop1_t[h])] + wq,
                           inc=(self.c_m20 if gs == 0 else self.c_m21))
            jc0 = max(0, 4 * g - 4)
            cc = 0
            first = True
            nstop = None
            jc_order = [4 * g] + [j_ for j_ in range(jc0, 4 * g + 4)
                                  if j_ != 4 * g]
            for jc in jc_order:
                if jc < 4 * g:
                    a = jc - (4 * g - 4)
                    N = 128 * (a + 1)
                    ocol = 0
                    roff = 0
                    mcol = 128 * a
                    moff = 128
                else:
                    b_ = jc - 4 * g
                    delta = 128 * b_
                    N = 512 - delta
                    ocol = delta
                    roff = delta
                    mcol = 0
                    moff = 0
                ks = cc % 2
                wk = [(self.c_ct, self.stgk_init),
                      (self.c_cv, self.stk_ones)]
                if self.stk_free[ks] is not None:
                    wk = [(self.c_ct, self.stk_free[ks])]
                kA = self.op("s", lambda ss, d=sb["STK"][0:64, ks * 128:(ks + 1) * 128],
                             s2=KTp[hb * 64: hb * 64 + 64, jc * 128:(jc + 1) * 128]:
                             ss.copy(out=d, in_=s2), waits=wk, inc=self.c_cs)
                sbi = cc % 2
                sbank = self.ps["PX"] if sbi == 0 else self.ps["PY"]
                wsim = [(self.c_cs, kA)]
                if self.sbk_free[sbi] is not None:
                    wsim.append(self.sbk_free[sbi])
                if cc == 0:
                    wsim += [(self.c_cs, sA),
                             ((self.c_m20 if gs == 0 else self.c_m21), hp2)]
                if N > 128:
                    rlo, rhi = (128, N) if mcol == 0 else (0, mcol)
                    self.op("t", lambda t, b=sbank[0:128, rlo:rhi],
                            lh=sb["STK"][0:65, ks * 128:(ks + 1) * 128],
                            rh=sb["STG"][0:65, gs * 512 + roff + rlo:
                                         gs * 512 + roff + rhi]:
                            t.matmul(b, lh, rh, start=True, stop=False),
                            waits=wsim, inc=self.c_ct)
                    self.op("t", lambda t, b=sbank[:, mcol:mcol + 128],
                            moff=moff:
                            t.matmul(b, sb["IDB"][:, :],
                                     sb["MTB"][:, moff:moff + 128],
                                     start=False, stop=False),
                            inc=self.c_ct)
                    st_ = self.op("t", lambda t, b=sbank[0:128, mcol:mcol + 128],
                                  lh=sb["STK"][0:65, ks * 128:(ks + 1) * 128],
                                  rh=sb["STG"][0:65, gs * 512 + roff + mcol:
                                               gs * 512 + roff + mcol + 128]:
                                  t.matmul(b, lh, rh, start=False, stop=True),
                                  inc=self.c_ct)
                else:
                    self.op("t", lambda t, b=sbank[:, mcol:mcol + 128],
                            moff=moff:
                            t.matmul(b, sb["IDB"][:, :],
                                     sb["MTB"][:, moff:moff + 128],
                                     start=True, stop=False),
                            waits=wsim, inc=self.c_ct)
                    st_ = self.op("t", lambda t, b=sbank[0:128, 0:N],
                                  lh=sb["STK"][0:65, ks * 128:(ks + 1) * 128],
                                  rh=sb["STG"][0:65, gs * 512 + roff:
                                               gs * 512 + roff + N]:
                                  t.matmul(b, lh, rh, start=False, stop=True),
                                  inc=self.c_ct)
                self.stk_free[ks] = st_
                we = [(self.c_ct, st_)]
                es = cc % 4
                if self.et_free[es] is not None:
                    we.append(self.et_free[es])
                et = self.op("s", lambda ss, b=sbank[0:128, 0:N],
                             d=sb["ET"][0:128, es * 512: es * 512 + N]:
                             ss.activation(out=d, in_=b, func=AF.Exp),
                             waits=we, inc=self.c_cs)
                self.sbk_free[sbi] = (self.c_cs, et)
                wn = [(self.c_cs, et)]
                if first:
                    wn.append(self.pn_free[gg2 % 2])
                    wn.append((self.c_cv, self.vb2_ones))
                    if l == 0 and h == 0 and g == 0:
                        wn.append((self.c_ct, self.proj_done_ct[7]))
                nstop = self.op("t", lambda t, b=pn[0:65, ocol:ocol + N],
                                lh=sb["ARB"][0:128, jc * 520 + h * 65:
                                             jc * 520 + h * 65 + 65],
                                rh=sb["ET"][0:128, es * 512: es * 512 + N],
                                st=first, sp=(jc == 4 * g + 3):
                                t.matmul(b, lh, rh, start=st, stop=sp),
                                waits=wn, inc=self.c_ct)
                self.et_free[es] = (self.c_ct, nstop)
                if feeder is not None:
                    next(feeder, None)
                if cc == 1 and self._den_pending:
                    for fn in self._den_pending:
                        fn()
                    self._den_pending = []
                first = False
                cc += 1
            self.stq_free[gs] = nstop
            self._den_pending.append(
                lambda gg2=gg2, pn=pn, nstop=nstop, l=l, h=h, g=g,
                ao_p0=ao_p0, ao_cb=ao_cb:
                self._b_den(gg2, pn, nstop, l, h, g, ao_p0, ao_cb))
            self._gg2 += 1

    def _b_den(self, gg2, pn, nstop, l, h, g, ao_p0, ao_cb):
            sb = self.sb
            rs = gg2 % 2
            rc = sb["SMB"][0:1, rs * 512: rs * 512 + 512]
            ri = sb["SM"][0:1, 512 + rs * 512: 1024 + rs * 512]
            lnt = self.op("s", lambda ss, ri=ri, pn=pn:
                          ss.activation(out=ri, in_=pn[64:65, 0:512],
                                        func=AF.Ln),
                          waits=[(self.c_ct, nstop), self.rc_free[rs]],
                          inc=self.c_cs)
            rt = self.op("s", lambda ss, ri=ri, rc=rc:
                         ss.activation(out=rc, in_=ri, func=AF.Exp,
                                       scale=-1.0),
                         waits=[(self.c_cs, lnt)], inc=self.c_cs)
            nb = sb["SMB"][64:128, rs * 512: rs * 512 + 512]
            nbt = self.op("v", lambda v, nb=nb, pn=pn:
                          v.tensor_copy(out=nb, in_=pn[0:64, 0:512]),
                          waits=[self.nb_free[rs], (self.c_ct, nstop)],
                          inc=self.c_cv)
            rpt = self.op("t", lambda t, pn=pn, rc=rc:
                          t.matmul(pn[64:128, 0:512], sb["ONE"][0:1, 0:64], rc,
                                   start=True, stop=True),
                          waits=[(self.c_cs, rt), (self.c_cv, nbt)],
                          inc=self.c_ct)
            self.rc_free[rs] = (self.c_ct, rpt)
            wao = [(self.c_ct, rpt)]
            if l == 1:
                wao.append((self.c_ct, self.wo_f_done))
            ev = self.op("v", lambda v, pn=pn, nb=nb,
                         d=sb["ARC"][ao_p0:ao_p0 + 64,
                                     ao_cb + g * 512: ao_cb + (g + 1) * 512]:
                         v.scalar_tensor_tensor(out=d, in0=pn[64:128, 0:512],
                                                scalar=1.0, in1=nb,
                                                op0=ALU.mult, op1=ALU.mult),
                         waits=wao, inc=self.c_cv)
            self.pn_free[gg2 % 2] = (self.c_cv, ev)
            self.nb_free[rs] = (self.c_cv, ev)

    def b_wo(self, l):
        sb, dr = self.sb, self.dr
        wdma = self.wo_dma_t
        k = 0
        for o in range(ND):
            for tc in range(4):
                bidx = k % 4
                for kk in range(4):
                    w = []
                    if kk == 0:
                        w.append(self.bank_free[bidx])
                        if k == 0:
                            w += [(self.c_cv, self.ao2_ready),
                                  (self.c_wod, wdma)]
                    self.op("t", lambda t, b=self.bank(bidx),
                            lh=sb["ARW"][:, kk * 1024 + o * 128: kk * 1024 + (o + 1) * 128],
                            rh=sb["ARC"][0:128, 8192 + kk * 2048 + tc * 512:
                                         8192 + kk * 2048 + tc * 512 + 512],
                            st=(kk == 0), sp=(kk == 3):
                            t.matmul(b, lh, rh, start=st, stop=sp),
                            waits=w, inc=self.c_ct)
                mmt = self.c_ct.v
                wev = [(self.c_ct, mmt)]
                if k == 0 and l == 0:
                    wev.append((self.c_spl, self.xa_spilled))
                ev = self.op("v", lambda v, b=self.bank(bidx),
                             d=sb["X"][:, o * T + tc * 512: o * T + tc * 512 + 512]:
                             v.scalar_tensor_tensor(out=d, in0=b, scalar=1.0,
                                                    in1=d, op0=ALU.mult,
                                                    op1=ALU.add),
                             waits=wev, inc=self.c_cv)
                self.bank_free[bidx] = (self.c_cv, ev)
                k += 1
        self.wo_f_done = self.c_ct.v
        self.x2_ready = self.c_cv.v

    def b_ffn(self, l):
        sb, dr = self.sb, self.dr
        X = sb["X"]
        XB2 = sb["ARB"][:, 0:4096]
        SQB = sb["ARB"][:, 4096:8192]
        hl_base = 8320
        H2C = sb["ARA"]
        PX, PY = self.ps["PX"], self.ps["PY"]
        PN0, PN1 = self.ps["PN0"], self.ps["PN1"]
        sm = sb["SM"]
        fill = sm[:, 127:128]
        w2mm_prev = None
        w1slot_t = [None, None]
        w2slot_t = [None, None]
        stg2_free = [None, None]
        for tq in range(4):
            wc = [(self.c_cv, self.x2_ready)]
            if tq == 0:
                wc += [(self.c_ct, self.attn_done_ct)]
            else:
                wc += [(self.c_ct, self.stats_mm_t)]
            d_lo = bass.AP(sb["ARB"].tensor, 0, [[16384, 128], [512, 4], [1, 512]])
            s_lo = bass.AP(X.tensor, tq * 512, [[ND * T, 128], [T, 4], [1, 512]])
            self.op("s", lambda ss, d=d_lo, s2=s_lo: ss.copy(out=d, in_=s2),
                    waits=wc)
            wz2 = []
            if tq == 0:
                ed = bass.AP(sb["ARB"].tensor, 2048, [[16384, 128], [512, 4], [1, 2]])
                mz2 = self.op("s", lambda ss, ed=ed: ss.memzero(ed),
                              inc=self.c_cs)
                wz2 = [(self.c_cs, mz2)]
            else:
                ed = bass.AP(sb["ARB"].tensor, 2048, [[16384, 128], [512, 4], [1, 1]])
                sv = bass.AP(sb["COLS"].tensor, 80, [[88, 128], [1, 4], [1, 1]])
                self.op("s", lambda ss, ed=ed, sv=sv: ss.copy(out=ed, in_=sv),
                        waits=[(self.c_cv, self.sv_t)])
            d_hi = bass.AP(sb["ARB"].tensor, 2049, [[16384, 128], [512, 4], [1, 511]])
            s_hi = bass.AP(X.tensor, 4 * T + tq * 512, [[ND * T, 128], [T, 4], [1, 511]])
            cst = self.op("s", lambda ss, d=d_hi, s2=s_hi: ss.copy(out=d, in_=s2),
                          waits=wz2, inc=self.c_cs)
            sqt = self.op("v", lambda v: v.tensor_mul(SQB, XB2, XB2),
                          waits=[(self.c_cs, cst)], inc=self.c_cv)
            for c in range(ND):
                self.op("t", lambda t, rh=XB2[:, c * 512:(c + 1) * 512],
                        st=(c == 0), sp=(c == ND - 1):
                        t.matmul(PX[0:1, 0:512], sb["ONE"][0:128, 0:1], rh,
                                 start=st, stop=sp),
                        waits=[(self.c_cs, cst), self.sbk_free[0] or (self.c_ct, 0)]
                        if c == 0 else [], inc=self.c_ct)
            for c in range(ND):
                self.op("t", lambda t, rh=SQB[:, c * 512:(c + 1) * 512],
                        st=(c == 0), sp=(c == ND - 1):
                        t.matmul(PY[0:1, 0:512], sb["ONE"][0:128, 0:1], rh,
                                 start=st, stop=sp),
                        waits=[(self.c_cv, sqt), self.sbk_free[1] or (self.c_ct, 0)]
                        if c == 0 else [], inc=self.c_ct)
            self.sbk_free = [None, None]
            sums = self.c_ct.v
            self.stats_mm_t = sums
            murow = sm[0:1, 0:512]
            e2row = sm[0:1, 512:1024]
            varrow = sm[0:1, 1024:1536]
            rrowb = sb["SMB"][0:1, 0:512]
            srowb = sb["SMB"][0:1, 512:1024]
            mu1 = self.op("v", lambda v: v.tensor_scalar(
                out=murow, in0=PX[0:1, 0:512], scalar1=1.0 / D, scalar2=None,
                op0=ALU.mult), waits=[(self.c_ct, sums)], inc=self.c_cv)
            self.op("v", lambda v: v.tensor_scalar(
                out=e2row, in0=PY[0:1, 0:512], scalar1=1.0 / D, scalar2=None,
                op0=ALU.mult), inc=self.c_cv)
            m3 = self.op("v", lambda v: v.tensor_mul(varrow, murow, murow),
                         waits=[(self.c_cv, mu1)], inc=self.c_cv)
            vv = self.op("v", lambda v: v.tensor_tensor(
                out=varrow, in0=e2row, in1=varrow, op=ALU.subtract),
                waits=[(self.c_cv, m3)], inc=self.c_cv)
            self.sbk_free = [(self.c_cv, vv), (self.c_cv, vv)]
            st0 = self.op("s", lambda s2: s2.activation(
                out=sm[0:1, 1024:1536], in_=varrow, func=AF.Ln,
                bias=sb["COLS"][0:1, 4:5]), waits=[(self.c_cv, vv)],
                inc=self.c_cs)
            st1 = self.op("s", lambda s2: s2.activation(
                out=sm[0:1, 512:1024], in_=sm[0:1, 1024:1536], func=AF.Exp,
                scale=-0.5), waits=[(self.c_cs, st0)], inc=self.c_cs)
            vv2 = self.op("v", lambda v: v.tensor_copy(out=rrowb,
                                                       in_=sm[0:1, 512:1024]),
                          waits=[(self.c_cs, st1)])
            vv2 = self.op("v", lambda v: v.tensor_tensor(
                out=srowb, in0=murow, in1=sm[0:1, 512:1024], op=ALU.mult),
                waits=[(self.c_cv, mu1)], inc=self.c_cv)
            rep1 = self.op("t", lambda t: t.matmul(
                PN0[0:128, 0:512], sb["ONE"][0:1, 0:128], rrowb,
                start=True, stop=True),
                waits=[(self.c_cv, vv2), self.pn_free[0]], inc=self.c_ct)
            rep2 = self.op("t", lambda t: t.matmul(
                PN1[0:128, 0:512], sb["ONE"][0:1, 0:128], srowb,
                start=True, stop=True),
                waits=[self.pn_free[1]], inc=self.c_ct)
            hln_t = None
            for p_ in range(4):
                stg_t = None
                for c in (2 * p_, 2 * p_ + 1):
                    Pc = sb["PRP"][:, (c % 2) * 512:(c % 2) * 512 + 512]
                    shift = 1 if c >= 4 else 0
                    w = [(self.c_ct, rep2)] if c == 0 else []
                    if c == 0 and tq == 0:
                        w.append((self.c_prp, self.prep_t))
                    if c >= 2 and c % 2 == 0:
                        w.append((self.c_cv, hln_t))
                    if c == 0 and tq == 0:
                        w.append((self.c_cg, self.qkv_prep_cg))
                        if self.hln_last is not None:
                            w.append((self.c_cv, self.hln_last))
                    elif c == 0 and self.hln_last is not None:
                        w.append((self.c_cv, self.hln_last))
                    if shift:
                        if tq == 0:
                            self.op("v", lambda v, Pc=Pc:
                                    v.memset(Pc[:, 0:1], 0.0), waits=w)
                        else:
                            self.op("v", lambda v, Pc=Pc,
                                    sv=sb["COLS"][:, 76 + c:77 + c]:
                                    v.tensor_tensor(out=Pc[:, 0:1], in0=sv,
                                                    in1=PN0[0:128, 0:1],
                                                    op=ALU.mult),
                                    waits=w)
                        stg_t = self.op("v", lambda v, Pc=Pc,
                                        xs=X[:, c * T + tq * 512:
                                             c * T + tq * 512 + 511]:
                                        v.tensor_tensor(out=Pc[:, 1:512],
                                                        in0=xs,
                                                        in1=PN0[0:128, 1:512],
                                                        op=ALU.mult),
                                        inc=self.c_cv)
                    else:
                        stg_t = self.op("v", lambda v, Pc=Pc,
                                        xs=X[:, c * T + tq * 512:
                                             c * T + tq * 512 + 512]:
                                        v.tensor_tensor(out=Pc, in0=xs,
                                                        in1=PN0[0:128, 0:512],
                                                        op=ALU.mult),
                                        waits=w, inc=self.c_cv)
                sub_t = None
                for i_, c in enumerate((2 * p_, 2 * p_ + 1)):
                    Pc = sb["PRP"][:, (c % 2) * 512:(c % 2) * 512 + 512]
                    sub_t = self.op("v", lambda v, Pc=Pc: v.tensor_tensor(
                        out=Pc, in0=Pc, in1=PN1[0:128, 0:512],
                        op=ALU.subtract),
                        waits=[(self.c_cv, stg_t)] if i_ == 0 else [],
                        inc=self.c_cv)
                for i_, c in enumerate((2 * p_, 2 * p_ + 1)):
                    Pc = sb["PRP"][:, (c % 2) * 512:(c % 2) * 512 + 512]
                    hln_t = self.op("v", lambda v, Pc=Pc,
                                    d=sb["ARB"][:, hl_base + c * 512:
                                                hl_base + (c + 1) * 512],
                                    sc=sb["COLS"][:, 8 + c:9 + c],
                                    bi=sb["COLS"][:, 16 + c:17 + c]:
                                    v.tensor_scalar(out=d, in0=Pc, scalar1=sc,
                                                    scalar2=bi, op0=ALU.mult,
                                                    op1=ALU.add),
                                    waits=[(self.c_cv, sub_t)] if i_ == 0
                                    else [], inc=self.c_cv)
            hln_done = hln_t
            self.hln_last = hln_done
            self.pn_free[0] = (self.c_cv, hln_done)
            self.pn_free[1] = (self.c_cv, hln_done)
            h2_t = None
            for hc in range(NHC):
                ws = hc % 2
                wsl = sb["ARW"][:, 4096 + ws * 2048: 4096 + (ws + 1) * 2048]
                wwait = []
                if w1slot_t[ws] is not None:
                    wwait.append((self.c_ct, w1slot_t[ws]))
                elif tq == 0 and hc < 2:
                    wwait.append((self.c_ct, self.wo_f_done))
                if tq == 0 and hc < 2:
                    wwait.append((self.c_pc, self.precast_t))
                wdma = self.dma("y", wsl, dr["w1s"][l][hc], waits=wwait,
                                inc=(self.c_w10 if ws == 0 else self.c_w11))
                ba = self.bank(2 * ws)
                bg = self.bank(2 * ws + 1)
                for kk in range(ND):
                    w = []
                    if kk == 0:
                        w += [self.bank_free[2 * ws], (self.c_cv, hln_done),
                              ((self.c_w10 if ws == 0 else self.c_w11), wdma)]
                        if tq > 0 and hc == 0:
                            w.append((self.c_ct, w2mm_prev))
                    self.op("t", lambda t, b=ba,
                            lh=wsl[:, kk * 256: kk * 256 + 128],
                            rh=sb["ARB"][:, hl_base + kk * 512: hl_base + (kk + 1) * 512],
                            st=(kk == 0), sp=(kk == ND - 1):
                            t.matmul(b, lh, rh, start=st, stop=sp),
                            waits=w, inc=self.c_ct)
                for kk in range(ND):
                    w = [self.bank_free[2 * ws + 1]] if kk == 0 else []
                    self.op("t", lambda t, b=bg,
                            lh=wsl[:, kk * 256 + 128: kk * 256 + 256],
                            rh=sb["ARB"][:, hl_base + kk * 512: hl_base + (kk + 1) * 512],
                            st=(kk == 0), sp=(kk == ND - 1):
                            t.matmul(b, lh, rh, start=st, stop=sp),
                            waits=w, inc=self.c_ct)
                w1mm = self.c_ct.v
                w1slot_t[ws] = w1mm
                wgl = [(self.c_ct, w1mm)]
                if tq == 0 and hc == 0:
                    wgl.append((self.c_prp, self.prep_t))
                if stg2_free[ws] is not None:
                    wgl.append((self.c_cv, stg2_free[ws]))
                gl = self.op("s", lambda ss, b=bg,
                             d=sb["STG"][:, ws * 512:(ws + 1) * 512],
                             bi=sb["COLS"][:, 54 + hc:55 + hc]:
                             ss.activation(out=d, in_=b, func=AF.Gelu,
                                           bias=bi),
                             waits=wgl, inc=self.c_cs)
                self.bank_free[2 * ws + 1] = (self.c_cs, gl)
                h2w = [(self.c_cs, gl)]
                if tq == 0 and hc < 2:
                    h2w.append((self.c_ct, self.attn_done_ct))
                if tq > 0 and hc == 0:
                    h2w.append((self.c_ct, w2mm_prev))
                h2_t = self.op("v", lambda v, b=ba, hc=hc:
                               v.scalar_tensor_tensor(
                                   out=H2C[:, hc * 512:(hc + 1) * 512],
                                   in0=b, scalar=sb["COLS"][:, 32 + hc:33 + hc],
                                   in1=sb["STG"][:, (hc % 2) * 512:((hc % 2) + 1) * 512],
                                   op0=ALU.add, op1=ALU.mult),
                               waits=h2w, inc=self.c_cv)
                self.bank_free[2 * ws] = (self.c_cv, h2_t)
                stg2_free[ws] = h2_t
            h2_done = h2_t
            svd = bass.AP(sb["COLS"].tensor, 80, [[88, 128], [1, 4], [1, 1]])
            svs = bass.AP(X.tensor, 4 * T + tq * 512 + 511,
                          [[ND * T, 128], [T, 4], [1, 1]])
            self.sv_t = self.op("v", lambda v, svd=svd, svs=svs:
                                v.tensor_copy(out=svd, in_=svs),
                                inc=self.c_cv)
            for o in range(ND):
                ws2 = o % 2
                wwait = []
                if w2slot_t[ws2] is not None:
                    wwait.append((self.c_ct, w2slot_t[ws2]))
                elif tq == 0 and o < 2:
                    wwait.append((self.c_ct, self.wo_f_done))
                    wwait.append((self.c_cg, self.qkv_prep_cg))
                    wwait.append((self.c_cv, self.qkv_prep_cv))
                if tq == 0 and o < 2:
                    wwait.append((self.c_pc, self.precast_t))
                wdma = self.dma("y", sb["ARC"][:, ws2 * 2816:(ws2 + 1) * 2816],
                                dr["w2s"][l][o], waits=wwait,
                                inc=(self.c_w20 if ws2 == 0 else self.c_w21))
                bidx = o % 4
                for kk in range(NHC):
                    w = []
                    if kk == 0:
                        w += [self.bank_free[bidx], (self.c_cv, h2_done),
                              ((self.c_w20 if ws2 == 0 else self.c_w21), wdma)]
                    self.op("t", lambda t, b=self.bank(bidx),
                            lh=sb["ARC"][:, ws2 * 2816 + kk * 128:
                                         ws2 * 2816 + (kk + 1) * 128],
                            rh=H2C[:, kk * 512:(kk + 1) * 512],
                            st=(kk == 0), sp=(kk == NHC - 1):
                            t.matmul(b, lh, rh, start=st, stop=sp),
                            waits=w, inc=self.c_ct)
                w2mm = self.c_ct.v
                w2slot_t[ws2] = w2mm
                ev = self.op("v", lambda v, b=self.bank(bidx),
                             d=X[:, o * T + tq * 512: o * T + tq * 512 + 512],
                             sc=sb["COLS"][:, 24 + o:25 + o]:
                             v.scalar_tensor_tensor(
                                 out=d, in0=b, scalar=sc,
                                 in1=d, op0=ALU.add, op1=ALU.add),
                             waits=[(self.c_ct, w2mm)], inc=self.c_cv)
                self.bank_free[bidx] = (self.c_cv, ev)
            w2mm_prev = self.c_ct.v
        self.w2mm_done = self.c_ct.v
        self.ffn_done_ct = self.c_ct.v
        self.ffn_done_cv = self.c_cv.v

    def st_final(self):
        sb, dr = self.sb, self.dr
        XA_lo = sb["ARA"].bitcast(F32)
        XA_hi = sb["ARB"].bitcast(F32)
        d1 = self.dma("y", XA_lo[:, 0:8192], dr["xa_spill"][:, 0:8192],
                      waits=[(self.c_ct, self.ffn_done_ct),
                             (self.c_cv, self.ffn_done_cv),
                             (self.c_cs, self.c_cs.v)], inc=self.c_rel)
        d2 = self.dma("y", XA_hi[:, 0:8192], dr["xa_spill"][:, 8192:16384],
                      inc=self.c_rel)
        last = None
        for i in range(16):
            src = (XA_lo if i < 8 else XA_hi)[:, (i % 8) * 1024:(i % 8) * 1024 + 1024]
            w = []
            if i == 0:
                w = [(self.c_rel, d2), (self.c_cv, self.ffn_done_cv)]
            last = self.op("v", lambda v, src=src,
                           d=sb["X"][:, i * 1024:(i + 1) * 1024]:
                           v.tensor_tensor(out=d, in0=d, in1=src, op=ALU.add),
                           waits=w, inc=self.c_cv)
        self.v_x0_done = last
        self.st_out(ln=True)


def VB_ap(sb, vb_base, jc, hb):
    return sb["ARB"][0:128, vb_base + jc * 130 + hb * 65:
                     vb_base + jc * 130 + hb * 65 + 65]



def build_nc(upto="full"):
    mk = MK(upto)
    mk.st_setup()
    if upto == "roundtrip":
        mk.st_out(ln=False)
        return mk.finalize()
    if upto == "proj0":
        mk.natk = 0
        mk.nat_free = [(mk.c_ct, 0), (mk.c_ct, 0)]
        mk.pair_ct = {}
        mk.proj_ready = {}
        mk.vb_ready = {}
        mk.wslot_ready = {}
        mk.proj_done_ct = {}
        mk.px_free = [(mk.c_cv, 0), (mk.c_cv, 0)]
        mk.a_wot()
        mk.a_pair_weights(0)
        mk.a_proj(0)
        mk.dbg("d_qt", mk.sb["ARA"][:, 0:2048], (128, 2048))
        mk.dbg("d_kt", mk.sb["ARA"][:, 2048:4096], (128, 2048))
        mk.dbg("d_vb", mk.sb["ARB"][:, 8192:8192 + 2080], (128, 2080))
        mk.dbg("d_wqt", mk.sb["ARW"][:, 2048:3072], (128, 1024))
        mk.dbg("d_wot", mk.sb["ARA"][:, 8192:16384], (128, 8192))
        mk.dbg("d_xb", mk.sb["ARC"][:, 0:2048], (128, 2048))
        return mk.finalize()
    if upto == "attn01":
        mk.natk = 0
        mk.nat_free = [(mk.c_ct, 0), (mk.c_ct, 0)]
        mk.pair_ct = {}
        mk.proj_ready = {}
        mk.vb_ready = {}
        mk.wslot_ready = {}
        mk.proj_done_ct = {}
        mk.et_free = [(mk.c_ct, 0), (mk.c_ct, 0)]
        mk.nb_free = [(mk.c_cs, 0), (mk.c_cs, 0)]
        mk.rc_free = [(mk.c_ct, 0), (mk.c_ct, 0)]
        mk.pn_free = [(mk.c_cv, 0), (mk.c_cv, 0)]
        mk.px_free = [(mk.c_cv, 0), (mk.c_cv, 0)]
        mk.wo_done = {}
        mk._gg = 0
        mk.a_wot()
        mk.a_pair_weights(0)
        mk.a_proj(0)
        mk.a_attn(0)
        mk.a_attn(1)
        mk.dbg("d_ao", mk.sb["ARB"][:, 0:2048], (128, 2048))
        return mk.finalize()
    mk.st_partA()
    if upto == "parta":
        mk.st_out(ln=False)
        return mk.finalize()
    if upto == "wtest":
        mk.dout("d_w1s", (NHC, 128, 2048), BF16)
        mk.dout("d_w2s", (ND, 128, 2816), BF16)
        mk.op("y", lambda y: y.nop(), waits=[(mk.c_sw, mk.precast_t)])
        mk.dma("y", mk.dr["d_w1s"], mk.dr["w1s"][0])
        t = mk.dma("y", mk.dr["d_w2s"], mk.dr["w2s"][0])
        mk.op("y", lambda y: y.nop(), waits=[(mk.c_sh, t)])
        return mk.finalize()
    if upto == "w2check":
        mk.st_partB(0)
        mk.st_partB(1)
        mk.st_final()
        mk.dout("d_w2s", (2, ND, 128, 2816), BF16)
        mk.op("y", lambda y: y.nop(), waits=[(mk.c_sh, mk.c_sh.v),
                                             (mk.c_sm, mk.c_sm.v)])
        t = mk.dma("y", mk.dr["d_w2s"], mk.dr["w2s"])
        mk.op("y", lambda y: y.nop(), waits=[(mk.c_sh, t)])
        return mk.finalize()
    if upto == "bqkv":
        mk.b_prep(0)
        mk.b_qkv(0)
        mk.dbg("d_qt2", mk.sb["ARA"][:, 0:8192], (128, 8192))
        mk.dbg("d_kt2", mk.sb["ARA"][:, 8192:16384], (128, 8192))
        mk.dbg("d_vb2", mk.sb["ARB"][:, 0:8320], (128, 8320))
        return mk.finalize()
    if upto == "battn":
        mk.b_prep(0)
        mk.b_qkv(0)
        mk.attn_state_init()
        for h in range(HB + 1):
            if h < HB:
                mk.b_pass1(0, h)
                mk.b_transport(0, h)
            if h >= 1:
                mk.b_pass2(0, h - 1)
        mk.dbg("d_ao2", mk.sb["ARC"][:, 8192:16384], (128, 8192))
        return mk.finalize()
    if upto == "bffn":
        mk.b_prep(0)
        mk.b_qkv(0)
        mk.wo_dma_t = mk.dma(
            "g", mk.sb["ARW"][:, 0:4096].rearrange("p (k m) -> p k m", k=4),
            mk.dr["wo_f"][0].rearrange("(k p) m -> p k m", p=128),
            waits=[(mk.c_ct, mk.qkv_wslot_done)], inc=mk.c_wod)
        mk.emit_precast()
        mk.attn_state_init()
        for h in range(HB + 1):
            feeder = mk.b_pass1_iter(0, h) if h < HB else None
            if h == 0:
                for _ in feeder:
                    pass
                feeder = None
            if h >= 1:
                for g in range(4):
                    mk.b_pass2_g(0, h - 1, g, feeder=feeder)
            if feeder is not None:
                for _ in feeder:
                    pass
            if h < HB:
                mk.b_transport(0, h)
        for fn in mk._den_pending:
            fn()
        mk._den_pending = []
        mk.ao2_ready = mk.c_cv.v
        mk.attn_done_ct = mk.c_ct.v
        mk.bank_free[0] = mk.pa_free[0] or mk.bank_free[0]
        mk.bank_free[1] = mk.pa_free[0] or mk.bank_free[1]
        mk.bank_free[2] = mk.pa_free[1] or mk.bank_free[2]
        mk.bank_free[3] = mk.pa_free[1] or mk.bank_free[3]
        mk.b_wo(0)
        mk.b_ffn(0)
        mk.dbg("d_hln", mk.sb["ARB"][:, 8320:12416], (128, 4096))
        mk.dbg("d_xb2", mk.sb["ARB"][:, 0:4096], (128, 4096))
        mk.dbg("d_sm", mk.sb["SM"][:, :], (128, 1536))
        mk.dbg("d_smb", mk.sb["SMB"][:, :], (128, 1280))
        mk.dbg("d_sqb", mk.sb["ARB"][:, 4096:8192], (128, 4096))
        mk.dbg("d_h2c", mk.sb["ARA"][:, 0:11264], (128, 11264))
        mk.st_out(ln=False)
        return mk.finalize()
    if upto == "bwo":
        mk.b_prep(0)
        mk.b_qkv(0)
        mk.wo_dma_t = mk.dma(
            "g", mk.sb["ARW"][:, 0:4096].rearrange("p (k m) -> p k m", k=4),
            mk.dr["wo_f"][0].rearrange("(k p) m -> p k m", p=128),
            waits=[(mk.c_ct, mk.qkv_wslot_done)], inc=mk.c_wod)
        mk.emit_precast()
        mk.attn_state_init()
        for h in range(HB + 1):
            feeder = mk.b_pass1_iter(0, h) if h < HB else None
            if h == 0:
                for _ in feeder:
                    pass
                feeder = None
            if h >= 1:
                for g in range(4):
                    mk.b_pass2_g(0, h - 1, g, feeder=feeder)
            if feeder is not None:
                for _ in feeder:
                    pass
            if h < HB:
                mk.b_transport(0, h)
        for fn in mk._den_pending:
            fn()
        mk._den_pending = []
        mk.ao2_ready = mk.c_cv.v
        mk.attn_done_ct = mk.c_ct.v
        mk.bank_free[0] = mk.pa_free[0] or mk.bank_free[0]
        mk.bank_free[1] = mk.pa_free[0] or mk.bank_free[1]
        mk.bank_free[2] = mk.pa_free[1] or mk.bank_free[2]
        mk.bank_free[3] = mk.pa_free[1] or mk.bank_free[3]
        mk.b_wo(0)
        mk.st_out(ln=False)
        return mk.finalize()
    mk.st_partB(0)
    if upto == "partb0":
        mk.st_out(ln=False)
        return mk.finalize()
    mk.st_partB(1)
    mk.st_final()
    return mk.nc if False else mk.finalize()


def make_inputs_per_core(inputs):
    c = host_constants()
    x = np.ascontiguousarray(inputs["L0_attn_x"], dtype=np.float32)
    b1 = np.asarray(inputs["L0_ffn_b1"], np.float32)
    b1a = np.zeros((2, HIDP), np.float32)
    b1g = np.zeros((2, HIDP), np.float32)
    b1a[:, :HID] = b1[:, :HID]
    b1g[:, :HID] = b1[:, HID:]
    w1 = np.asarray(inputs["L0_ffn_w1"], np.float32)
    w1i = np.zeros((2, D, NHC, 2, 128), np.float32)
    w1a = np.zeros((2, D, HIDP), np.float32)
    w1g = np.zeros((2, D, HIDP), np.float32)
    w1a[:, :, :HID] = w1[:, :, :HID]
    w1g[:, :, :HID] = w1[:, :, HID:]
    w1i[:, :, :, 0, :] = w1a.reshape(2, D, NHC, 128)
    w1i[:, :, :, 1, :] = w1g.reshape(2, D, NHC, 128)
    w1i = np.ascontiguousarray(w1i.reshape(2, D, 2 * HIDP))
    w2 = np.asarray(inputs["L0_ffn_w2"], np.float32)
    w2p = np.zeros((2, HIDP, D), np.float32)
    w2p[:, :HID, :] = w2
    qs = np.asarray(inputs["L0_ffn_q_scale"], np.float32)
    ks = np.asarray(inputs["L0_ffn_k_scale"], np.float32)
    qs_rep = np.ascontiguousarray(np.broadcast_to(
        np.tile(qs, (1, HB))[:, None, :], (2, 128, INNER))).astype(np.float32)
    ks_rep = np.ascontiguousarray(np.broadcast_to(
        np.tile(ks, (1, HB))[:, None, :], (2, 128, INNER))).astype(np.float32)
    import ml_dtypes
    shared = {
        "wq": np.ascontiguousarray(inputs["L0_attn_Wq"], np.float32),
        "wkv": np.ascontiguousarray(inputs["L0_attn_Wkv"], np.float32),
        "wo_a": np.ascontiguousarray(inputs["L0_attn_Wo"], np.float32),
        "wqkv": np.ascontiguousarray(inputs["L0_ffn_wqkv"], np.float32),
        "qs_rep": qs_rep,
        "ks_rep": ks_rep,
        "wo_f": np.ascontiguousarray(inputs["L0_ffn_wo"], np.float32),
        "lng_c": col_major(np.asarray(inputs["L0_ffn_ln_g"], np.float32)),
        "lnb_c": col_major(np.asarray(inputs["L0_ffn_ln_b"], np.float32)),
        "w1i": w1i,
        "b1a_c": col_major(b1a),
        "b1g_c": col_major(b1g),
        "w2p": w2p,
        "b2_c": col_major(np.asarray(inputs["L0_ffn_b2"], np.float32)),
        "ident_f": c["ident_f"],
        "ident_b": c["ident_b"].astype(ml_dtypes.bfloat16),
        "mask_tri": c["mask_tri"],
        "mask_low": c["mask_low"],
        "ones_b": c["ones_b"].astype(ml_dtypes.bfloat16),
        "rot_qc": c["rot_qc"],
        "rot_qs": c["rot_qs"],
        "rot_kc": c["rot_kc"],
        "rot_ks": c["rot_ks"],
    }
    in_maps = []
    for core in range(8):
        m = dict(shared)
        m["x0"] = np.ascontiguousarray(x[core])
        in_maps.append(m)
    return in_maps


def kernel(**inputs):
    in_maps = make_inputs_per_core(inputs)
    nc = build_nc("full")
    res = run_bass_kernel_spmd(nc, in_maps, core_ids=list(range(8)))
    out = np.stack([np.asarray(r["y"]) for r in res.results], axis=0)
    return out.astype(np.float32)
